# version 02
# baseline (speedup 1.0000x reference)
import numpy as np
import ml_dtypes
import concourse.bass as bass
import concourse.mybir as mybir
from concourse.bass_utils import run_bass_kernel_spmd
from concourse.bass_core_rust import IndirectOffsetOnAxis
from contextlib import ExitStack, contextmanager

F32 = mybir.dt.float32
BF16 = mybir.dt.bfloat16
U32 = mybir.dt.uint32
AF = mybir.ActivationFunctionType
ALU = mybir.AluOpType
AX = mybir.AxisListType
BF = ml_dtypes.bfloat16

NCORES = 8
BPC = 2
S = 2048
D = 1024
NT = S // 128
NEG = -1.0e37
EPS = 1e-5
_STAGES = list(range(8))
_L1F_PARTS = ("shared", "experts")

ENGS = ("pe", "dve", "act", "pool", "sp")
_ENG_ATTR = {"pe": "tensor", "dve": "vector", "act": "scalar", "pool": "gpsimd", "sp": "sync"}
_DMA_NSEM = {"sp": 10, "act": 6, "pool": 10}


class _Buf:
    __slots__ = ("w", "readers")
    def __init__(self):
        self.w = None
        self.readers = []


class Prog:

    def __init__(self, nc, es):
        self.nc = nc
        self.semh = {}
        self.cnt = {}
        for ek in ENGS:
            self.semh[ek] = es.enter_context(nc.semaphore(f"S{ek}"))
            self.cnt[ek] = 0
        self.dmasems = {}
        self.dmarr = {}
        for qk, n in _DMA_NSEM.items():
            ring = []
            for i in range(n):
                sk = f"D{qk}{i}"
                self.semh[sk] = es.enter_context(nc.semaphore(sk))
                self.cnt[sk] = 0
                ring.append(sk)
            self.dmasems[qk] = ring
            self.dmarr[qk] = 0
        self.queue = {ek: [] for ek in ENGS}
        self.hw = {ek: {} for ek in ENGS}
        self.bufs = {}
        self.nisn = 0

    def _key(self, x):
        if isinstance(x, str):
            return x
        t = getattr(x, "tensor", None)
        return t.name if t is not None else x.name

    def _buf(self, x):
        k = self._key(x)
        b = self.bufs.get(k)
        if b is None:
            b = self.bufs[k] = _Buf()
        return b

    def _deps_for(self, reads, writes):
        deps = []
        for x in reads:
            b = self._buf(x)
            if b.w:
                deps.append(b.w)
        for x in writes:
            b = self._buf(x)
            if b.w:
                deps.append(b.w)
            deps.extend(b.readers)
        return deps

    def _dedup(self, ek, deps):
        hw = self.hw[ek]
        out = []
        for (sk, v) in deps:
            if hw.get(sk, 0) < v:
                hw[sk] = v
                out.append((sk, v))
        return out

    def _register(self, dep, reads, writes):
        for x in reads:
            self._buf(x).readers.append(dep)
        for x in writes:
            b = self._buf(x)
            b.w = dep
            b.readers = []

    def ex(self, ek, build, reads=(), writes=()):
        wl = self._dedup(ek, self._deps_for(reads, writes))
        self.cnt[ek] += 1
        dep = (ek, self.cnt[ek])
        semh, myh = self.semh, self.semh[ek]

        def item(eng, build=build, wl=wl):
            for sk, v in wl:
                eng.wait_ge(semh[sk], v)
            build(eng).then_inc(myh, 1)
        self.queue[ek].append(item)
        self.nisn += 1 + len(wl)
        self._register(dep, reads, writes)
        return dep

    def mmg(self, terms, reads=(), writes=(), start=True, stop=True):
        psum_ap = terms[0][0]
        opkeys = set()
        for (pap, lap, rap) in terms:
            opkeys.add(self._key(lap))
            opkeys.add(self._key(rap))
        opkeys.update(self._key(x) for x in reads)
        wkeys = {self._key(psum_ap)}
        wkeys.update(self._key(x) for x in writes)
        wl = self._dedup("pe", self._deps_for(opkeys, wkeys))
        self.cnt["pe"] += 1
        dep = ("pe", self.cnt["pe"])
        semh, myh = self.semh, self.semh["pe"]
        n = len(terms)

        def item(eng, terms=terms, wl=wl, n=n, start=start, stop=stop):
            for sk, v in wl:
                eng.wait_ge(semh[sk], v)
            for i, (pap, lap, rap) in enumerate(terms):
                inst = eng.matmul(pap, lap, rap, start=(start and i == 0),
                                  stop=(stop and i == n - 1))
            inst.then_inc(myh, 1)
        self.queue["pe"].append(item)
        self.nisn += n + len(wl)
        self._register(dep, opkeys, wkeys)
        return dep

    def _dma_dep(self, qk):
        ring = self.dmasems[qk]
        sk = ring[self.dmarr[qk] % len(ring)]
        self.dmarr[qk] += 1
        self.cnt[sk] += 16
        return sk, self.cnt[sk]

    def dma(self, qk, out_ap, in_ap, reads=(), writes=(), transpose=False):
        rk = [self._key(in_ap)] + [self._key(x) for x in reads]
        wk = [self._key(out_ap)] + [self._key(x) for x in writes]
        wl = self._dedup(qk, self._deps_for(rk, wk))
        sk, v = self._dma_dep(qk)
        dep = (sk, v)
        semh = self.semh

        def item(eng, out_ap=out_ap, in_ap=in_ap, wl=wl, sk=sk, transpose=transpose):
            for k2, v2 in wl:
                eng.wait_ge(semh[k2], v2)
            if transpose:
                inst = eng.dma_start_transpose(out_ap, in_ap)
            else:
                inst = eng.dma_start(out_ap, in_ap)
            inst.then_inc(semh[sk], 16)
        self.queue[qk].append(item)
        self.nisn += 1 + len(wl)
        self._register(dep, rk, wk)
        return dep

    def dma_gather(self, out_ap, in_dram_ap, idx_ap, reads=(), writes=()):
        rk = [self._key(in_dram_ap), self._key(idx_ap)] + [self._key(x) for x in reads]
        wk = [self._key(out_ap)] + [self._key(x) for x in writes]
        wl = self._dedup("pool", self._deps_for(rk, wk))
        sk, v = self._dma_dep("pool")
        dep = (sk, v)
        semh = self.semh

        def item(eng, out_ap=out_ap, in_dram_ap=in_dram_ap, idx_ap=idx_ap, wl=wl, sk=sk):
            for k2, v2 in wl:
                eng.wait_ge(semh[k2], v2)
            eng.indirect_dma_start(
                out_ap, None, in_dram_ap, IndirectOffsetOnAxis(idx_ap, 0)
            ).then_inc(semh[sk], 16)
        self.queue["pool"].append(item)
        self.nisn += 1 + len(wl)
        self._register(dep, rk, wk)
        return dep

    def barrier(self):
        allsems = list(self.cnt.keys())
        for ek in ENGS:
            wl = []
            hw = self.hw[ek]
            for sk in allsems:
                cur = self.cnt[sk]
                if cur > hw.get(sk, 0):
                    hw[sk] = cur
                    wl.append((sk, cur))
            if not wl:
                continue
            semh = self.semh

            def item(eng, wl=wl):
                for sk, v in wl:
                    eng.wait_ge(semh[sk], v)
            self.queue[ek].append(item)
            self.nisn += len(wl)
        self.bufs = {}

    @contextmanager
    def phase(self, name=""):
        es = ExitStack()
        try:
            yield es
        finally:
            self.barrier()
            es.close()

    def replay(self, es):
        blk = es.enter_context(self.nc.Block())
        for ek in ENGS:
            q = self.queue[ek]
            getattr(blk, _ENG_ATTR[ek])(lambda eng, q=q: [it(eng) for it in q])



def _hilo_rows(b):
    hi = b.astype(BF).astype(np.float32)
    lo = (b - hi).astype(BF).astype(np.float32)
    return np.ascontiguousarray(np.stack([hi, lo]))


def _col8(b):
    c = b.size // 128
    return np.ascontiguousarray(b.reshape(c, 128).T)


def _prep_consts():
    d = {}
    d["c_ident"] = np.eye(128, dtype=np.float32)
    cc = np.arange(2560)[None, :]
    ii = np.arange(128)[:, None]
    dd = ii + 2047 - cc
    alibi = np.where((dd < 0) | (dd >= 1024), NEG, -dd.astype(np.float64)).astype(np.float32)
    causal = np.where(dd < 0, NEG, 0.0).astype(np.float32)
    d["c_strip_alibi"] = np.ascontiguousarray(alibi)
    d["c_strip_causal"] = np.ascontiguousarray(causal)
    d["c_ones2"] = np.ones((2, 128), dtype=np.float32)
    e16 = np.zeros((16, 16 * 128), dtype=np.float32)
    for e in range(16):
        e16[e, e * 128:(e + 1) * 128] = 1.0
    d["c_e16"] = e16
    hd = 64
    theta = 1.0 / 10000.0 ** (np.arange(0, hd, 2, dtype=np.float32) / hd)
    t = np.arange(S, dtype=np.float32)
    fr = np.outer(theta, t)
    fr2 = np.concatenate([fr, fr], 0)
    cos = np.cos(fr2)
    sin = np.sin(fr2)
    sin_signed = sin.copy()
    sin_signed[:32] = -sin[:32]
    d["c_cos"] = np.ascontiguousarray(np.concatenate([cos, cos], 0).astype(np.float32))
    d["c_sin"] = np.ascontiguousarray(np.concatenate([sin_signed, sin_signed], 0).astype(np.float32))
    return d


def _rope_perm(nout):
    j = np.arange(nout)
    h = j // 64
    dd = j % 64
    return h * 64 + (dd + 32) % 64


def _prep_weights(inp):
    d = {}
    T = lambda a: np.ascontiguousarray(a.T)
    d["l0a_wqT"] = T(inp["L0_attn_a_wq"]); d["l0a_wkT"] = T(inp["L0_attn_a_wk"])
    d["l0a_wvT"] = T(inp["L0_attn_a_wv"]); d["l0a_woT"] = T(inp["L0_attn_a_wo"])
    d["l0a_bq"] = _col8(inp["L0_attn_a_bq"]); d["l0a_bk"] = _col8(inp["L0_attn_a_bk"])
    d["l0a_bv2"] = _hilo_rows(inp["L0_attn_a_bv"])
    d["l0a_bo2"] = _hilo_rows(inp["L0_attn_a_bo"])
    d["l0b_wqT"] = T(inp["L0_attn_b_Wq"]); d["l0b_wkT"] = T(inp["L0_attn_b_Wk"])
    d["l0b_wvT"] = T(inp["L0_attn_b_Wv"]); d["l0b_woT"] = T(inp["L0_attn_b_Wo"])
    d["l0b_bq"] = _col8(inp["L0_attn_b_bq"]); d["l0b_bk"] = _col8(inp["L0_attn_b_bk"])
    d["l0b_bv2"] = _hilo_rows(inp["L0_attn_b_bv"])
    d["l0b_bo2"] = _hilo_rows(inp["L0_attn_b_bo"])
    d["l0f_rw"] = np.ascontiguousarray(inp["L0_ffn_router_w"])
    d["l0f_rb2"] = _hilo_rows(inp["L0_ffn_router_b"])
    d["l0f_w1"] = np.ascontiguousarray(inp["L0_ffn_w1"])
    d["l0f_w2"] = np.ascontiguousarray(inp["L0_ffn_w2"])
    d["l0f_b1"] = np.ascontiguousarray(
        np.stack([_col8(inp["L0_ffn_b1"][e]) for e in range(16)])
        .transpose(1, 0, 2).reshape(128, 256))
    d["l0f_b2"] = np.ascontiguousarray(inp["L0_ffn_b2"])
    for k in ("W_dq", "W_uq", "W_dkv", "W_uk", "W_uv", "W_o"):
        d["l1a_" + k] = np.ascontiguousarray(inp["L1_attn_a_" + k])
    wqT = T(inp["L1_attn_b_Wq"])
    wkT = T(inp["L1_attn_b_Wk"])
    d["l1b_wqT"] = wqT
    d["l1b_wqTr"] = np.ascontiguousarray(wqT[:, _rope_perm(1024)])
    def _dup(w):
        r = w.reshape(1024, 4, 1, 64)
        return np.ascontiguousarray(
            np.concatenate([r, r], axis=2).reshape(1024, 512))
    d["l1b_wkT"] = _dup(wkT)
    d["l1b_wkTr"] = _dup(np.ascontiguousarray(wkT[:, _rope_perm(256)]))
    d["l1b_wvT"] = T(inp["L1_attn_b_Wv"])
    d["l1b_woT"] = T(inp["L1_attn_b_Wo"])
    d["l1b_bo2"] = _hilo_rows(inp["L1_attn_b_bo"])
    wsT = T(inp["L1_attn_b_Ws"])
    d["l1b_wsT"] = np.ascontiguousarray(np.concatenate([wsT, wsT], 0))
    d["l1b_bs"] = np.ascontiguousarray(inp["L1_attn_b_bs"].reshape(16, 1))
    d["l1f_gwT"] = T(inp["L1_ffn_gate_w"])
    d["l1f_wg"] = np.ascontiguousarray(inp["L1_ffn_Wg"].transpose(0, 2, 1))
    d["l1f_wu"] = np.ascontiguousarray(inp["L1_ffn_Wu"].transpose(0, 2, 1))
    d["l1f_wd"] = np.ascontiguousarray(inp["L1_ffn_Wd"].transpose(0, 2, 1))
    d["l1f_wsg"] = T(inp["L1_ffn_Ws_g"])
    d["l1f_wsu"] = T(inp["L1_ffn_Ws_u"])
    d["l1f_wsd"] = T(inp["L1_ffn_Ws_d"])
    return d



def _k(x):
    return x if isinstance(x, str) else x.tensor.name


def _apify(items):
    out = []
    for x in items:
        if isinstance(x, str) or hasattr(x, "tensor"):
            out.append(x)
    return out


class _B:
    pass


def _sb(c, es, name, rows, cols, dtype):
    return es.enter_context(c.nc.sbuf_tensor(name, [rows, cols], dtype))


def _ps(c, es, name, rows, cols):
    return es.enter_context(c.nc.psum_tensor(name, [rows, cols], F32))


_UNIQ = [0]
def _nm(base):
    _UNIQ[0] += 1
    return f"{base}_{_UNIQ[0]}"


def _vop(p, meth, out_ap, ins, reads=(), writes=(), eng="dve", **kw):
    def build(e, meth=meth, out_ap=out_ap, ins=ins, kw=kw):
        return getattr(e, meth)(out_ap, *ins, **kw)
    return p.ex(eng, build,
                reads=_apify(list(ins)) + _apify(list(reads)) + _apify(list(kw.values())),
                writes=[out_ap] + _apify(list(writes)))


def _act(p, out_ap, in_ap, func, bias=0.0, scale=1.0, accum=None):
    def build(e, out_ap=out_ap, in_ap=in_ap, func=func, bias=bias, scale=scale, accum=accum):
        if accum is not None:
            return e.activation(out_ap, in_ap, func, bias=bias, scale=scale, accum_out=accum)
        return e.activation(out_ap, in_ap, func, bias=bias, scale=scale)
    writes = [out_ap] + ([accum] if accum is not None else [])
    return p.ex("act", build, reads=_apify([in_ap, bias, scale]), writes=writes)


def _sadd(p, out_ap, in_ap, addv):
    def build(e, out_ap=out_ap, in_ap=in_ap, addv=addv):
        return e.add(out_ap, in_ap, addv)
    return p.ex("act", build, reads=_apify([in_ap, addv]), writes=[out_ap])


def _wslab_load(c, slab, wT_dram, din, dout):
    src = wT_dram.rearrange("(ic p) o -> p ic o", p=128)
    dst = slab[:].rearrange("p (ic o) -> p ic o", o=dout)
    return c.p.dma("pool", dst, src)


def _ln_tiles(c, es, b, xsrc_t, xn_fm, stats, rstd, tiles, pT, xw, xnf, xn_dram_t=None,
              st12=None):
    p, nc = c.p, c.nc
    for j, tt in enumerate(tiles):
        w = xw[j % 2]
        p.dma("sp", w[:], xsrc_t[b * NT + tt])
        _vop(p, "bn_stats", st12[:, 0:6], [w[:, 0:512]])
        _vop(p, "bn_stats", st12[:, 6:12], [w[:, 512:1024]])
        _vop(p, "bn_aggr", stats[:, 2 * tt:2 * tt + 2], [st12[:]])
        _act(p, rstd[:, tt:tt + 1], stats[:, 2 * tt + 1:2 * tt + 2], AF.Ln,
             bias=c.konst[:, 0:1])
        _act(p, rstd[:, tt:tt + 1], rstd[:, tt:tt + 1], AF.Exp, scale=-0.5)
        xf = xnf[j % 2]
        _vop(p, "tensor_scalar", xf[:], [w[:], stats[:, 2 * tt:2 * tt + 1],
                                         rstd[:, tt:tt + 1]],
             op0=ALU.subtract, op1=ALU.mult)
        if xn_dram_t is not None:
            p.dma("pool", xn_dram_t[b * NT + tt], xf[:])
        for half in range(2):
            pb = pT[half]
            for q in range(4):
                dc = half * 4 + q
                p.ex("pe", lambda e, o=pb[:, q * 128:(q + 1) * 128],
                     i=xf[:, dc * 128:(dc + 1) * 128], idn=c.ident[:]:
                     e.transpose(o, i, idn),
                     reads=[xf[:], c.ident[:]], writes=[pb[:]])
            dst = xn_fm[:].rearrange("p (dc t) -> p dc t", t=S)[
                :, half * 4:half * 4 + 4, tt * 128:(tt + 1) * 128]
            src = pb[:].rearrange("p (dc t) -> p dc t", t=128)
            _vop(p, "tensor_copy", dst, [src])


def _proj_fm(c, out_fm, slab, din, dout, src_fm, src_w, bias_col, pbanks, tlo, thi,
             out_w=S):
    p = c.p
    nic, noc = din // 128, dout // 128
    nb = len(pbanks)
    i = 0
    for oc in range(noc):
        for t0 in range(tlo, thi, 512):
            pb = pbanks[i % nb]; i += 1
            terms = [(pb[:, 0:512],
                      slab[:, ic * dout + oc * 128: ic * dout + oc * 128 + 128],
                      src_fm[:, ic * src_w + t0: ic * src_w + t0 + 512])
                     for ic in range(nic)]
            p.mmg(terms)
            dst = out_fm[:, oc * out_w + (t0 - tlo): oc * out_w + (t0 - tlo) + 512]
            bias = bias_col[:, oc:oc + 1] if bias_col is not None else 0.0
            _sadd(p, dst, pb[:, 0:512], bias)


def _proj_tm(c, out_tm, slab, din, dout, src_fm, src_w, bias2, pbanks, tiles):
    p = c.p
    nic = din // 128
    nb = len(pbanks)
    i = 0
    for tt in tiles:
        for o0 in range(0, dout, 512):
            pb = pbanks[i % nb]; i += 1
            terms = []
            if bias2 is not None:
                terms.append((pb[:, 0:512], c.ones2[:, 0:128], bias2[:, o0:o0 + 512]))
            terms += [(pb[:, 0:512],
                       src_fm[:, ic * src_w + tt * 128: ic * src_w + tt * 128 + 128],
                       slab[:, ic * dout + o0: ic * dout + o0 + 512])
                      for ic in range(nic)]
            p.mmg(terms)
            _vop(p, "tensor_copy", out_tm[:, tt * dout + o0: tt * dout + o0 + 512],
                 [pb[:, 0:512]])


def _attn_out_proj(c, es, b, qch, attn_fm, wo_slab, bo2_sb, xdst_t, pbanks, xr,
                   resid_src_t):
    p = c.p
    for j in range(4):
        tt = qch * 4 + j
        xt = xr[j % 2]
        p.dma("sp", xt[:], resid_src_t[b * NT + tt])
        for oh in range(2):
            pb = pbanks[(j * 2 + oh) % len(pbanks)]
            terms = [(pb[:, 0:512], c.ones2[:, 0:128], bo2_sb[:, oh * 512:oh * 512 + 512])]
            terms += [(pb[:, 0:512],
                       attn_fm[:, dc * 512 + j * 128: dc * 512 + j * 128 + 128],
                       wo_slab[:, dc * 1024 + oh * 512: dc * 1024 + oh * 512 + 512])
                      for dc in range(8)]
            p.mmg(terms)
            _vop(p, "tensor_add", xt[:, oh * 512:oh * 512 + 512],
                 [xt[:, oh * 512:oh * 512 + 512], pb[:, 0:512]])
        p.dma("act", xdst_t[b * NT + tt], xt[:])


def _load_bias_rows2(c, es, name, dram, n):
    t = _sb(c, es, _nm(name), 2, n, BF16)
    c.p.dma("pool", t[:], dram)
    return t


def _load_bias_col(c, es, name, dram, ncol):
    t = _sb(c, es, _nm(name), 128, ncol, F32)
    c.p.dma("sp", t[:], dram)
    return t


def _attn_core(c, es, b, kind, K_fm, V_tm, wq_slab, wo_slab, bq_col, bo2_sb,
               resid_src_t, xdst_t, xn_fm, xdst_is_out=False, cq_fm=None):
    p, nc = c.p, c.nc
    scale = 0.125 if kind in ("l0a", "l1a") else 1.0 / 32.0
    KW = 1536 if kind == "l0a" else 2048
    Qf = _sb(c, es, _nm("Qf"), 128, 8 * 512, BF16)
    probs = [_sb(c, es, _nm("probs"), 128, KW, BF16) for _ in range(2)]
    probsT = [_sb(c, es, _nm("probsT"), 128, KW, BF16) for _ in range(2)]
    if kind == "l0a":
        asb = [_sb(c, es, _nm("asb"), 128, KW, F32) for _ in range(2)]
        pS = [_ps(c, es, _nm("pS"), 128, 512) for _ in range(4)]
        sc_bank = lambda par, i2: pS[par * 2 + (i2 % 2)][:]
        pAV = [_ps(c, es, _nm("pAV"), 128, 512) for _ in range(2)]
        av_ap = lambda par: pAV[par][0:64, 0:128]
        pb_t = [pAV[0][:, 0:512], pAV[1][:, 0:512]]
        exp_src = lambda par, i2: asb[par][:, i2 * 512:(i2 + 1) * 512]
    else:
        asb = None
        pS = [_ps(c, es, _nm("pS"), 128, 2048) for _ in range(2)]
        sc_bank = lambda par, i2: pS[par][:, i2 * 512:(i2 + 1) * 512]
        av_ap = lambda par: pS[par][0:64, 0:128]
        pb_t = [pS[0][:, 0:512], pS[1][:, 0:512]]
        exp_src = sc_bank
    afm = _sb(c, es, _nm("afm"), 128, 8 * 512, BF16)
    xr = [_sb(c, es, _nm("xr"), 128, 1024, F32) for _ in range(2)]
    nmc = _sb(c, es, _nm("nmc"), 128, 2 * 4, F32)
    nm1 = _sb(c, es, _nm("nm"), 128, 4, F32)
    zc = _sb(c, es, _nm("zc"), 128, 2 * 4, F32)
    zz = _sb(c, es, _nm("zz"), 128, 2, F32)
    sidx = 0

    for qch in range(4):
        if kind == "l1a":
            nic, src, srcw = 2, cq_fm, S
            off = qch * 512
        else:
            nic, src, srcw = 8, xn_fm, S
            off = qch * 512
        for oc in range(8):
            pb = pb_t[oc % 2]
            terms = [(pb, wq_slab[:, ic * 1024 + oc * 128: ic * 1024 + oc * 128 + 128],
                      src[:, ic * srcw + off: ic * srcw + off + 512])
                     for ic in range(nic)]
            p.mmg(terms)
            bias = bq_col[:, oc:oc + 1] if bq_col is not None else 0.0
            _sadd(p, Qf[:, oc * 512:(oc + 1) * 512], pb, bias)

        for h in range(16):
            for j in range(4):
                qt = qch * 4 + j
                qb = qt * 128
                par = sidx % 2
                sidx += 1
                if kind == "l0a":
                    kc_lo = max(0, qb - 1023) // 512
                    kc_hi = (qb + 127) // 512
                elif kind == "l1a":
                    kc_lo, kc_hi = 0, (qb + 127) // 512
                else:
                    kc_lo, kc_hi = 0, 3
                nkc = kc_hi - kc_lo + 1
                bw = nkc * 512
                lq = Qf[(h % 2) * 64:(h % 2) * 64 + 64,
                        (h // 2) * 512 + j * 128: (h // 2) * 512 + j * 128 + 128]
                for i2, kc in enumerate(range(kc_lo, kc_hi + 1)):
                    pb = sc_bank(par, i2)
                    rk = K_fm[(h % 2) * 64:(h % 2) * 64 + 64,
                              (h // 2) * 2048 + kc * 512: (h // 2) * 2048 + kc * 512 + 512]
                    p.mmg([(pb, lq, rk)])
                    if kind == "l0a":
                        co = 2047 - (qb - kc * 512)
                        slope = (2.0 ** (-2.0 ** -1.0)) ** (h + 1)
                        _vop(p, "scalar_tensor_tensor",
                             asb[par][:, i2 * 512:(i2 + 1) * 512],
                             [c.strip[:, co:co + 512], 8.0 * slope, pb],
                             op0=ALU.mult, op1=ALU.add)
                    elif kind == "l1a" and kc == kc_hi:
                        co = 2047 - (qb - kc * 512)
                        _vop(p, "tensor_add", pb, [pb, c.strip[:, co:co + 512]])
                    _vop(p, "reduce_max", nmc[:, par * 4 + i2: par * 4 + i2 + 1],
                         [exp_src(par, i2)], axis=AX.X, negate=True)
                nmv = nm1[:, par:par + 1]
                nms = nm1[:, 2 + par:3 + par]
                zzv = zz[:, par:par + 1]
                _vop(p, "tensor_reduce", nmv, [nmc[:, par * 4: par * 4 + nkc]],
                     axis=AX.X, op=ALU.min)
                _vop(p, "tensor_scalar_mul", nms, [nmv, scale])
                for i2 in range(nkc):
                    _act(p, probs[par][:, i2 * 512:(i2 + 1) * 512], exp_src(par, i2),
                         AF.Exp, bias=nms, scale=scale,
                         accum=zc[:, par * 4 + i2: par * 4 + i2 + 1])
                _vop(p, "reduce_sum", zzv, [zc[:, par * 4: par * 4 + nkc]], axis=AX.X)
                _vop(p, "reciprocal", zzv, [zzv])
                pav = av_ap(par)
                for i2 in range(nkc):
                    sl = probs[par][:, i2 * 512:(i2 + 1) * 512]
                    _vop(p, "tensor_scalar_mul", sl, [sl, zzv])
                    p.dma("sp", probsT[par][:, i2 * 512:(i2 + 1) * 512].rearrange(
                        "p (kt q) -> p kt q", q=128), sl, transpose=True)
                    terms = []
                    for kt in range(4):
                        ktg = (kc_lo + i2) * 4 + kt
                        terms.append((pav,
                                      V_tm[:, ktg * 1024 + h * 64: ktg * 1024 + h * 64 + 64],
                                      probsT[par][:, (i2 * 4 + kt) * 128:
                                                  (i2 * 4 + kt) * 128 + 128]))
                    p.mmg(terms, start=(i2 == 0), stop=(i2 == nkc - 1))
                dst = afm[(h % 2) * 64:(h % 2) * 64 + 64,
                          (h // 2) * 512 + j * 128: (h // 2) * 512 + j * 128 + 128]
                _vop(p, "tensor_copy", dst, [pav])
        _attn_out_proj(c, es, b, qch, afm, wo_slab, bo2_sb, xdst_t,
                       pb_t, xr, resid_src_t)



def _stage_attn_std(c, b, kind, xsrc_t, xdst_t, wk, wv, wq, wo, bq, bk, bv2, bo2):
    p = c.p
    with p.phase() as eso:
        if kind == "l0a":
            c.strip = _sb(c, eso, _nm("strip"), 128, 2560, F32)
            p.dma("sp", c.strip[:], c.dram["c_strip_alibi"])
        slab_a = _sb(c, eso, _nm("slabA"), 128, 8 * 1024, BF16)
        slab_b = _sb(c, eso, _nm("slabB"), 128, 8 * 1024, BF16)
        K_fm = _sb(c, eso, _nm("Kfm"), 128, 8 * 2048, BF16)
        V_tm = _sb(c, eso, _nm("Vtm"), 128, 16 * 1024, BF16)
        stats = _sb(c, eso, _nm("stats"), 128, 2 * NT, F32)
        rstd = _sb(c, eso, _nm("rstd"), 128, NT, F32)
        bq_col = _load_bias_col(c, eso, "bq", c.dram[bq], 8)
        bk_col = _load_bias_col(c, eso, "bk", c.dram[bk], 8)
        bv2_sb = _load_bias_rows2(c, eso, "bv2", c.dram[bv2], 1024)
        bo2_sb = _load_bias_rows2(c, eso, "bo2", c.dram[bo2], 1024)
        xn_fm = _sb(c, eso, _nm("xnfm"), 128, 8 * 2048, BF16)
        _wslab_load(c, slab_a, c.dram[wk], 1024, 1024)
        _wslab_load(c, slab_b, c.dram[wv], 1024, 1024)
        with p.phase() as es1:
            st12 = _sb(c, es1, _nm("st12"), 128, 12, F32)
            xw = [_sb(c, es1, _nm("xw"), 128, 1024, F32) for _ in range(2)]
            xnf = [_sb(c, es1, _nm("xnf"), 128, 1024, F32) for _ in range(2)]
            pT = [_ps(c, es1, _nm("pT"), 128, 512) for _ in range(2)]
            pK = [_ps(c, es1, _nm("pK"), 128, 512) for _ in range(2)]
            pV = [_ps(c, es1, _nm("pV"), 128, 512) for _ in range(2)]
            _ln_tiles(c, es1, b, xsrc_t, xn_fm, stats, rstd, range(NT), pT,
                      xw, xnf, st12=st12)
            _proj_fm(c, K_fm, slab_a, 1024, 1024, xn_fm, S, bk_col, pK, 0, S)
            _proj_tm(c, V_tm, slab_b, 1024, 1024, xn_fm, S, bv2_sb, pV, range(NT))
        _wslab_load(c, slab_a, c.dram[wq], 1024, 1024)
        _wslab_load(c, slab_b, c.dram[wo], 1024, 1024)
        with p.phase() as es2:
            _attn_core(c, es2, b, kind, K_fm, V_tm, slab_a, slab_b, bq_col,
                       bo2_sb, xsrc_t, xdst_t, xn_fm)


def _stage_l1a(c, b, xsrc_t, xdst_t):
    p = c.p
    with p.phase() as eso:
        c.strip = _sb(c, eso, _nm("strip"), 128, 2560, F32)
        p.dma("sp", c.strip[:], c.dram["c_strip_causal"])
        s_dq = _sb(c, eso, _nm("sdq"), 128, 8 * 256, BF16)
        s_dkv = _sb(c, eso, _nm("sdkv"), 128, 8 * 256, BF16)
        s_uk = _sb(c, eso, _nm("suk"), 128, 2 * 1024, BF16)
        s_uv = _sb(c, eso, _nm("suv"), 128, 2 * 1024, BF16)
        K_fm = _sb(c, eso, _nm("Kfm"), 128, 8 * 2048, BF16)
        V_tm = _sb(c, eso, _nm("Vtm"), 128, 16 * 1024, BF16)
        cq_fm = _sb(c, eso, _nm("cqfm"), 128, 2 * 2048, BF16)
        ckv_fm = _sb(c, eso, _nm("ckvfm"), 128, 2 * 2048, BF16)
        stats = _sb(c, eso, _nm("stats"), 128, 2 * NT, F32)
        rstd = _sb(c, eso, _nm("rstd"), 128, NT, F32)
        st12 = _sb(c, eso, _nm("st12"), 128, 12, F32)
        xw = [_sb(c, eso, _nm("xw"), 128, 1024, F32) for _ in range(2)]
        xnf = [_sb(c, eso, _nm("xnf"), 128, 1024, F32) for _ in range(2)]
        zero2 = _sb(c, eso, _nm("zero2"), 2, 1024, BF16)
        _vop(p, "memset", zero2[:], [0.0], eng="dve")
        _wslab_load(c, s_dq, c.dram["l1a_W_dq"], 1024, 256)
        _wslab_load(c, s_dkv, c.dram["l1a_W_dkv"], 1024, 256)
        _wslab_load(c, s_uk, c.dram["l1a_W_uk"], 256, 1024)
        _wslab_load(c, s_uv, c.dram["l1a_W_uv"], 256, 1024)
        with p.phase() as es1:
            xn_fm = _sb(c, es1, _nm("xnfm"), 128, 8 * 2048, BF16)
            pT = [_ps(c, es1, _nm("pT"), 128, 512) for _ in range(2)]
            pK = [_ps(c, es1, _nm("pK"), 128, 512) for _ in range(2)]
            pV = [_ps(c, es1, _nm("pV"), 128, 512) for _ in range(2)]
            _ln_tiles(c, es1, b, xsrc_t, xn_fm, stats, rstd, range(NT), pT,
                      xw, xnf, st12=st12)
            _proj_fm(c, cq_fm, s_dq, 1024, 256, xn_fm, S, None, pK, 0, S)
            _proj_fm(c, ckv_fm, s_dkv, 1024, 256, xn_fm, S, None, pV, 0, S)
            _proj_fm(c, K_fm, s_uk, 256, 1024, ckv_fm, S, None, pK, 0, S)
            _proj_tm(c, V_tm, s_uv, 256, 1024, ckv_fm, S, None, pV, range(NT))
        s_uq = _sb(c, eso, _nm("suq"), 128, 2 * 1024, BF16)
        s_o = _sb(c, eso, _nm("so"), 128, 8 * 1024, BF16)
        _wslab_load(c, s_uq, c.dram["l1a_W_uq"], 256, 1024)
        _wslab_load(c, s_o, c.dram["l1a_W_o"], 1024, 1024)
        with p.phase() as es2:
            _attn_core(c, es2, b, "l1a", K_fm, V_tm, s_uq, s_o, None,
                       zero2, xsrc_t, xdst_t, None, cq_fm=cq_fm)


def _stage_l0f(c, b, xsrc_t, xdst_t):
    p = c.p
    with p.phase() as eso:
        xn_fm = _sb(c, eso, _nm("xnfm"), 128, 8 * 2048, BF16)
        acc = _sb(c, eso, _nm("acc"), 128, 16 * 1024, F32)
        C_sb = _sb(c, eso, _nm("Csb"), 16, 2048, BF16)
        stats = _sb(c, eso, _nm("stats"), 128, 2 * NT, F32)
        rstd = _sb(c, eso, _nm("rstd"), 128, NT, F32)
        with p.phase() as es1:
            st12 = _sb(c, es1, _nm("st12"), 128, 12, F32)
            xw = [_sb(c, es1, _nm("xw"), 128, 1024, F32) for _ in range(2)]
            xnf = [_sb(c, es1, _nm("xnf"), 128, 1024, F32) for _ in range(2)]
            rw_slab = _sb(c, es1, _nm("rws"), 128, 8 * 16, BF16)
            rb2_sb = _load_bias_rows2(c, es1, "rb2", c.dram["l0f_rb2"], 16)
            _wslab_load(c, rw_slab, c.dram["l0f_rw"], 1024, 16)
            pT = [_ps(c, es1, _nm("pT"), 128, 512) for _ in range(2)]
            pR = [_ps(c, es1, _nm("pR"), 128, 512) for _ in range(2)]
            _ln_tiles(c, es1, b, xsrc_t, xn_fm, stats, rstd, range(NT), pT,
                      xw, xnf, st12=st12)
            prb = _sb(c, es1, _nm("prb"), 128, 16, F32)
            m8 = _sb(c, es1, _nm("m8"), 128, 8, F32)
            msk = _sb(c, es1, _nm("msk"), 128, 16, F32)
            ctm = _sb(c, es1, _nm("ctm"), 128, 16, F32)
            nmz = _sb(c, es1, _nm("nmz"), 128, 2, F32)
            for tt in range(NT):
                pb = pR[tt % 2]
                terms = [(pb[:, 0:16], c.ones2[:, 0:128], rb2_sb[:, 0:16])]
                terms += [(pb[:, 0:16],
                           xn_fm[:, ic * 2048 + tt * 128: ic * 2048 + tt * 128 + 128],
                           rw_slab[:, ic * 16: ic * 16 + 16]) for ic in range(8)]
                p.mmg(terms)
                _vop(p, "reduce_max", nmz[:, 0:1], [pb[:, 0:16]], axis=AX.X, negate=True)
                _act(p, prb[:], pb[:, 0:16], AF.Exp, bias=nmz[:, 0:1])
                _vop(p, "max", m8[:], [prb[:]])
                _vop(p, "tensor_scalar", msk[:], [prb[:], m8[:, 3:4], None],
                     op0=ALU.is_ge)
                _vop(p, "tensor_mul", msk[:], [msk[:], prb[:]])
                _vop(p, "reduce_sum", nmz[:, 1:2], [msk[:]], axis=AX.X)
                _vop(p, "reciprocal", nmz[:, 1:2], [nmz[:, 1:2]])
                _vop(p, "tensor_scalar_mul", ctm[:], [msk[:], nmz[:, 1:2]])
                p.ex("pe", lambda e, o=pb[0:16, 16:144], i=ctm[:], idn=c.ident[:]:
                     e.transpose(o, i, idn), reads=[ctm[:], c.ident[:]],
                     writes=[pb])
                _vop(p, "tensor_copy", C_sb[0:16, tt * 128:(tt + 1) * 128],
                     [pb[0:16, 16:144]])
        with p.phase() as es2:
            w1s = [_sb(c, es2, _nm("w1s"), 128, 8 * 2048, BF16)]
            w2s = [_sb(c, es2, _nm("w2s"), 128, 16 * 1024, BF16)]
            hbuf = _sb(c, es2, _nm("hbuf"), 128, 16 * 512, BF16)
            b1_all = _sb(c, es2, _nm("b1a"), 128, 16 * 16, F32)
            p.dma("sp", b1_all[:], c.dram["l0f_b1"])
            b2_sb = _sb(c, es2, _nm("b2s"), 16, 1024, BF16)
            p.dma("pool", b2_sb[:], c.dram["l0f_b2"])
            _wslab_load(c, w1s[0], c.dram["l0f_w1"][0], 1024, 2048)
            _wslab_load(c, w2s[0], c.dram["l0f_w2"][0], 2048, 1024)
            pUP = [_ps(c, es2, _nm("pUP"), 128, 512) for _ in range(2)]
            pDN = [_ps(c, es2, _nm("pDN"), 128, 512) for _ in range(2)]
            pCE = _ps(c, es2, _nm("pCE"), 128, 512)
            ceb = _sb(c, es2, _nm("ceb"), 128, 512, BF16)
            for tt in range(NT):
                for dh in range(2):
                    pb = pDN[(tt * 2 + dh) % 2]
                    p.mmg([(pb[:, 0:512], C_sb[0:16, tt * 128:(tt + 1) * 128],
                            b2_sb[0:16, dh * 512:(dh + 1) * 512])])
                    _vop(p, "tensor_copy",
                         acc[:, tt * 1024 + dh * 512: tt * 1024 + dh * 512 + 512],
                         [pb[:, 0:512]])
            for e in range(16):
                for tch in range(4):
                    p.mmg([(pCE[:, 0:512], c.e16[:, e * 128:(e + 1) * 128],
                            C_sb[0:16, tch * 512:(tch + 1) * 512])])
                    _vop(p, "tensor_copy", ceb[:], [pCE[:, 0:512]])
                    for fc in range(16):
                        pb = pUP[fc % 2]
                        terms = [(pb[:, 0:512],
                                  w1s[0][:, ic * 2048 + fc * 128: ic * 2048 + fc * 128 + 128],
                                  xn_fm[:, ic * 2048 + tch * 512: ic * 2048 + tch * 512 + 512])
                                 for ic in range(8)]
                        p.mmg(terms)
                        _act(p, hbuf[:, fc * 512:(fc + 1) * 512], pb[:, 0:512],
                             AF.Gelu, bias=b1_all[:, e * 16 + fc: e * 16 + fc + 1])
                    _vop(p, "tensor_tensor", hbuf[:].rearrange("p (fc t) -> p fc t", t=512),
                         [hbuf[:].rearrange("p (fc t) -> p fc t", t=512),
                          ceb[:].rearrange("p (o t) -> p o t", o=1).to_broadcast([128, 16, 512])],
                         op=ALU.mult)
                    for j in range(4):
                        tt = tch * 4 + j
                        for dh in range(2):
                            pb = pDN[(j * 2 + dh) % 2]
                            terms = [(pb[:, 0:512],
                                      hbuf[:, fc * 512 + j * 128: fc * 512 + j * 128 + 128],
                                      w2s[0][:, fc * 1024 + dh * 512: fc * 1024 + dh * 512 + 512])
                                     for fc in range(16)]
                            p.mmg(terms)
                            dst = acc[:, tt * 1024 + dh * 512: tt * 1024 + dh * 512 + 512]
                            _vop(p, "tensor_add", dst, [dst, pb[:, 0:512]])
                if e + 1 < 16:
                    _wslab_load(c, w1s[0], c.dram["l0f_w1"][e + 1], 1024, 2048)
                    _wslab_load(c, w2s[0], c.dram["l0f_w2"][e + 1], 2048, 1024)
        with p.phase() as es3:
            xr = [_sb(c, es3, _nm("xr"), 128, 1024, F32) for _ in range(2)]
            for tt in range(NT):
                xt = xr[tt % 2]
                p.dma("sp", xt[:], xsrc_t[b * NT + tt])
                _vop(p, "tensor_add", xt[:], [xt[:], acc[:, tt * 1024:(tt + 1) * 1024]])
                p.dma("act", xdst_t[b * NT + tt], xt[:])


def _rope_combine(c, pA, pB, cosv, sinv, dst):
    p = c.p
    t1 = c.rs1
    t2 = c.rs2
    _vop(p, "tensor_mul", t1[:], [pB, sinv])
    _vop(p, "tensor_mul", t2[:], [pA, cosv])
    _vop(p, "tensor_add", dst, [t1[:], t2[:]])


def _stage_l1b(c, b, xsrc_t, xdst_t):
    p = c.p
    with p.phase() as eso:
        c.strip = _sb(c, eso, _nm("strip"), 128, 2560, F32)
        p.dma("sp", c.strip[:], c.dram["c_strip_causal"])
        K_ro = _sb(c, eso, _nm("Kro"), 128, 4 * 2048, BF16)
        V_tm = _sb(c, eso, _nm("Vtm"), 128, 16 * 256, BF16)
        k_sp = _sb(c, eso, _nm("ksp"), 16, 4 * 2048, BF16)
        stats = _sb(c, eso, _nm("stats"), 128, 2 * NT, F32)
        rstd = _sb(c, eso, _nm("rstd"), 128, NT, F32)
        cosb = _sb(c, eso, _nm("cosb"), 128, 2048, F32)
        sinb = _sb(c, eso, _nm("sinb"), 128, 2048, F32)
        wsT = _sb(c, eso, _nm("wsT"), 128, 16, BF16)
        bs_col = _sb(c, eso, _nm("bs"), 16, 1, F32)
        bo2_sb = _load_bias_rows2(c, eso, "bo2", c.dram["l1b_bo2"], 1024)
        c.rs1 = _sb(c, eso, _nm("rs1"), 128, 512, F32)
        c.rs2 = _sb(c, eso, _nm("rs2"), 128, 512, F32)
        p.dma("sp", cosb[:], c.dram["c_cos"])
        p.dma("sp", sinb[:], c.dram["c_sin"])
        p.dma("pool", wsT[:], c.dram["l1b_wsT"])
        p.dma("sp", bs_col[:], c.dram["l1b_bs"])
        with p.phase() as es1:
            xn_fm = _sb(c, es1, _nm("xnfm"), 128, 8 * 2048, BF16)
            st12 = _sb(c, es1, _nm("st12"), 128, 12, F32)
            xw = [_sb(c, es1, _nm("xw"), 128, 1024, F32) for _ in range(2)]
            xnf = [_sb(c, es1, _nm("xnf"), 128, 1024, F32) for _ in range(2)]
            s_k = _sb(c, es1, _nm("sk"), 128, 8 * 512, BF16)
            s_kr = _sb(c, es1, _nm("skr"), 128, 8 * 512, BF16)
            s_v = _sb(c, es1, _nm("sv"), 128, 8 * 256, BF16)
            _wslab_load(c, s_k, c.dram["l1b_wkT"], 1024, 512)
            _wslab_load(c, s_kr, c.dram["l1b_wkTr"], 1024, 512)
            _wslab_load(c, s_v, c.dram["l1b_wvT"], 1024, 256)
            pT = [_ps(c, es1, _nm("pT"), 128, 512) for _ in range(2)]
            pK = [_ps(c, es1, _nm("pK"), 128, 512) for _ in range(2)]
            pV = [_ps(c, es1, _nm("pV"), 128, 512) for _ in range(2)]
            _ln_tiles(c, es1, b, xsrc_t, xn_fm, stats, rstd, range(NT), pT,
                      xw, xnf, st12=st12)
            for oc in range(4):
                for tch in range(4):
                    pa, pb = pK[0][:, 0:512], pK[1][:, 0:512]
                    t0 = tch * 512
                    terms = [(pa, s_k[:, ic * 512 + oc * 128: ic * 512 + oc * 128 + 128],
                              xn_fm[:, ic * 2048 + t0: ic * 2048 + t0 + 512])
                             for ic in range(8)]
                    p.mmg(terms)
                    terms = [(pb, s_kr[:, ic * 512 + oc * 128: ic * 512 + oc * 128 + 128],
                              xn_fm[:, ic * 2048 + t0: ic * 2048 + t0 + 512])
                             for ic in range(8)]
                    p.mmg(terms)
                    _rope_combine(c, pa, pb, cosb[:, t0:t0 + 512], sinb[:, t0:t0 + 512],
                                  K_ro[:, oc * 2048 + t0: oc * 2048 + t0 + 512])
            for tt in range(NT):
                pbv = pV[tt % 2][:, 0:256]
                terms = [(pbv,
                          xn_fm[:, ic * 2048 + tt * 128: ic * 2048 + tt * 128 + 128],
                          s_v[:, ic * 256: ic * 256 + 256]) for ic in range(8)]
                p.mmg(terms)
                _vop(p, "tensor_copy", V_tm[:, tt * 256:(tt + 1) * 256], [pbv])
            for kvh in range(4):
                for tch in range(4):
                    pbs = pT[tch % 2][0:16, 0:512]
                    rk = K_ro[0:64, kvh * 2048 + tch * 512: kvh * 2048 + tch * 512 + 512]
                    p.mmg([(pbs, wsT[0:64, 0:16], rk)])
                    _sadd(p, k_sp[0:16, kvh * 2048 + tch * 512: kvh * 2048 + tch * 512 + 512],
                          pbs, bs_col[0:16, 0:1])
        s_q = _sb(c, eso, _nm("sq"), 128, 8 * 1024, BF16)
        s_qr = _sb(c, eso, _nm("sqr"), 128, 8 * 1024, BF16)
        s_o = _sb(c, eso, _nm("so"), 128, 8 * 1024, BF16)
        _wslab_load(c, s_q, c.dram["l1b_wqT"], 1024, 1024)
        _wslab_load(c, s_qr, c.dram["l1b_wqTr"], 1024, 1024)
        _wslab_load(c, s_o, c.dram["l1b_woT"], 1024, 1024)
        with p.phase() as es2:
            xnf = [_sb(c, es2, _nm("xnf"), 128, 1024, F32) for _ in range(2)]
            xnq = _sb(c, es2, _nm("xnq"), 128, 8 * 512, BF16)
            Qro = _sb(c, es2, _nm("Qro"), 128, 8 * 512, BF16)
            q_sp = _sb(c, es2, _nm("qsp"), 16, 512, BF16)
            tnh = [c.rs1, c.rs2]
            probs = [_sb(c, es2, _nm("probs"), 128, 2048, BF16) for _ in range(2)]
            probsT = [_sb(c, es2, _nm("probsT"), 128, 2048, BF16) for _ in range(2)]
            asb = [_sb(c, es2, _nm("asb"), 128, 2048, F32) for _ in range(2)]
            afm = _sb(c, es2, _nm("afm"), 128, 8 * 512, BF16)
            xr = [_sb(c, es2, _nm("xr"), 128, 1024, F32) for _ in range(2)]
            nm1 = _sb(c, es2, _nm("nm"), 128, 4, F32)
            zz = _sb(c, es2, _nm("zz"), 128, 2, F32)
            pQK = [_ps(c, es2, _nm("pQK"), 128, 512) for _ in range(2)]
            pSP = [_ps(c, es2, _nm("pSP"), 128, 512) for _ in range(2)]
            pU = [_ps(c, es2, _nm("pU"), 128, 512) for _ in range(2)]
            pb_t = [pU[0][:, 0:512], pU[1][:, 0:512]]
            sidx = 0
            for qch in range(4):
                for j in range(4):
                    tt = qch * 4 + j
                    w = xr[j % 2]
                    p.dma("sp", w[:], xsrc_t[b * NT + tt])
                    xf = xnf[j % 2]
                    _vop(p, "tensor_scalar", xf[:], [w[:], stats[:, 2 * tt:2 * tt + 1],
                                                     rstd[:, tt:tt + 1]],
                         op0=ALU.subtract, op1=ALU.mult)
                    for half in range(2):
                        pbh = pb_t[half]
                        for q in range(4):
                            dc = half * 4 + q
                            p.ex("pe", lambda e, o=pbh[:, q * 128:(q + 1) * 128],
                                 i=xf[:, dc * 128:(dc + 1) * 128], idn=c.ident[:]:
                                 e.transpose(o, i, idn),
                                 reads=[xf[:], c.ident[:]], writes=[pbh])
                        dst = xnq[:].rearrange("p (dc t) -> p dc t", t=512)[
                            :, half * 4:half * 4 + 4, j * 128:(j + 1) * 128]
                        src = pbh.rearrange("p (dc t) -> p dc t", t=128)
                        _vop(p, "tensor_copy", dst, [src])
                t0g = qch * 512
                for oc in range(8):
                    pa, pb2 = pQK[0][:, 0:512], pQK[1][:, 0:512]
                    terms = [(pa, s_q[:, ic * 1024 + oc * 128: ic * 1024 + oc * 128 + 128],
                              xnq[:, ic * 512: ic * 512 + 512]) for ic in range(8)]
                    p.mmg(terms)
                    terms = [(pb2, s_qr[:, ic * 1024 + oc * 128: ic * 1024 + oc * 128 + 128],
                              xnq[:, ic * 512: ic * 512 + 512]) for ic in range(8)]
                    p.mmg(terms)
                    _rope_combine(c, pa, pb2, cosb[:, t0g:t0g + 512],
                                  sinb[:, t0g:t0g + 512], Qro[:, oc * 512:(oc + 1) * 512])
                for h in range(16):
                    kvh = h // 4
                    base = (h % 2) * 64
                    pbs = pSP[0][0:16, 0:512]
                    p.mmg([(pbs, wsT[base:base + 64, 0:16],
                            Qro[base:base + 64, (h // 2) * 512:(h // 2) * 512 + 512])])
                    _sadd(p, q_sp[0:16, 0:512], pbs, bs_col[0:16, 0:1])
                    for j in range(4):
                        qt = qch * 4 + j
                        qb = qt * 128
                        par = sidx % 2
                        sidx += 1
                        kc_hi = (qb + 127) // 512
                        bw = (kc_hi + 1) * 512
                        lq = Qro[base:base + 64, (h // 2) * 512 + j * 128:
                                 (h // 2) * 512 + j * 128 + 128]
                        for kc in range(kc_hi + 1):
                            pqk = pQK[kc % 2][:, 0:512]
                            psp = pSP[kc % 2][:, 0:512]
                            rk = K_ro[base:base + 64,
                                      kvh * 2048 + kc * 512: kvh * 2048 + kc * 512 + 512]
                            p.mmg([(pqk, lq, rk)])
                            p.mmg([(psp, q_sp[0:16, j * 128:j * 128 + 128],
                                    k_sp[0:16, kvh * 2048 + kc * 512: kvh * 2048 + kc * 512 + 512])])
                            tb = tnh[kc % 2]
                            _act(p, tb[:], psp, AF.Tanh, scale=0.125)
                            _vop(p, "scalar_tensor_tensor",
                                 asb[par][:, kc * 512:(kc + 1) * 512],
                                 [tb[:], 1.0, pqk], op0=ALU.add, op1=ALU.mult)
                            if kc == kc_hi:
                                co = 2047 - (qb - kc * 512)
                                _vop(p, "tensor_add",
                                     asb[par][:, kc * 512:(kc + 1) * 512],
                                     [asb[par][:, kc * 512:(kc + 1) * 512],
                                      c.strip[:, co:co + 512]])
                        nmv = nm1[:, par:par + 1]
                        nms = nm1[:, 2 + par:3 + par]
                        zzv = zz[:, par:par + 1]
                        _vop(p, "reduce_max", nmv, [asb[par][:, 0:bw]], axis=AX.X,
                             negate=True)
                        _vop(p, "tensor_scalar_mul", nms, [nmv, 1.0 / 16.0])
                        _act(p, probs[par][:, 0:bw], asb[par][:, 0:bw], AF.Exp,
                             bias=nms, scale=1.0 / 16.0, accum=zzv)
                        _vop(p, "reciprocal", zzv, [zzv])
                        _vop(p, "tensor_scalar_mul", probs[par][:, 0:bw],
                             [probs[par][:, 0:bw], zzv])
                        p.dma("sp", probsT[par][:, 0:bw].rearrange(
                            "p (kt q) -> p kt q", q=128), probs[par][:, 0:bw],
                            transpose=True)
                        pav = pQK[par][0:64, 0:128]
                        terms = []
                        for kt in range((kc_hi + 1) * 4):
                            terms.append((pav,
                                          V_tm[:, kt * 256 + kvh * 64: kt * 256 + kvh * 64 + 64],
                                          probsT[par][:, kt * 128: kt * 128 + 128]))
                        p.mmg(terms)
                        dst = afm[base:base + 64,
                                  (h // 2) * 512 + j * 128:(h // 2) * 512 + j * 128 + 128]
                        _vop(p, "tensor_copy", dst, [pav])
                _attn_out_proj(c, es2, b, qch, afm, s_o, bo2_sb, xdst_t,
                               pb_t, xr, xsrc_t)


def _stage_l1f(c, b, xsrc_t, xdst_t):
    p = c.p
    E, CAP = 16, 256
    with p.phase() as eso:
        acc = _sb(c, eso, _nm("acc"), 128, 16 * 1024, F32)
        stats = _sb(c, eso, _nm("stats"), 128, 2 * NT, F32)
        rstd = _sb(c, eso, _nm("rstd"), 128, NT, F32)
        valsT = _sb(c, eso, _nm("valsT"), 128, 2 * 16, F32)
        idxT = _sb(c, eso, _nm("idxT"), 128, 2 * 16, F32)
        with p.phase() as es1:
            xn_fm = _sb(c, es1, _nm("xnfm"), 128, 8 * 2048, BF16)
            st12 = _sb(c, es1, _nm("st12"), 128, 12, F32)
            xw = [_sb(c, es1, _nm("xw"), 128, 1024, F32) for _ in range(2)]
            xnf = [_sb(c, es1, _nm("xnf"), 128, 1024, F32) for _ in range(2)]
            gw_slab = _sb(c, es1, _nm("gws"), 128, 8 * 16, BF16)
            _wslab_load(c, gw_slab, c.dram["l1f_gwT"], 1024, 16)
            S_eb = _sb(c, es1, _nm("Seb"), 16, 2048, F32)
            prb = _sb(c, es1, _nm("prb"), 128, 16, F32)
            sctm = _sb(c, es1, _nm("sctm"), 128, 16, F32)
            nmz = _sb(c, es1, _nm("nmz"), 128, 2, F32)
            w0 = _sb(c, es1, _nm("w0"), 16, 2048, F32)
            w1 = _sb(c, es1, _nm("w1"), 16, 2048, F32)
            vals = _sb(c, es1, _nm("vals"), 16, 256, F32)
            idxu = _sb(c, es1, _nm("idxu"), 16, 256, U32)
            idxf = _sb(c, es1, _nm("idxf"), 16, 256, F32)
            pT = [_ps(c, es1, _nm("pT"), 128, 512) for _ in range(2)]
            pR = [_ps(c, es1, _nm("pR"), 128, 512) for _ in range(2)]
            _ln_tiles(c, es1, b, xsrc_t, xn_fm, stats, rstd, range(NT), pT,
                      xw, xnf, xn_dram_t=c.xnd_t, st12=st12)
            for tt in range(NT):
                pb = pR[tt % 2]
                terms = [(pb[:, 0:16],
                          xn_fm[:, ic * 2048 + tt * 128: ic * 2048 + tt * 128 + 128],
                          gw_slab[:, ic * 16: ic * 16 + 16]) for ic in range(8)]
                p.mmg(terms)
                _vop(p, "reduce_max", nmz[:, 0:1], [pb[:, 0:16]], axis=AX.X, negate=True)
                _act(p, prb[:], pb[:, 0:16], AF.Exp, bias=nmz[:, 0:1],
                     accum=nmz[:, 1:2])
                _vop(p, "reciprocal", nmz[:, 1:2], [nmz[:, 1:2]])
                _vop(p, "tensor_scalar_mul", sctm[:], [prb[:], nmz[:, 1:2]])
                p.ex("pe", lambda e, o=pb[0:16, 16:144], i=sctm[:], idn=c.ident[:]:
                     e.transpose(o, i, idn), reads=[sctm[:], c.ident[:]],
                     writes=[pb])
                _vop(p, "tensor_copy", S_eb[0:16, tt * 128:(tt + 1) * 128],
                     [pb[0:16, 16:144]])
            _vop(p, "tensor_copy", w0[:], [S_eb[:]])
            for r in range(32):
                src = w0 if r % 2 == 0 else w1
                dst = w1 if r % 2 == 0 else w0
                vs = vals[0:16, r * 8:(r + 1) * 8]
                _vop(p, "max", vs, [src[:]])
                _vop(p, "max_index", idxu[0:16, r * 8:(r + 1) * 8], [vs, src[:]])
                _vop(p, "match_replace", dst[:], [vs, src[:]], imm_value=-1.0)
            _vop(p, "tensor_copy", idxf[:], [idxu[:]])
            for ct in range(2):
                p.ex("pe", lambda e, o=pR[0][:, 0:16],
                     i=vals[0:16, ct * 128:(ct + 1) * 128], idn=c.ident[0:16, 0:16]:
                     e.transpose(o, i, idn), reads=[vals[:], c.ident[:]],
                     writes=[pR[0]])
                _vop(p, "tensor_copy", valsT[:, ct * 16:(ct + 1) * 16],
                     [pR[0][:, 0:16]])
                p.ex("pe", lambda e, o=pR[1][:, 0:16],
                     i=idxf[0:16, ct * 128:(ct + 1) * 128], idn=c.ident[0:16, 0:16]:
                     e.transpose(o, i, idn), reads=[idxf[:], c.ident[:]],
                     writes=[pR[1]])
                _vop(p, "tensor_copy", idxT[:, ct * 16:(ct + 1) * 16],
                     [pR[1][:, 0:16]])
        if "shared" not in _L1F_PARTS:
            with p.phase() as esz:
                zcl = _sb(c, esz, _nm("zcl"), 128, 1024, F32)
                _vop(p, "memset", zcl[:], [0.0])
                for tt in range(NT):
                    _vop(p, "tensor_copy", acc[:, tt * 1024:(tt + 1) * 1024], [zcl[:]])
        if "shared" in _L1F_PARTS:
         with p.phase() as es2:
            xnq2 = _sb(c, es2, _nm("xnq2"), 128, 8 * 512, BF16)
            wsd = _sb(c, es2, _nm("wsd"), 128, 16 * 1024, BF16)
            wsg = _sb(c, es2, _nm("wsg"), 128, 8 * 2048, BF16)
            wsu = _sb(c, es2, _nm("wsu"), 128, 8 * 2048, BF16)
            h_sh = _sb(c, es2, _nm("hsh"), 128, 16 * 512, BF16)
            gtmp = _sb(c, es2, _nm("gtmp"), 128, 512, BF16)
            xw2 = [_sb(c, es2, _nm("xw2"), 128, 1024, F32)]
            _wslab_load(c, wsd, c.dram["l1f_wsd"], 2048, 1024)
            _wslab_load(c, wsg, c.dram["l1f_wsg"], 1024, 2048)
            _wslab_load(c, wsu, c.dram["l1f_wsu"], 1024, 2048)
            pG = [_ps(c, es2, _nm("pG"), 128, 512) for _ in range(2)]
            pU = [_ps(c, es2, _nm("pU"), 128, 512) for _ in range(2)]
            pDN = [_ps(c, es2, _nm("pDN"), 128, 512) for _ in range(2)]
            for tch in range(4):
                for j in range(4):
                    tt = tch * 4 + j
                    w = xw2[0]
                    p.dma("sp", w[:], c.xnd_t[b * NT + tt])
                    for half in range(2):
                        pb = pG[half]
                        for q in range(4):
                            dc = half * 4 + q
                            p.ex("pe", lambda e, o=pb[:, q * 128:(q + 1) * 128],
                                 i=w[:, dc * 128:(dc + 1) * 128], idn=c.ident[:]:
                                 e.transpose(o, i, idn),
                                 reads=[w[:], c.ident[:]], writes=[pb])
                        dst = xnq2[:].rearrange("p (dc t) -> p dc t", t=512)[
                            :, half * 4:half * 4 + 4, j * 128:(j + 1) * 128]
                        src = pb[:].rearrange("p (dc t) -> p dc t", t=128)
                        _vop(p, "tensor_copy", dst, [src])
                for fc in range(16):
                    pg = pG[fc % 2][:, 0:512]
                    pu = pU[fc % 2][:, 0:512]
                    terms = [(pg, wsg[:, ic * 2048 + fc * 128: ic * 2048 + fc * 128 + 128],
                              xnq2[:, ic * 512: ic * 512 + 512]) for ic in range(8)]
                    p.mmg(terms)
                    terms = [(pu, wsu[:, ic * 2048 + fc * 128: ic * 2048 + fc * 128 + 128],
                              xnq2[:, ic * 512: ic * 512 + 512]) for ic in range(8)]
                    p.mmg(terms)
                    _act(p, gtmp[:], pg, AF.Gelu)
                    _vop(p, "tensor_tensor", h_sh[:, fc * 512:(fc + 1) * 512],
                         [gtmp[:], pu], op=ALU.mult)
                for j in range(4):
                    tt = tch * 4 + j
                    for dh in range(2):
                        pb = pDN[(j * 2 + dh) % 2][:, 0:512]
                        terms = [(pb,
                                  h_sh[:, fc2 * 512 + j * 128: fc2 * 512 + j * 128 + 128],
                                  wsd[:, fc2 * 1024 + dh * 512: fc2 * 1024 + dh * 512 + 512])
                                 for fc2 in range(16)]
                        p.mmg(terms)
                        _vop(p, "tensor_copy",
                             acc[:, tt * 1024 + dh * 512: tt * 1024 + dh * 512 + 512],
                             [pb])
        if "experts" in _L1F_PARTS:
         with p.phase() as es3:
            wg_h = _sb(c, es3, _nm("wgh"), 128, 8 * 1024, BF16)
            wu_h = _sb(c, es3, _nm("wuh"), 128, 8 * 1024, BF16)
            wd_s = _sb(c, es3, _nm("wds"), 128, 16 * 1024, BF16)
            xtk = _sb(c, es3, _nm("xtk"), 128, 2 * 1024, F32)
            xtf = _sb(c, es3, _nm("xtf"), 128, 8 * 256, BF16)
            h_fm = _sb(c, es3, _nm("hfm"), 128, 16 * 256, BF16)
            gtmp = _sb(c, es3, _nm("gtmp2"), 128, 256, BF16)
            eo = [_sb(c, es3, _nm("eo"), 128, 2 * 1024, BF16) for _ in range(4)]
            Ptt = _sb(c, es3, _nm("Ptt"), 128, 8 * 128, BF16)
            iu32 = _sb(c, es3, _nm("iu32"), 128, 2, U32)
            pG = [_ps(c, es3, _nm("pG3"), 128, 512) for _ in range(2)]
            pU = [_ps(c, es3, _nm("pU3"), 128, 512) for _ in range(2)]
            pDN = [_ps(c, es3, _nm("pDN3"), 128, 512) for _ in range(2)]
            pSC = [_ps(c, es3, _nm("pSC"), 128, 512) for _ in range(2)]
            xnd_rows = c.xnd
            for rnd in range(4):
                for er in range(4):
                    e = rnd * 4 + er
                    for ct in range(2):
                        _vop(p, "tensor_scalar_add", iu32[:, ct:ct + 1],
                             [idxT[:, ct * 16 + e: ct * 16 + e + 1], float(b * S)])
                        p.dma_gather(xtk[:, ct * 1024:(ct + 1) * 1024], xnd_rows,
                                     iu32[:, ct:ct + 1])
                        for half in range(2):
                            pb = pG[half]
                            for q in range(4):
                                dc = half * 4 + q
                                p.ex("pe", lambda e2, o=pb[:, q * 128:(q + 1) * 128],
                                     i=xtk[:, ct * 1024 + dc * 128: ct * 1024 + dc * 128 + 128],
                                     idn=c.ident[:]: e2.transpose(o, i, idn),
                                     reads=[xtk[:], c.ident[:]], writes=[pb])
                            dst = xtf[:].rearrange("p (dc t) -> p dc t", t=256)[
                                :, half * 4:half * 4 + 4, ct * 128:(ct + 1) * 128]
                            src = pb[:].rearrange("p (dc t) -> p dc t", t=128)
                            _vop(p, "tensor_copy", dst, [src])
                    _wslab_load(c, wd_s, c.dram["l1f_wd"][e], 2048, 1024)
                    for fh in range(2):
                        _wslab_load(c, wg_h,
                                    c.dram["l1f_wg"][e][:, fh * 1024:(fh + 1) * 1024],
                                    1024, 1024)
                        _wslab_load(c, wu_h,
                                    c.dram["l1f_wu"][e][:, fh * 1024:(fh + 1) * 1024],
                                    1024, 1024)
                        for fcl in range(8):
                            fc = fh * 8 + fcl
                            pg = pG[fcl % 2][:, 0:256]
                            pu = pU[fcl % 2][:, 0:256]
                            terms = [(pg,
                                      wg_h[:, ic * 1024 + fcl * 128: ic * 1024 + fcl * 128 + 128],
                                      xtf[:, ic * 256: ic * 256 + 256]) for ic in range(8)]
                            p.mmg(terms)
                            terms = [(pu,
                                      wu_h[:, ic * 1024 + fcl * 128: ic * 1024 + fcl * 128 + 128],
                                      xtf[:, ic * 256: ic * 256 + 256]) for ic in range(8)]
                            p.mmg(terms)
                            _act(p, gtmp[:], pg, AF.Gelu)
                            _vop(p, "tensor_tensor", h_fm[:, fc * 256:(fc + 1) * 256],
                                 [gtmp[:], pu], op=ALU.mult)
                    for ct in range(2):
                        for dh in range(2):
                            pb = pDN[(ct * 2 + dh) % 2][:, 0:512]
                            terms = [(pb,
                                      h_fm[:, fc2 * 256 + ct * 128: fc2 * 256 + ct * 128 + 128],
                                      wd_s[:, fc2 * 1024 + dh * 512: fc2 * 1024 + dh * 512 + 512])
                                     for fc2 in range(16)]
                            p.mmg(terms)
                            _vop(p, "tensor_copy",
                                 eo[er][:, ct * 1024 + dh * 512: ct * 1024 + dh * 512 + 512],
                                 [pb])
                for tt in range(NT):
                    for k in range(8):
                        er, ct = k // 2, k % 2
                        e = rnd * 4 + er
                        _vop(p, "tensor_scalar", Ptt[:, k * 128:(k + 1) * 128],
                             [c.iota_t[:, tt * 128:(tt + 1) * 128],
                              idxT[:, ct * 16 + e: ct * 16 + e + 1],
                              valsT[:, ct * 16 + e: ct * 16 + e + 1]],
                             op0=ALU.is_equal, op1=ALU.mult)
                    for dh in range(2):
                        pb = pSC[dh][:, 0:512]
                        terms = [(pb, Ptt[:, k * 128:(k + 1) * 128],
                                  eo[k // 2][:, (k % 2) * 1024 + dh * 512:
                                             (k % 2) * 1024 + dh * 512 + 512])
                                 for k in range(8)]
                        p.mmg(terms)
                        dst = acc[:, tt * 1024 + dh * 512: tt * 1024 + dh * 512 + 512]
                        _vop(p, "tensor_add", dst, [dst, pb])
        with p.phase() as es4:
            xr = [_sb(c, es4, _nm("xr"), 128, 1024, F32) for _ in range(2)]
            for tt in range(NT):
                xt = xr[tt % 2]
                p.dma("sp", xt[:], xsrc_t[b * NT + tt])
                _vop(p, "tensor_add", xt[:], [xt[:], acc[:, tt * 1024:(tt + 1) * 1024]])
                p.dma("act", xdst_t[b * NT + tt], xt[:])



def _build(arrs):
    nc = bass.Bass("TRN2", target_bir_lowering=False)
    es = ExitStack()
    c = _B()
    c.nc = nc
    c.dram = {}
    for k, a in arrs.items():
        dt = F32
        c.dram[k] = nc.dram_tensor(k, list(a.shape), dt, kind="ExternalInput").ap()
    x0 = nc.dram_tensor("x0", [BPC, S, D], F32, kind="ExternalInput").ap()
    out = nc.dram_tensor("out", [BPC, S, D], F32, kind="ExternalOutput").ap()
    xs1 = nc.dram_tensor("xs1", [BPC, S, D], F32, kind="Internal").ap()
    xs2 = nc.dram_tensor("xs2", [BPC, S, D], F32, kind="Internal").ap()
    xnd = nc.dram_tensor("xnd", [BPC * S, D], F32, kind="Internal").ap()
    tile = lambda ap: ap.rearrange("b (tt p) d -> (b tt) p d", p=128)
    x0_t, out_t, xs1_t, xs2_t = tile(x0), tile(out), tile(xs1), tile(xs2)

    p = c.p = Prog(nc, es)
    c.ident = _sb(c, es, "ident", 128, 128, F32)
    p.dma("sp", c.ident[:], c.dram["c_ident"])
    c.ones2 = _sb(c, es, "ones2", 2, 128, BF16)
    p.dma("pool", c.ones2[:], c.dram["c_ones2"])
    c.e16 = _sb(c, es, "e16", 16, 16 * 128, BF16)
    p.dma("pool", c.e16[:], c.dram["c_e16"])
    c.konst = _sb(c, es, "konst", 128, 4, F32)
    c.iota_t = _sb(c, es, "iotat", 128, 2048, F32)
    p.ex("pool", lambda e: e.iota(c.iota_t[:], [[1, 2048]], base=0,
                                  channel_multiplier=0,
                                  allow_small_or_imprecise_dtypes=True),
         writes=[c.iota_t[:]])
    _vop(p, "memset", c.konst[:, 0:1], [EPS])
    _vop(p, "memset", c.konst[:, 1:2], [0.0])
    c.xnd_t = xnd.rearrange("(n p) d -> n p d", p=128)
    c.xnd = xnd

    stages = sorted(_STAGES)
    last = stages[-1]

    def src_dst(i):
        srcs = x0_t if i == stages[0] else (xs1_t if i % 2 == 1 else xs2_t)
        dsts = out_t if i == last else (xs1_t if i % 2 == 0 else xs2_t)
        return srcs, dsts

    for b in range(BPC):
        for i in stages:
            xsrc_t, xdst_t = src_dst(i)
            if i == 0:
                _stage_attn_std(c, b, "l0a", xsrc_t, xdst_t,
                                "l0a_wkT", "l0a_wvT", "l0a_wqT", "l0a_woT",
                                "l0a_bq", "l0a_bk", "l0a_bv2", "l0a_bo2")
            elif i == 2:
                _stage_attn_std(c, b, "l0b", xsrc_t, xdst_t,
                                "l0b_wkT", "l0b_wvT", "l0b_wqT", "l0b_woT",
                                "l0b_bq", "l0b_bk", "l0b_bv2", "l0b_bo2")
            elif i == 4:
                _stage_l1a(c, b, xsrc_t, xdst_t)
            elif i in (1, 3):
                _stage_l0f(c, b, xsrc_t, xdst_t)
            elif i == 6:
                _stage_l1b(c, b, xsrc_t, xdst_t)
            else:
                _stage_l1f(c, b, xsrc_t, xdst_t)
    p.replay(es)
    c._es = es
    return nc, p


def kernel(**inputs):
    consts = _prep_consts()
    w = _prep_weights({k: np.asarray(v, dtype=np.float32) for k, v in inputs.items()
                       if k != "L0_attn_a_x"})
    x = np.asarray(inputs["L0_attn_a_x"], dtype=np.float32)
    common = {}
    common.update(consts)
    common.update(w)
    nc, p = _build(common)
    in_maps = []
    for cid in range(NCORES):
        m = dict(common)
        m["x0"] = np.ascontiguousarray(x[cid * BPC:(cid + 1) * BPC])
        in_maps.append(m)
    res = run_bass_kernel_spmd(nc, in_maps, core_ids=list(range(NCORES)))
    out = np.concatenate([np.asarray(r["out"], dtype=np.float32)
                          for r in res.results], axis=0)
    return out


# version 03
# speedup vs baseline: 1.1523x; 1.1523x over previous
import numpy as np
import ml_dtypes
import concourse.bass as bass
import concourse.mybir as mybir
from concourse.bass_utils import run_bass_kernel_spmd
from concourse.bass_core_rust import IndirectOffsetOnAxis
from contextlib import ExitStack, contextmanager

F32 = mybir.dt.float32
BF16 = mybir.dt.bfloat16
U32 = mybir.dt.uint32
AF = mybir.ActivationFunctionType
ALU = mybir.AluOpType
AX = mybir.AxisListType
BF = ml_dtypes.bfloat16

NCORES = 8
BPC = 2
S = 2048
D = 1024
NT = S // 128
NEG = -1.0e37
EPS = 1e-5
_STAGES = list(range(8))
_L1F_PARTS = ("shared", "experts")

ENGS = ("pe", "dve", "act", "pool", "sp")
_ENG_ATTR = {"pe": "tensor", "dve": "vector", "act": "scalar", "pool": "gpsimd", "sp": "sync"}
_DMA_NSEM = {"sp": 10, "act": 6, "pool": 10}


class _Buf:
    __slots__ = ("w", "readers")
    def __init__(self):
        self.w = None
        self.readers = []


class Prog:

    def __init__(self, nc, es):
        self.nc = nc
        self.semh = {}
        self.cnt = {}
        for ek in ENGS:
            self.semh[ek] = es.enter_context(nc.semaphore(f"S{ek}"))
            self.cnt[ek] = 0
        self.dmasems = {}
        self.dmarr = {}
        for qk, n in _DMA_NSEM.items():
            ring = []
            for i in range(n):
                sk = f"D{qk}{i}"
                self.semh[sk] = es.enter_context(nc.semaphore(sk))
                self.cnt[sk] = 0
                ring.append(sk)
            self.dmasems[qk] = ring
            self.dmarr[qk] = 0
        self.queue = {ek: [] for ek in ENGS}
        self.hw = {ek: {} for ek in ENGS}
        self.bufs = {}
        self.nisn = 0

    def _key(self, x):
        if isinstance(x, str):
            return x
        t = getattr(x, "tensor", None)
        return t.name if t is not None else x.name

    def _buf(self, x):
        k = self._key(x)
        b = self.bufs.get(k)
        if b is None:
            b = self.bufs[k] = _Buf()
        return b

    def _deps_for(self, reads, writes):
        deps = []
        for x in reads:
            b = self._buf(x)
            if b.w:
                deps.append(b.w)
        for x in writes:
            b = self._buf(x)
            if b.w:
                deps.append(b.w)
            deps.extend(b.readers)
        return deps

    def _dedup(self, ek, deps):
        hw = self.hw[ek]
        out = []
        for (sk, v) in deps:
            if hw.get(sk, 0) < v:
                hw[sk] = v
                out.append((sk, v))
        return out

    def _register(self, dep, reads, writes):
        for x in reads:
            self._buf(x).readers.append(dep)
        for x in writes:
            b = self._buf(x)
            b.w = dep
            b.readers = []

    def ex(self, ek, build, reads=(), writes=()):
        wl = self._dedup(ek, self._deps_for(reads, writes))
        self.cnt[ek] += 1
        dep = (ek, self.cnt[ek])
        semh, myh = self.semh, self.semh[ek]

        def item(eng, build=build, wl=wl):
            for sk, v in wl:
                eng.wait_ge(semh[sk], v)
            build(eng).then_inc(myh, 1)
        self.queue[ek].append(item)
        self.nisn += 1 + len(wl)
        self._register(dep, reads, writes)
        return dep

    def mmg(self, terms, reads=(), writes=(), start=True, stop=True):
        psum_ap = terms[0][0]
        opkeys = set()
        for (pap, lap, rap) in terms:
            opkeys.add(self._key(lap))
            opkeys.add(self._key(rap))
        opkeys.update(self._key(x) for x in reads)
        wkeys = {self._key(psum_ap)}
        wkeys.update(self._key(x) for x in writes)
        wl = self._dedup("pe", self._deps_for(opkeys, wkeys))
        self.cnt["pe"] += 1
        dep = ("pe", self.cnt["pe"])
        semh, myh = self.semh, self.semh["pe"]
        n = len(terms)

        def item(eng, terms=terms, wl=wl, n=n, start=start, stop=stop):
            for sk, v in wl:
                eng.wait_ge(semh[sk], v)
            for i, (pap, lap, rap) in enumerate(terms):
                inst = eng.matmul(pap, lap, rap, start=(start and i == 0),
                                  stop=(stop and i == n - 1))
            inst.then_inc(myh, 1)
        self.queue["pe"].append(item)
        self.nisn += n + len(wl)
        self._register(dep, opkeys, wkeys)
        return dep

    def _dma_dep(self, qk):
        ring = self.dmasems[qk]
        sk = ring[self.dmarr[qk] % len(ring)]
        self.dmarr[qk] += 1
        self.cnt[sk] += 16
        return sk, self.cnt[sk]

    def dma(self, qk, out_ap, in_ap, reads=(), writes=(), transpose=False):
        rk = [self._key(in_ap)] + [self._key(x) for x in reads]
        wk = [self._key(out_ap)] + [self._key(x) for x in writes]
        deps = self._deps_for(rk, wk)
        sk, v = self._dma_dep(qk)
        if v > 16:
            deps = deps + [(sk, v - 16)]
        wl = self._dedup(qk, deps)
        dep = (sk, v)
        semh = self.semh

        def item(eng, out_ap=out_ap, in_ap=in_ap, wl=wl, sk=sk, transpose=transpose):
            for k2, v2 in wl:
                eng.wait_ge(semh[k2], v2)
            if transpose:
                inst = eng.dma_start_transpose(out_ap, in_ap)
            else:
                inst = eng.dma_start(out_ap, in_ap)
            inst.then_inc(semh[sk], 16)
        self.queue[qk].append(item)
        self.nisn += 1 + len(wl)
        self._register(dep, rk, wk)
        return dep

    def dma_gather(self, out_ap, in_dram_ap, idx_ap, reads=(), writes=()):
        rk = [self._key(in_dram_ap), self._key(idx_ap)] + [self._key(x) for x in reads]
        wk = [self._key(out_ap)] + [self._key(x) for x in writes]
        deps = self._deps_for(rk, wk)
        sk, v = self._dma_dep("pool")
        if v > 16:
            deps = deps + [(sk, v - 16)]
        wl = self._dedup("pool", deps)
        dep = (sk, v)
        semh = self.semh

        def item(eng, out_ap=out_ap, in_dram_ap=in_dram_ap, idx_ap=idx_ap, wl=wl, sk=sk):
            for k2, v2 in wl:
                eng.wait_ge(semh[k2], v2)
            eng.indirect_dma_start(
                out_ap, None, in_dram_ap, IndirectOffsetOnAxis(idx_ap, 0)
            ).then_inc(semh[sk], 16)
        self.queue["pool"].append(item)
        self.nisn += 1 + len(wl)
        self._register(dep, rk, wk)
        return dep

    def barrier(self):
        allsems = list(self.cnt.keys())
        for ek in ENGS:
            wl = []
            hw = self.hw[ek]
            for sk in allsems:
                cur = self.cnt[sk]
                if cur > hw.get(sk, 0):
                    hw[sk] = cur
                    wl.append((sk, cur))
            if not wl:
                continue
            semh = self.semh

            def item(eng, wl=wl):
                for sk, v in wl:
                    eng.wait_ge(semh[sk], v)
            self.queue[ek].append(item)
            self.nisn += len(wl)
        self.bufs = {}

    @contextmanager
    def phase(self, name=""):
        es = ExitStack()
        try:
            yield es
        finally:
            self.barrier()
            es.close()

    def replay(self, es):
        blk = es.enter_context(self.nc.Block())
        for ek in ENGS:
            q = self.queue[ek]
            getattr(blk, _ENG_ATTR[ek])(lambda eng, q=q: [it(eng) for it in q])



def _hilo_rows(b):
    hi = b.astype(BF).astype(np.float32)
    lo = (b - hi).astype(BF).astype(np.float32)
    return np.ascontiguousarray(np.stack([hi, lo]))


def _col8(b):
    c = b.size // 128
    return np.ascontiguousarray(b.reshape(c, 128).T)


def _prep_consts():
    d = {}
    d["c_ident"] = np.eye(128, dtype=np.float32)
    cc = np.arange(2560)[None, :]
    ii = np.arange(128)[:, None]
    dd = ii + 2047 - cc
    alibi = np.where((dd < 0) | (dd >= 1024), NEG, -dd.astype(np.float64)).astype(np.float32)
    causal = np.where(dd < 0, NEG, 0.0).astype(np.float32)
    d["c_strip_alibi"] = np.ascontiguousarray(alibi)
    d["c_strip_causal"] = np.ascontiguousarray(causal)
    d["c_ones2"] = np.ones((2, 128), dtype=np.float32)
    e16 = np.zeros((16, 16 * 128), dtype=np.float32)
    for e in range(16):
        e16[e, e * 128:(e + 1) * 128] = 1.0
    d["c_e16"] = e16
    hd = 64
    theta = 1.0 / 10000.0 ** (np.arange(0, hd, 2, dtype=np.float32) / hd)
    t = np.arange(S, dtype=np.float32)
    fr = np.outer(theta, t)
    fr2 = np.concatenate([fr, fr], 0)
    cos = np.cos(fr2)
    sin = np.sin(fr2)
    sin_signed = sin.copy()
    sin_signed[:32] = -sin[:32]
    d["c_cos"] = np.ascontiguousarray(np.concatenate([cos, cos], 0).astype(np.float32))
    d["c_sin"] = np.ascontiguousarray(np.concatenate([sin_signed, sin_signed], 0).astype(np.float32))
    return d


def _rope_perm(nout):
    j = np.arange(nout)
    h = j // 64
    dd = j % 64
    return h * 64 + (dd + 32) % 64


def _prep_weights(inp):
    d = {}
    T = lambda a: np.ascontiguousarray(a.T)
    d["l0a_wqT"] = T(inp["L0_attn_a_wq"]); d["l0a_wkT"] = T(inp["L0_attn_a_wk"])
    d["l0a_wvT"] = T(inp["L0_attn_a_wv"]); d["l0a_woT"] = T(inp["L0_attn_a_wo"])
    d["l0a_bq"] = _col8(inp["L0_attn_a_bq"]); d["l0a_bk"] = _col8(inp["L0_attn_a_bk"])
    d["l0a_bv2"] = _hilo_rows(inp["L0_attn_a_bv"])
    d["l0a_bo2"] = _hilo_rows(inp["L0_attn_a_bo"])
    d["l0b_wqT"] = T(inp["L0_attn_b_Wq"]); d["l0b_wkT"] = T(inp["L0_attn_b_Wk"])
    d["l0b_wvT"] = T(inp["L0_attn_b_Wv"]); d["l0b_woT"] = T(inp["L0_attn_b_Wo"])
    d["l0b_bq"] = _col8(inp["L0_attn_b_bq"]); d["l0b_bk"] = _col8(inp["L0_attn_b_bk"])
    d["l0b_bv2"] = _hilo_rows(inp["L0_attn_b_bv"])
    d["l0b_bo2"] = _hilo_rows(inp["L0_attn_b_bo"])
    d["l0f_rw"] = np.ascontiguousarray(inp["L0_ffn_router_w"])
    d["l0f_rb2"] = _hilo_rows(inp["L0_ffn_router_b"])
    d["l0f_w1"] = np.ascontiguousarray(inp["L0_ffn_w1"])
    d["l0f_w2"] = np.ascontiguousarray(inp["L0_ffn_w2"])
    d["l0f_b1"] = np.ascontiguousarray(
        np.stack([_col8(inp["L0_ffn_b1"][e]) for e in range(16)])
        .transpose(1, 0, 2).reshape(128, 256))
    d["l0f_b2"] = np.ascontiguousarray(inp["L0_ffn_b2"])
    for k in ("W_dq", "W_uq", "W_dkv", "W_uk", "W_uv", "W_o"):
        d["l1a_" + k] = np.ascontiguousarray(inp["L1_attn_a_" + k])
    wqT = T(inp["L1_attn_b_Wq"])
    wkT = T(inp["L1_attn_b_Wk"])
    d["l1b_wqT"] = wqT
    d["l1b_wqTr"] = np.ascontiguousarray(wqT[:, _rope_perm(1024)])
    def _dup(w):
        r = w.reshape(1024, 4, 1, 64)
        return np.ascontiguousarray(
            np.concatenate([r, r], axis=2).reshape(1024, 512))
    d["l1b_wkT"] = _dup(wkT)
    d["l1b_wkTr"] = _dup(np.ascontiguousarray(wkT[:, _rope_perm(256)]))
    d["l1b_wvT"] = T(inp["L1_attn_b_Wv"])
    d["l1b_woT"] = T(inp["L1_attn_b_Wo"])
    d["l1b_bo2"] = _hilo_rows(inp["L1_attn_b_bo"])
    wsT = T(inp["L1_attn_b_Ws"])
    d["l1b_wsT"] = np.ascontiguousarray(np.concatenate([wsT, wsT], 0))
    d["l1b_bs"] = np.ascontiguousarray(inp["L1_attn_b_bs"].reshape(16, 1))
    d["l1f_gwT"] = T(inp["L1_ffn_gate_w"])
    d["l1f_wg"] = np.ascontiguousarray(inp["L1_ffn_Wg"].transpose(0, 2, 1))
    d["l1f_wu"] = np.ascontiguousarray(inp["L1_ffn_Wu"].transpose(0, 2, 1))
    d["l1f_wd"] = np.ascontiguousarray(inp["L1_ffn_Wd"].transpose(0, 2, 1))
    d["l1f_wsg"] = T(inp["L1_ffn_Ws_g"])
    d["l1f_wsu"] = T(inp["L1_ffn_Ws_u"])
    d["l1f_wsd"] = T(inp["L1_ffn_Ws_d"])
    return d



def _k(x):
    return x if isinstance(x, str) else x.tensor.name


def _apify(items):
    out = []
    for x in items:
        if isinstance(x, str) or hasattr(x, "tensor"):
            out.append(x)
    return out


class _B:
    pass


def _sb(c, es, name, rows, cols, dtype):
    return es.enter_context(c.nc.sbuf_tensor(name, [rows, cols], dtype))


def _ps(c, es, name, rows, cols):
    return es.enter_context(c.nc.psum_tensor(name, [rows, cols], F32))


_UNIQ = [0]
def _nm(base):
    _UNIQ[0] += 1
    return f"{base}_{_UNIQ[0]}"


def _vop(p, meth, out_ap, ins, reads=(), writes=(), eng="dve", **kw):
    def build(e, meth=meth, out_ap=out_ap, ins=ins, kw=kw):
        return getattr(e, meth)(out_ap, *ins, **kw)
    return p.ex(eng, build,
                reads=_apify(list(ins)) + _apify(list(reads)) + _apify(list(kw.values())),
                writes=[out_ap] + _apify(list(writes)))


def _act(p, out_ap, in_ap, func, bias=0.0, scale=1.0, accum=None):
    def build(e, out_ap=out_ap, in_ap=in_ap, func=func, bias=bias, scale=scale, accum=accum):
        if accum is not None:
            return e.activation(out_ap, in_ap, func, bias=bias, scale=scale, accum_out=accum)
        return e.activation(out_ap, in_ap, func, bias=bias, scale=scale)
    writes = [out_ap] + ([accum] if accum is not None else [])
    return p.ex("act", build, reads=_apify([in_ap, bias, scale]), writes=writes)


def _sadd(p, out_ap, in_ap, addv):
    def build(e, out_ap=out_ap, in_ap=in_ap, addv=addv):
        return e.add(out_ap, in_ap, addv)
    return p.ex("act", build, reads=_apify([in_ap, addv]), writes=[out_ap])


def _wslab_load(c, slab, wT_dram, din, dout):
    src = wT_dram.rearrange("(ic p) o -> p ic o", p=128)
    dst = slab[:].rearrange("p (ic o) -> p ic o", o=dout)
    return c.p.dma("pool", dst, src)


def _ln_tiles(c, es, b, xsrc_t, xn_fm, stats, rstd, tiles, pT, xw, xnf, xn_dram_t=None,
              st12=None):
    p, nc = c.p, c.nc
    for j, tt in enumerate(tiles):
        w = xw[j % 2]
        p.dma("sp", w[:], xsrc_t[b * NT + tt])
        _vop(p, "bn_stats", st12[:, 0:6], [w[:, 0:512]])
        _vop(p, "bn_stats", st12[:, 6:12], [w[:, 512:1024]])
        _vop(p, "bn_aggr", stats[:, 2 * tt:2 * tt + 2], [st12[:]])
        _act(p, rstd[:, tt:tt + 1], stats[:, 2 * tt + 1:2 * tt + 2], AF.Ln,
             bias=c.konst[:, 0:1])
        _act(p, rstd[:, tt:tt + 1], rstd[:, tt:tt + 1], AF.Exp, scale=-0.5)
        xf = xnf[j % 2]
        _vop(p, "tensor_scalar", xf[:], [w[:], stats[:, 2 * tt:2 * tt + 1],
                                         rstd[:, tt:tt + 1]],
             op0=ALU.subtract, op1=ALU.mult)
        if xn_dram_t is not None:
            p.dma("pool", xn_dram_t[b * NT + tt], xf[:])
        for half in range(2):
            pb = pT[half]
            for q in range(4):
                dc = half * 4 + q
                p.ex("pe", lambda e, o=pb[:, q * 128:(q + 1) * 128],
                     i=xf[:, dc * 128:(dc + 1) * 128], idn=c.ident[:]:
                     e.transpose(o, i, idn),
                     reads=[xf[:], c.ident[:]], writes=[pb[:]])
            dst = xn_fm[:].rearrange("p (dc t) -> p dc t", t=S)[
                :, half * 4:half * 4 + 4, tt * 128:(tt + 1) * 128]
            src = pb[:].rearrange("p (dc t) -> p dc t", t=128)
            _vop(p, "tensor_copy", dst, [src])


def _proj_fm(c, out_fm, slab, din, dout, src_fm, src_w, bias_col, pbanks, tlo, thi,
             out_w=S):
    p = c.p
    nic, noc = din // 128, dout // 128
    nb = len(pbanks)
    i = 0
    for oc in range(noc):
        for t0 in range(tlo, thi, 512):
            pb = pbanks[i % nb]; i += 1
            terms = [(pb[:, 0:512],
                      slab[:, ic * dout + oc * 128: ic * dout + oc * 128 + 128],
                      src_fm[:, ic * src_w + t0: ic * src_w + t0 + 512])
                     for ic in range(nic)]
            p.mmg(terms)
            dst = out_fm[:, oc * out_w + (t0 - tlo): oc * out_w + (t0 - tlo) + 512]
            bias = bias_col[:, oc:oc + 1] if bias_col is not None else 0.0
            _sadd(p, dst, pb[:, 0:512], bias)


def _proj_tm(c, out_tm, slab, din, dout, src_fm, src_w, bias2, pbanks, tiles):
    p = c.p
    nic = din // 128
    nb = len(pbanks)
    i = 0
    for tt in tiles:
        for o0 in range(0, dout, 512):
            pb = pbanks[i % nb]; i += 1
            terms = []
            if bias2 is not None:
                terms.append((pb[:, 0:512], c.ones2[:, 0:128], bias2[:, o0:o0 + 512]))
            terms += [(pb[:, 0:512],
                       src_fm[:, ic * src_w + tt * 128: ic * src_w + tt * 128 + 128],
                       slab[:, ic * dout + o0: ic * dout + o0 + 512])
                      for ic in range(nic)]
            p.mmg(terms)
            _vop(p, "tensor_copy", out_tm[:, tt * dout + o0: tt * dout + o0 + 512],
                 [pb[:, 0:512]])


def _attn_out_proj(c, es, b, qch, attn_fm, wo_slab, bo2_sb, xdst_t, pbanks, xr,
                   resid_src_t):
    p = c.p
    for j in range(4):
        tt = qch * 4 + j
        xt = xr[j % 2]
        p.dma("sp", xt[:], resid_src_t[b * NT + tt])
        for oh in range(2):
            pb = pbanks[(j * 2 + oh) % len(pbanks)]
            terms = [(pb[:, 0:512], c.ones2[:, 0:128], bo2_sb[:, oh * 512:oh * 512 + 512])]
            terms += [(pb[:, 0:512],
                       attn_fm[:, dc * 512 + j * 128: dc * 512 + j * 128 + 128],
                       wo_slab[:, dc * 1024 + oh * 512: dc * 1024 + oh * 512 + 512])
                      for dc in range(8)]
            p.mmg(terms)
            _vop(p, "tensor_add", xt[:, oh * 512:oh * 512 + 512],
                 [xt[:, oh * 512:oh * 512 + 512], pb[:, 0:512]])
        p.dma("act", xdst_t[b * NT + tt], xt[:])


def _load_bias_rows2(c, es, name, dram, n):
    t = _sb(c, es, _nm(name), 2, n, BF16)
    c.p.dma("pool", t[:], dram)
    return t


def _load_bias_col(c, es, name, dram, ncol):
    t = _sb(c, es, _nm(name), 128, ncol, F32)
    c.p.dma("sp", t[:], dram)
    return t


def _attn_core(c, es, b, kind, K_fm, V_tm, wq_slab, wo_slab, bq_col, bo2_sb,
               resid_src_t, xdst_t, xn_fm, cq_fm=None):
    p, nc = c.p, c.nc
    scale = 0.125 if kind in ("l0a", "l1a") else 1.0 / 32.0
    KW = 1536 if kind == "l0a" else 2048
    R = 3
    Qf = _sb(c, es, _nm("Qf"), 128, 8 * 512, BF16)
    probs = [_sb(c, es, _nm("probs"), 128, KW, BF16) for _ in range(R)]
    probsT = [_sb(c, es, _nm("probsT"), 128, KW, BF16) for _ in range(R)]
    if kind == "l0a":
        asbc = [_sb(c, es, _nm("asbc"), 128, 512, F32) for _ in range(2)]
    pS = [_ps(c, es, _nm("pS"), 128, 512) for _ in range(4)]
    pAV = [_ps(c, es, _nm("pAV"), 128, 512) for _ in range(2)]
    pb_t = [pAV[0][:, 0:512], pAV[1][:, 0:512]]
    afm = _sb(c, es, _nm("afm"), 128, 8 * 512, BF16)
    xr = [_sb(c, es, _nm("xr"), 128, 1024, F32) for _ in range(2)]
    nmc = _sb(c, es, _nm("nmc"), 128, R * 4, F32)
    nmsc = _sb(c, es, _nm("nmsc"), 128, R * 4, F32)
    fvec = _sb(c, es, _nm("fvec"), 128, R * 4, F32)
    zc = _sb(c, es, _nm("zc"), 128, R * 4, F32)
    nm1 = _sb(c, es, _nm("nm"), 128, R, F32)
    zz = _sb(c, es, _nm("zz"), 128, R, F32)
    scnt = [0]

    def slot_geom(i):
        h, j = divmod(i, 4)
        return h, j

    for qch in range(4):
        if kind == "l1a":
            nic, src, srcw = 2, cq_fm, S
        else:
            nic, src, srcw = 8, xn_fm, S
        off = qch * 512
        for oc in range(8):
            pb = pb_t[oc % 2]
            terms = [(pb, wq_slab[:, ic * 1024 + oc * 128: ic * 1024 + oc * 128 + 128],
                      src[:, ic * srcw + off: ic * srcw + off + 512])
                     for ic in range(nic)]
            p.mmg(terms)
            bias = bq_col[:, oc:oc + 1] if bq_col is not None else 0.0
            _sadd(p, Qf[:, oc * 512:(oc + 1) * 512], pb, bias)

        def geom(i):
            h, j = slot_geom(i)
            qt = qch * 4 + j
            qb = qt * 128
            if kind == "l0a":
                kc_lo = max(0, qb - 1023) // 512
                kc_hi = (qb + 127) // 512
            elif kind == "l1a":
                kc_lo, kc_hi = 0, (qb + 127) // 512
            else:
                kc_lo, kc_hi = 0, 3
            return h, j, qb, kc_lo, kc_hi

        def S1(i):
            h, j, qb, kc_lo, kc_hi = geom(i)
            r = i % R
            lq = Qf[(h % 2) * 64:(h % 2) * 64 + 64,
                    (h // 2) * 512 + j * 128: (h // 2) * 512 + j * 128 + 128]
            for i2, kc in enumerate(range(kc_lo, kc_hi + 1)):
                pb = pS[scnt[0] % 4][:]
                scnt[0] += 1
                rk = K_fm[(h % 2) * 64:(h % 2) * 64 + 64,
                          (h // 2) * 2048 + kc * 512: (h // 2) * 2048 + kc * 512 + 512]
                p.mmg([(pb, lq, rk)])
                esrc = pb
                if kind == "l0a":
                    co = 2047 - (qb - kc * 512)
                    slope = (2.0 ** (-2.0 ** -1.0)) ** (h + 1)
                    ac = asbc[scnt[0] % 2][:]
                    _vop(p, "scalar_tensor_tensor", ac,
                         [c.strip[:, co:co + 512], 8.0 * slope, pb],
                         op0=ALU.mult, op1=ALU.add)
                    esrc = ac
                elif kind == "l1a" and kc == kc_hi:
                    co = 2047 - (qb - kc * 512)
                    _vop(p, "tensor_add", pb, [pb, c.strip[:, co:co + 512]])
                col = nmc[:, r * 4 + i2: r * 4 + i2 + 1]
                cols = nmsc[:, r * 4 + i2: r * 4 + i2 + 1]
                _vop(p, "reduce_max", col, [esrc], axis=AX.X, negate=True)
                _vop(p, "tensor_scalar_mul", cols, [col, scale])
                _act(p, probs[r][:, i2 * 512:(i2 + 1) * 512], esrc, AF.Exp,
                     bias=cols, scale=scale,
                     accum=zc[:, r * 4 + i2: r * 4 + i2 + 1])

        def S2(i):
            h, j, qb, kc_lo, kc_hi = geom(i)
            r = i % R
            nkc = kc_hi - kc_lo + 1
            nmv = nm1[:, r:r + 1]
            zzv = zz[:, r:r + 1]
            fv = fvec[:, r * 4: r * 4 + nkc]
            _vop(p, "tensor_reduce", nmv, [nmc[:, r * 4: r * 4 + nkc]],
                 axis=AX.X, op=ALU.min)
            _vop(p, "tensor_scalar", fv, [nmc[:, r * 4: r * 4 + nkc], nmv, None],
                 op0=ALU.subtract)
            _act(p, fv, fv, AF.Exp, scale=-scale)
            _vop(p, "tensor_mul", zc[:, r * 4: r * 4 + nkc],
                 [zc[:, r * 4: r * 4 + nkc], fv])
            _vop(p, "reduce_sum", zzv, [zc[:, r * 4: r * 4 + nkc]], axis=AX.X)
            _vop(p, "reciprocal", zzv, [zzv])
            _vop(p, "tensor_scalar_mul", fv, [fv, zzv])
            for i2 in range(nkc):
                sl = probs[r][:, i2 * 512:(i2 + 1) * 512]
                _vop(p, "tensor_scalar_mul", sl, [sl, fv[:, i2:i2 + 1]])
                p.dma("sp", probsT[r][:, i2 * 512:(i2 + 1) * 512].rearrange(
                    "p (kt q) -> p kt q", q=128), sl, transpose=True)

        def AV(i):
            h, j, qb, kc_lo, kc_hi = geom(i)
            r = i % R
            nkc = kc_hi - kc_lo + 1
            pav = pAV[i % 2][0:64, 0:128]
            terms = []
            for kt in range(nkc * 4):
                ktg = kc_lo * 4 + kt
                terms.append((pav,
                              V_tm[:, ktg * 1024 + h * 64: ktg * 1024 + h * 64 + 64],
                              probsT[r][:, kt * 128: kt * 128 + 128]))
            p.mmg(terms)
            dst = afm[(h % 2) * 64:(h % 2) * 64 + 64,
                      (h // 2) * 512 + j * 128: (h // 2) * 512 + j * 128 + 128]
            _vop(p, "tensor_copy", dst, [pav])

        n = 64
        for i in range(n + 2):
            if i < n:
                S1(i)
            if 1 <= i <= n:
                S2(i - 1)
            if i >= 2:
                AV(i - 2)
        _attn_out_proj(c, es, b, qch, afm, wo_slab, bo2_sb, xdst_t,
                       pb_t, xr, resid_src_t)


def _stage_attn_std(c, b, kind, xsrc_t, xdst_t, wk, wv, wq, wo, bq, bk, bv2, bo2):
    p = c.p
    with p.phase() as eso:
        if kind == "l0a":
            c.strip = _sb(c, eso, _nm("strip"), 128, 2560, F32)
            p.dma("sp", c.strip[:], c.dram["c_strip_alibi"])
        slab_a = _sb(c, eso, _nm("slabA"), 128, 8 * 1024, BF16)
        slab_b = _sb(c, eso, _nm("slabB"), 128, 8 * 1024, BF16)
        K_fm = _sb(c, eso, _nm("Kfm"), 128, 8 * 2048, BF16)
        V_tm = _sb(c, eso, _nm("Vtm"), 128, 16 * 1024, BF16)
        stats = _sb(c, eso, _nm("stats"), 128, 2 * NT, F32)
        rstd = _sb(c, eso, _nm("rstd"), 128, NT, F32)
        bq_col = _load_bias_col(c, eso, "bq", c.dram[bq], 8)
        bk_col = _load_bias_col(c, eso, "bk", c.dram[bk], 8)
        bv2_sb = _load_bias_rows2(c, eso, "bv2", c.dram[bv2], 1024)
        bo2_sb = _load_bias_rows2(c, eso, "bo2", c.dram[bo2], 1024)
        xn_fm = _sb(c, eso, _nm("xnfm"), 128, 8 * 2048, BF16)
        _wslab_load(c, slab_a, c.dram[wk], 1024, 1024)
        _wslab_load(c, slab_b, c.dram[wv], 1024, 1024)
        with p.phase() as es1:
            st12 = _sb(c, es1, _nm("st12"), 128, 12, F32)
            xw = [_sb(c, es1, _nm("xw"), 128, 1024, F32) for _ in range(2)]
            xnf = [_sb(c, es1, _nm("xnf"), 128, 1024, F32) for _ in range(2)]
            pT = [_ps(c, es1, _nm("pT"), 128, 512) for _ in range(2)]
            pK = [_ps(c, es1, _nm("pK"), 128, 512) for _ in range(2)]
            pV = [_ps(c, es1, _nm("pV"), 128, 512) for _ in range(2)]
            _ln_tiles(c, es1, b, xsrc_t, xn_fm, stats, rstd, range(NT), pT,
                      xw, xnf, st12=st12)
            _proj_fm(c, K_fm, slab_a, 1024, 1024, xn_fm, S, bk_col, pK, 0, S)
            _proj_tm(c, V_tm, slab_b, 1024, 1024, xn_fm, S, bv2_sb, pV, range(NT))
        _wslab_load(c, slab_a, c.dram[wq], 1024, 1024)
        _wslab_load(c, slab_b, c.dram[wo], 1024, 1024)
        with p.phase() as es2:
            _attn_core(c, es2, b, kind, K_fm, V_tm, slab_a, slab_b, bq_col,
                       bo2_sb, xsrc_t, xdst_t, xn_fm)


def _stage_l1a(c, b, xsrc_t, xdst_t):
    p = c.p
    with p.phase() as eso:
        c.strip = _sb(c, eso, _nm("strip"), 128, 2560, F32)
        p.dma("sp", c.strip[:], c.dram["c_strip_causal"])
        s_dq = _sb(c, eso, _nm("sdq"), 128, 8 * 256, BF16)
        s_dkv = _sb(c, eso, _nm("sdkv"), 128, 8 * 256, BF16)
        s_uk = _sb(c, eso, _nm("suk"), 128, 2 * 1024, BF16)
        s_uv = _sb(c, eso, _nm("suv"), 128, 2 * 1024, BF16)
        K_fm = _sb(c, eso, _nm("Kfm"), 128, 8 * 2048, BF16)
        V_tm = _sb(c, eso, _nm("Vtm"), 128, 16 * 1024, BF16)
        cq_fm = _sb(c, eso, _nm("cqfm"), 128, 2 * 2048, BF16)
        ckv_fm = _sb(c, eso, _nm("ckvfm"), 128, 2 * 2048, BF16)
        stats = _sb(c, eso, _nm("stats"), 128, 2 * NT, F32)
        rstd = _sb(c, eso, _nm("rstd"), 128, NT, F32)
        st12 = _sb(c, eso, _nm("st12"), 128, 12, F32)
        xw = [_sb(c, eso, _nm("xw"), 128, 1024, F32) for _ in range(2)]
        xnf = [_sb(c, eso, _nm("xnf"), 128, 1024, F32) for _ in range(2)]
        zero2 = _sb(c, eso, _nm("zero2"), 2, 1024, BF16)
        _vop(p, "memset", zero2[:], [0.0], eng="dve")
        _wslab_load(c, s_dq, c.dram["l1a_W_dq"], 1024, 256)
        _wslab_load(c, s_dkv, c.dram["l1a_W_dkv"], 1024, 256)
        _wslab_load(c, s_uk, c.dram["l1a_W_uk"], 256, 1024)
        _wslab_load(c, s_uv, c.dram["l1a_W_uv"], 256, 1024)
        with p.phase() as es1:
            xn_fm = _sb(c, es1, _nm("xnfm"), 128, 8 * 2048, BF16)
            pT = [_ps(c, es1, _nm("pT"), 128, 512) for _ in range(2)]
            pK = [_ps(c, es1, _nm("pK"), 128, 512) for _ in range(2)]
            pV = [_ps(c, es1, _nm("pV"), 128, 512) for _ in range(2)]
            _ln_tiles(c, es1, b, xsrc_t, xn_fm, stats, rstd, range(NT), pT,
                      xw, xnf, st12=st12)
            _proj_fm(c, cq_fm, s_dq, 1024, 256, xn_fm, S, None, pK, 0, S)
            _proj_fm(c, ckv_fm, s_dkv, 1024, 256, xn_fm, S, None, pV, 0, S)
            _proj_fm(c, K_fm, s_uk, 256, 1024, ckv_fm, S, None, pK, 0, S)
            _proj_tm(c, V_tm, s_uv, 256, 1024, ckv_fm, S, None, pV, range(NT))
        s_uq = _sb(c, eso, _nm("suq"), 128, 2 * 1024, BF16)
        s_o = _sb(c, eso, _nm("so"), 128, 8 * 1024, BF16)
        _wslab_load(c, s_uq, c.dram["l1a_W_uq"], 256, 1024)
        _wslab_load(c, s_o, c.dram["l1a_W_o"], 1024, 1024)
        with p.phase() as es2:
            _attn_core(c, es2, b, "l1a", K_fm, V_tm, s_uq, s_o, None,
                       zero2, xsrc_t, xdst_t, None, cq_fm=cq_fm)


def _stage_l0f(c, b, xsrc_t, xdst_t):
    p = c.p
    with p.phase() as eso:
        xn_fm = _sb(c, eso, _nm("xnfm"), 128, 8 * 2048, BF16)
        acc = _sb(c, eso, _nm("acc"), 128, 16 * 1024, F32)
        C_sb = _sb(c, eso, _nm("Csb"), 16, 2048, BF16)
        stats = _sb(c, eso, _nm("stats"), 128, 2 * NT, F32)
        rstd = _sb(c, eso, _nm("rstd"), 128, NT, F32)
        with p.phase() as es1:
            st12 = _sb(c, es1, _nm("st12"), 128, 12, F32)
            xw = [_sb(c, es1, _nm("xw"), 128, 1024, F32) for _ in range(2)]
            xnf = [_sb(c, es1, _nm("xnf"), 128, 1024, F32) for _ in range(2)]
            rw_slab = _sb(c, es1, _nm("rws"), 128, 8 * 16, BF16)
            rb2_sb = _load_bias_rows2(c, es1, "rb2", c.dram["l0f_rb2"], 16)
            _wslab_load(c, rw_slab, c.dram["l0f_rw"], 1024, 16)
            pT = [_ps(c, es1, _nm("pT"), 128, 512) for _ in range(2)]
            pR = [_ps(c, es1, _nm("pR"), 128, 512) for _ in range(2)]
            _ln_tiles(c, es1, b, xsrc_t, xn_fm, stats, rstd, range(NT), pT,
                      xw, xnf, st12=st12)
            prb = _sb(c, es1, _nm("prb"), 128, 16, F32)
            m8 = _sb(c, es1, _nm("m8"), 128, 8, F32)
            msk = _sb(c, es1, _nm("msk"), 128, 16, F32)
            ctm = _sb(c, es1, _nm("ctm"), 128, 16, F32)
            nmz = _sb(c, es1, _nm("nmz"), 128, 2, F32)
            for tt in range(NT):
                pb = pR[tt % 2]
                terms = [(pb[:, 0:16], c.ones2[:, 0:128], rb2_sb[:, 0:16])]
                terms += [(pb[:, 0:16],
                           xn_fm[:, ic * 2048 + tt * 128: ic * 2048 + tt * 128 + 128],
                           rw_slab[:, ic * 16: ic * 16 + 16]) for ic in range(8)]
                p.mmg(terms)
                _vop(p, "reduce_max", nmz[:, 0:1], [pb[:, 0:16]], axis=AX.X, negate=True)
                _act(p, prb[:], pb[:, 0:16], AF.Exp, bias=nmz[:, 0:1])
                _vop(p, "max", m8[:], [prb[:]])
                _vop(p, "tensor_scalar", msk[:], [prb[:], m8[:, 3:4], None],
                     op0=ALU.is_ge)
                _vop(p, "tensor_mul", msk[:], [msk[:], prb[:]])
                _vop(p, "reduce_sum", nmz[:, 1:2], [msk[:]], axis=AX.X)
                _vop(p, "reciprocal", nmz[:, 1:2], [nmz[:, 1:2]])
                _vop(p, "tensor_scalar_mul", ctm[:], [msk[:], nmz[:, 1:2]])
                p.ex("pe", lambda e, o=pb[0:16, 16:144], i=ctm[:], idn=c.ident[:]:
                     e.transpose(o, i, idn), reads=[ctm[:], c.ident[:]],
                     writes=[pb])
                _vop(p, "tensor_copy", C_sb[0:16, tt * 128:(tt + 1) * 128],
                     [pb[0:16, 16:144]])
        with p.phase() as es2:
            w1s = [_sb(c, es2, _nm("w1s"), 128, 8 * 2048, BF16)]
            w2s = [_sb(c, es2, _nm("w2s"), 128, 16 * 1024, BF16)]
            hbuf = _sb(c, es2, _nm("hbuf"), 128, 16 * 512, BF16)
            b1_all = _sb(c, es2, _nm("b1a"), 128, 16 * 16, F32)
            p.dma("sp", b1_all[:], c.dram["l0f_b1"])
            b2_sb = _sb(c, es2, _nm("b2s"), 16, 1024, BF16)
            p.dma("pool", b2_sb[:], c.dram["l0f_b2"])
            _wslab_load(c, w1s[0], c.dram["l0f_w1"][0], 1024, 2048)
            _wslab_load(c, w2s[0], c.dram["l0f_w2"][0], 2048, 1024)
            pUP = [_ps(c, es2, _nm("pUP"), 128, 512) for _ in range(2)]
            pDN = [_ps(c, es2, _nm("pDN"), 128, 512) for _ in range(2)]
            pCE = _ps(c, es2, _nm("pCE"), 128, 512)
            ceb = _sb(c, es2, _nm("ceb"), 128, 512, BF16)
            for tt in range(NT):
                for dh in range(2):
                    pb = pDN[(tt * 2 + dh) % 2]
                    p.mmg([(pb[:, 0:512], C_sb[0:16, tt * 128:(tt + 1) * 128],
                            b2_sb[0:16, dh * 512:(dh + 1) * 512])])
                    _vop(p, "tensor_copy",
                         acc[:, tt * 1024 + dh * 512: tt * 1024 + dh * 512 + 512],
                         [pb[:, 0:512]])
            for e in range(16):
                for tch in range(4):
                    p.mmg([(pCE[:, 0:512], c.e16[:, e * 128:(e + 1) * 128],
                            C_sb[0:16, tch * 512:(tch + 1) * 512])])
                    _vop(p, "tensor_copy", ceb[:], [pCE[:, 0:512]])
                    for fc in range(16):
                        pb = pUP[fc % 2]
                        terms = [(pb[:, 0:512],
                                  w1s[0][:, ic * 2048 + fc * 128: ic * 2048 + fc * 128 + 128],
                                  xn_fm[:, ic * 2048 + tch * 512: ic * 2048 + tch * 512 + 512])
                                 for ic in range(8)]
                        p.mmg(terms)
                        _act(p, hbuf[:, fc * 512:(fc + 1) * 512], pb[:, 0:512],
                             AF.Gelu, bias=b1_all[:, e * 16 + fc: e * 16 + fc + 1])
                    _vop(p, "tensor_tensor", hbuf[:].rearrange("p (fc t) -> p fc t", t=512),
                         [hbuf[:].rearrange("p (fc t) -> p fc t", t=512),
                          ceb[:].rearrange("p (o t) -> p o t", o=1).to_broadcast([128, 16, 512])],
                         op=ALU.mult)
                    for j in range(4):
                        tt = tch * 4 + j
                        for dh in range(2):
                            pb = pDN[(j * 2 + dh) % 2]
                            terms = [(pb[:, 0:512],
                                      hbuf[:, fc * 512 + j * 128: fc * 512 + j * 128 + 128],
                                      w2s[0][:, fc * 1024 + dh * 512: fc * 1024 + dh * 512 + 512])
                                     for fc in range(16)]
                            p.mmg(terms)
                            dst = acc[:, tt * 1024 + dh * 512: tt * 1024 + dh * 512 + 512]
                            _vop(p, "tensor_add", dst, [dst, pb[:, 0:512]])
                if e + 1 < 16:
                    _wslab_load(c, w1s[0], c.dram["l0f_w1"][e + 1], 1024, 2048)
                    _wslab_load(c, w2s[0], c.dram["l0f_w2"][e + 1], 2048, 1024)
        with p.phase() as es3:
            xr = [_sb(c, es3, _nm("xr"), 128, 1024, F32) for _ in range(2)]
            for tt in range(NT):
                xt = xr[tt % 2]
                p.dma("sp", xt[:], xsrc_t[b * NT + tt])
                _vop(p, "tensor_add", xt[:], [xt[:], acc[:, tt * 1024:(tt + 1) * 1024]])
                p.dma("act", xdst_t[b * NT + tt], xt[:])


def _rope_combine(c, pA, pB, cosv, sinv, dst):
    p = c.p
    t1 = c.rs1
    t2 = c.rs2
    _vop(p, "tensor_mul", t1[:], [pB, sinv])
    _vop(p, "tensor_mul", t2[:], [pA, cosv])
    _vop(p, "tensor_add", dst, [t1[:], t2[:]])


def _stage_l1b(c, b, xsrc_t, xdst_t):
    p = c.p
    with p.phase() as eso:
        c.strip = _sb(c, eso, _nm("strip"), 128, 2560, F32)
        p.dma("sp", c.strip[:], c.dram["c_strip_causal"])
        K_ro = _sb(c, eso, _nm("Kro"), 128, 4 * 2048, BF16)
        V_tm = _sb(c, eso, _nm("Vtm"), 128, 16 * 256, BF16)
        k_sp = _sb(c, eso, _nm("ksp"), 16, 4 * 2048, BF16)
        stats = _sb(c, eso, _nm("stats"), 128, 2 * NT, F32)
        rstd = _sb(c, eso, _nm("rstd"), 128, NT, F32)
        cosb = _sb(c, eso, _nm("cosb"), 128, 2048, F32)
        sinb = _sb(c, eso, _nm("sinb"), 128, 2048, F32)
        wsT = _sb(c, eso, _nm("wsT"), 128, 16, BF16)
        bs_col = _sb(c, eso, _nm("bs"), 16, 1, F32)
        bo2_sb = _load_bias_rows2(c, eso, "bo2", c.dram["l1b_bo2"], 1024)
        c.rs1 = _sb(c, eso, _nm("rs1"), 128, 512, F32)
        c.rs2 = _sb(c, eso, _nm("rs2"), 128, 512, F32)
        p.dma("sp", cosb[:], c.dram["c_cos"])
        p.dma("sp", sinb[:], c.dram["c_sin"])
        p.dma("pool", wsT[:], c.dram["l1b_wsT"])
        p.dma("sp", bs_col[:], c.dram["l1b_bs"])
        with p.phase() as es1:
            xn_fm = _sb(c, es1, _nm("xnfm"), 128, 8 * 2048, BF16)
            st12 = _sb(c, es1, _nm("st12"), 128, 12, F32)
            xw = [_sb(c, es1, _nm("xw"), 128, 1024, F32) for _ in range(2)]
            xnf = [_sb(c, es1, _nm("xnf"), 128, 1024, F32) for _ in range(2)]
            s_k = _sb(c, es1, _nm("sk"), 128, 8 * 512, BF16)
            s_kr = _sb(c, es1, _nm("skr"), 128, 8 * 512, BF16)
            s_v = _sb(c, es1, _nm("sv"), 128, 8 * 256, BF16)
            _wslab_load(c, s_k, c.dram["l1b_wkT"], 1024, 512)
            _wslab_load(c, s_kr, c.dram["l1b_wkTr"], 1024, 512)
            _wslab_load(c, s_v, c.dram["l1b_wvT"], 1024, 256)
            pT = [_ps(c, es1, _nm("pT"), 128, 512) for _ in range(2)]
            pK = [_ps(c, es1, _nm("pK"), 128, 512) for _ in range(2)]
            pV = [_ps(c, es1, _nm("pV"), 128, 512) for _ in range(2)]
            _ln_tiles(c, es1, b, xsrc_t, xn_fm, stats, rstd, range(NT), pT,
                      xw, xnf, st12=st12)
            for oc in range(4):
                for tch in range(4):
                    pa, pb = pK[0][:, 0:512], pK[1][:, 0:512]
                    t0 = tch * 512
                    terms = [(pa, s_k[:, ic * 512 + oc * 128: ic * 512 + oc * 128 + 128],
                              xn_fm[:, ic * 2048 + t0: ic * 2048 + t0 + 512])
                             for ic in range(8)]
                    p.mmg(terms)
                    terms = [(pb, s_kr[:, ic * 512 + oc * 128: ic * 512 + oc * 128 + 128],
                              xn_fm[:, ic * 2048 + t0: ic * 2048 + t0 + 512])
                             for ic in range(8)]
                    p.mmg(terms)
                    _rope_combine(c, pa, pb, cosb[:, t0:t0 + 512], sinb[:, t0:t0 + 512],
                                  K_ro[:, oc * 2048 + t0: oc * 2048 + t0 + 512])
            for tt in range(NT):
                pbv = pV[tt % 2][:, 0:256]
                terms = [(pbv,
                          xn_fm[:, ic * 2048 + tt * 128: ic * 2048 + tt * 128 + 128],
                          s_v[:, ic * 256: ic * 256 + 256]) for ic in range(8)]
                p.mmg(terms)
                _vop(p, "tensor_copy", V_tm[:, tt * 256:(tt + 1) * 256], [pbv])
            for kvh in range(4):
                for tch in range(4):
                    pbs = pT[tch % 2][0:16, 0:512]
                    rk = K_ro[0:64, kvh * 2048 + tch * 512: kvh * 2048 + tch * 512 + 512]
                    p.mmg([(pbs, wsT[0:64, 0:16], rk)])
                    _sadd(p, k_sp[0:16, kvh * 2048 + tch * 512: kvh * 2048 + tch * 512 + 512],
                          pbs, bs_col[0:16, 0:1])
        s_q = _sb(c, eso, _nm("sq"), 128, 8 * 1024, BF16)
        s_qr = _sb(c, eso, _nm("sqr"), 128, 8 * 1024, BF16)
        s_o = _sb(c, eso, _nm("so"), 128, 8 * 1024, BF16)
        _wslab_load(c, s_q, c.dram["l1b_wqT"], 1024, 1024)
        _wslab_load(c, s_qr, c.dram["l1b_wqTr"], 1024, 1024)
        _wslab_load(c, s_o, c.dram["l1b_woT"], 1024, 1024)
        with p.phase() as es2:
            xnf = [_sb(c, es2, _nm("xnf"), 128, 1024, F32) for _ in range(2)]
            xnq = _sb(c, es2, _nm("xnq"), 128, 8 * 512, BF16)
            Qro = _sb(c, es2, _nm("Qro"), 128, 8 * 512, BF16)
            q_sp = _sb(c, es2, _nm("qsp"), 16, 512, BF16)
            tnh = [c.rs1, c.rs2]
            probs = [_sb(c, es2, _nm("probs"), 128, 2048, BF16) for _ in range(3)]
            probsT = [_sb(c, es2, _nm("probsT"), 128, 2048, BF16) for _ in range(3)]
            asbc = [_sb(c, es2, _nm("asbc"), 128, 512, F32) for _ in range(2)]
            afm = _sb(c, es2, _nm("afm"), 128, 8 * 512, BF16)
            xr = [_sb(c, es2, _nm("xr"), 128, 1024, F32) for _ in range(2)]
            nmc = _sb(c, es2, _nm("nmc"), 128, 12, F32)
            nmsc = _sb(c, es2, _nm("nmsc"), 128, 12, F32)
            fvec = _sb(c, es2, _nm("fvec"), 128, 12, F32)
            zc = _sb(c, es2, _nm("zc"), 128, 12, F32)
            nm1 = _sb(c, es2, _nm("nm"), 128, 3, F32)
            zz = _sb(c, es2, _nm("zz"), 128, 3, F32)
            pQK = [_ps(c, es2, _nm("pQK"), 128, 512) for _ in range(2)]
            pSP = [_ps(c, es2, _nm("pSP"), 128, 512) for _ in range(2)]
            pAV = [_ps(c, es2, _nm("pAV"), 128, 512) for _ in range(2)]
            pb_t = [pAV[0][:, 0:512], pAV[1][:, 0:512]]
            for qch in range(4):
                for j in range(4):
                    tt = qch * 4 + j
                    w = xr[j % 2]
                    p.dma("sp", w[:], xsrc_t[b * NT + tt])
                    xf = xnf[j % 2]
                    _vop(p, "tensor_scalar", xf[:], [w[:], stats[:, 2 * tt:2 * tt + 1],
                                                     rstd[:, tt:tt + 1]],
                         op0=ALU.subtract, op1=ALU.mult)
                    for half in range(2):
                        pbh = pb_t[half]
                        for q in range(4):
                            dc = half * 4 + q
                            p.ex("pe", lambda e, o=pbh[:, q * 128:(q + 1) * 128],
                                 i=xf[:, dc * 128:(dc + 1) * 128], idn=c.ident[:]:
                                 e.transpose(o, i, idn),
                                 reads=[xf[:], c.ident[:]], writes=[pbh])
                        dst = xnq[:].rearrange("p (dc t) -> p dc t", t=512)[
                            :, half * 4:half * 4 + 4, j * 128:(j + 1) * 128]
                        src = pbh.rearrange("p (dc t) -> p dc t", t=128)
                        _vop(p, "tensor_copy", dst, [src])
                t0g = qch * 512
                for oc in range(8):
                    pa, pb2 = pQK[0][:, 0:512], pQK[1][:, 0:512]
                    terms = [(pa, s_q[:, ic * 1024 + oc * 128: ic * 1024 + oc * 128 + 128],
                              xnq[:, ic * 512: ic * 512 + 512]) for ic in range(8)]
                    p.mmg(terms)
                    terms = [(pb2, s_qr[:, ic * 1024 + oc * 128: ic * 1024 + oc * 128 + 128],
                              xnq[:, ic * 512: ic * 512 + 512]) for ic in range(8)]
                    p.mmg(terms)
                    _rope_combine(c, pa, pb2, cosb[:, t0g:t0g + 512],
                                  sinb[:, t0g:t0g + 512], Qro[:, oc * 512:(oc + 1) * 512])
                def geom(i):
                    h, j = divmod(i, 4)
                    qt = qch * 4 + j
                    qb = qt * 128
                    return h, j, qb, (qb + 127) // 512

                def S1(i):
                    h, j, qb, kc_hi = geom(i)
                    r = i % 3
                    kvh = h // 4
                    base = (h % 2) * 64
                    if j == 0:
                        pbs = pSP[0][0:16, 0:512]
                        p.mmg([(pbs, wsT[base:base + 64, 0:16],
                                Qro[base:base + 64, (h // 2) * 512:(h // 2) * 512 + 512])])
                        _sadd(p, q_sp[0:16, 0:512], pbs, bs_col[0:16, 0:1])
                    lq = Qro[base:base + 64, (h // 2) * 512 + j * 128:
                             (h // 2) * 512 + j * 128 + 128]
                    for kc in range(kc_hi + 1):
                        pqk = pQK[kc % 2][:, 0:512]
                        psp = pSP[1][:, 0:512]
                        rk = K_ro[base:base + 64,
                                  kvh * 2048 + kc * 512: kvh * 2048 + kc * 512 + 512]
                        p.mmg([(pqk, lq, rk)])
                        p.mmg([(psp, q_sp[0:16, j * 128:j * 128 + 128],
                                k_sp[0:16, kvh * 2048 + kc * 512: kvh * 2048 + kc * 512 + 512])])
                        tb = tnh[kc % 2]
                        _act(p, tb[:], psp, AF.Tanh, scale=0.125)
                        ac = asbc[kc % 2][:]
                        _vop(p, "scalar_tensor_tensor", ac, [tb[:], 1.0, pqk],
                             op0=ALU.add, op1=ALU.mult)
                        if kc == kc_hi:
                            co = 2047 - (qb - kc * 512)
                            _vop(p, "tensor_add", ac, [ac, c.strip[:, co:co + 512]])
                        col = nmc[:, r * 4 + kc: r * 4 + kc + 1]
                        cols = nmsc[:, r * 4 + kc: r * 4 + kc + 1]
                        _vop(p, "reduce_max", col, [ac], axis=AX.X, negate=True)
                        _vop(p, "tensor_scalar_mul", cols, [col, 1.0 / 16.0])
                        _act(p, probs[r][:, kc * 512:(kc + 1) * 512], ac, AF.Exp,
                             bias=cols, scale=1.0 / 16.0,
                             accum=zc[:, r * 4 + kc: r * 4 + kc + 1])

                def S2(i):
                    h, j, qb, kc_hi = geom(i)
                    r = i % 3
                    nkc = kc_hi + 1
                    nmv = nm1[:, r:r + 1]
                    zzv = zz[:, r:r + 1]
                    fv = fvec[:, r * 4: r * 4 + nkc]
                    _vop(p, "tensor_reduce", nmv, [nmc[:, r * 4: r * 4 + nkc]],
                         axis=AX.X, op=ALU.min)
                    _vop(p, "tensor_scalar", fv, [nmc[:, r * 4: r * 4 + nkc], nmv, None],
                         op0=ALU.subtract)
                    _act(p, fv, fv, AF.Exp, scale=-1.0 / 16.0)
                    _vop(p, "tensor_mul", zc[:, r * 4: r * 4 + nkc],
                         [zc[:, r * 4: r * 4 + nkc], fv])
                    _vop(p, "reduce_sum", zzv, [zc[:, r * 4: r * 4 + nkc]], axis=AX.X)
                    _vop(p, "reciprocal", zzv, [zzv])
                    _vop(p, "tensor_scalar_mul", fv, [fv, zzv])
                    for kc in range(nkc):
                        sl = probs[r][:, kc * 512:(kc + 1) * 512]
                        _vop(p, "tensor_scalar_mul", sl, [sl, fv[:, kc:kc + 1]])
                        p.dma("sp", probsT[r][:, kc * 512:(kc + 1) * 512].rearrange(
                            "p (kt q) -> p kt q", q=128), sl, transpose=True)

                def AV(i):
                    h, j, qb, kc_hi = geom(i)
                    r = i % 3
                    kvh = h // 4
                    base = (h % 2) * 64
                    pav = pAV[i % 2][0:64, 0:128]
                    terms = []
                    for kt in range((kc_hi + 1) * 4):
                        terms.append((pav,
                                      V_tm[:, kt * 256 + kvh * 64: kt * 256 + kvh * 64 + 64],
                                      probsT[r][:, kt * 128: kt * 128 + 128]))
                    p.mmg(terms)
                    dst = afm[base:base + 64,
                              (h // 2) * 512 + j * 128:(h // 2) * 512 + j * 128 + 128]
                    _vop(p, "tensor_copy", dst, [pav])

                for i in range(66):
                    if i < 64:
                        S1(i)
                    if 1 <= i <= 64:
                        S2(i - 1)
                    if i >= 2:
                        AV(i - 2)
                _attn_out_proj(c, es2, b, qch, afm, s_o, bo2_sb, xdst_t,
                               pb_t, xr, xsrc_t)


def _stage_l1f(c, b, xsrc_t, xdst_t):
    p = c.p
    E, CAP = 16, 256
    with p.phase() as eso:
        acc = _sb(c, eso, _nm("acc"), 128, 16 * 1024, F32)
        stats = _sb(c, eso, _nm("stats"), 128, 2 * NT, F32)
        rstd = _sb(c, eso, _nm("rstd"), 128, NT, F32)
        valsT = _sb(c, eso, _nm("valsT"), 128, 2 * 16, F32)
        idxT = _sb(c, eso, _nm("idxT"), 128, 2 * 16, F32)
        with p.phase() as es1:
            xn_fm = _sb(c, es1, _nm("xnfm"), 128, 8 * 2048, BF16)
            st12 = _sb(c, es1, _nm("st12"), 128, 12, F32)
            xw = [_sb(c, es1, _nm("xw"), 128, 1024, F32) for _ in range(2)]
            xnf = [_sb(c, es1, _nm("xnf"), 128, 1024, F32) for _ in range(2)]
            gw_slab = _sb(c, es1, _nm("gws"), 128, 8 * 16, BF16)
            _wslab_load(c, gw_slab, c.dram["l1f_gwT"], 1024, 16)
            S_eb = _sb(c, es1, _nm("Seb"), 16, 2048, F32)
            prb = _sb(c, es1, _nm("prb"), 128, 16, F32)
            sctm = _sb(c, es1, _nm("sctm"), 128, 16, F32)
            nmz = _sb(c, es1, _nm("nmz"), 128, 2, F32)
            w0 = _sb(c, es1, _nm("w0"), 16, 2048, F32)
            w1 = _sb(c, es1, _nm("w1"), 16, 2048, F32)
            vals = _sb(c, es1, _nm("vals"), 16, 256, F32)
            idxu = _sb(c, es1, _nm("idxu"), 16, 256, U32)
            idxf = _sb(c, es1, _nm("idxf"), 16, 256, F32)
            pT = [_ps(c, es1, _nm("pT"), 128, 512) for _ in range(2)]
            pR = [_ps(c, es1, _nm("pR"), 128, 512) for _ in range(2)]
            _ln_tiles(c, es1, b, xsrc_t, xn_fm, stats, rstd, range(NT), pT,
                      xw, xnf, xn_dram_t=c.xnd_t, st12=st12)
            for tt in range(NT):
                pb = pR[tt % 2]
                terms = [(pb[:, 0:16],
                          xn_fm[:, ic * 2048 + tt * 128: ic * 2048 + tt * 128 + 128],
                          gw_slab[:, ic * 16: ic * 16 + 16]) for ic in range(8)]
                p.mmg(terms)
                _vop(p, "reduce_max", nmz[:, 0:1], [pb[:, 0:16]], axis=AX.X, negate=True)
                _act(p, prb[:], pb[:, 0:16], AF.Exp, bias=nmz[:, 0:1],
                     accum=nmz[:, 1:2])
                _vop(p, "reciprocal", nmz[:, 1:2], [nmz[:, 1:2]])
                _vop(p, "tensor_scalar_mul", sctm[:], [prb[:], nmz[:, 1:2]])
                p.ex("pe", lambda e, o=pb[0:16, 16:144], i=sctm[:], idn=c.ident[:]:
                     e.transpose(o, i, idn), reads=[sctm[:], c.ident[:]],
                     writes=[pb])
                _vop(p, "tensor_copy", S_eb[0:16, tt * 128:(tt + 1) * 128],
                     [pb[0:16, 16:144]])
            _vop(p, "tensor_copy", w0[:], [S_eb[:]])
            for r in range(32):
                src = w0 if r % 2 == 0 else w1
                dst = w1 if r % 2 == 0 else w0
                vs = vals[0:16, r * 8:(r + 1) * 8]
                _vop(p, "max", vs, [src[:]])
                _vop(p, "max_index", idxu[0:16, r * 8:(r + 1) * 8], [vs, src[:]])
                _vop(p, "match_replace", dst[:], [vs, src[:]], imm_value=-1.0)
            _vop(p, "tensor_copy", idxf[:], [idxu[:]])
            for ct in range(2):
                p.ex("pe", lambda e, o=pR[0][:, 0:16],
                     i=vals[0:16, ct * 128:(ct + 1) * 128], idn=c.ident[0:16, 0:16]:
                     e.transpose(o, i, idn), reads=[vals[:], c.ident[:]],
                     writes=[pR[0]])
                _vop(p, "tensor_copy", valsT[:, ct * 16:(ct + 1) * 16],
                     [pR[0][:, 0:16]])
                p.ex("pe", lambda e, o=pR[1][:, 0:16],
                     i=idxf[0:16, ct * 128:(ct + 1) * 128], idn=c.ident[0:16, 0:16]:
                     e.transpose(o, i, idn), reads=[idxf[:], c.ident[:]],
                     writes=[pR[1]])
                _vop(p, "tensor_copy", idxT[:, ct * 16:(ct + 1) * 16],
                     [pR[1][:, 0:16]])
        if "shared" not in _L1F_PARTS:
            with p.phase() as esz:
                zcl = _sb(c, esz, _nm("zcl"), 128, 1024, F32)
                _vop(p, "memset", zcl[:], [0.0])
                for tt in range(NT):
                    _vop(p, "tensor_copy", acc[:, tt * 1024:(tt + 1) * 1024], [zcl[:]])
        if "shared" in _L1F_PARTS:
         with p.phase() as es2:
            xnq2 = _sb(c, es2, _nm("xnq2"), 128, 8 * 512, BF16)
            wsd = _sb(c, es2, _nm("wsd"), 128, 16 * 1024, BF16)
            wsg = _sb(c, es2, _nm("wsg"), 128, 8 * 2048, BF16)
            wsu = _sb(c, es2, _nm("wsu"), 128, 8 * 2048, BF16)
            h_sh = _sb(c, es2, _nm("hsh"), 128, 16 * 512, BF16)
            gtmp = _sb(c, es2, _nm("gtmp"), 128, 512, BF16)
            xw2 = [_sb(c, es2, _nm("xw2"), 128, 1024, F32)]
            _wslab_load(c, wsd, c.dram["l1f_wsd"], 2048, 1024)
            _wslab_load(c, wsg, c.dram["l1f_wsg"], 1024, 2048)
            _wslab_load(c, wsu, c.dram["l1f_wsu"], 1024, 2048)
            pG = [_ps(c, es2, _nm("pG"), 128, 512) for _ in range(2)]
            pU = [_ps(c, es2, _nm("pU"), 128, 512) for _ in range(2)]
            pDN = [_ps(c, es2, _nm("pDN"), 128, 512) for _ in range(2)]
            for tch in range(4):
                for j in range(4):
                    tt = tch * 4 + j
                    w = xw2[0]
                    p.dma("sp", w[:], c.xnd_t[b * NT + tt])
                    for half in range(2):
                        pb = pG[half]
                        for q in range(4):
                            dc = half * 4 + q
                            p.ex("pe", lambda e, o=pb[:, q * 128:(q + 1) * 128],
                                 i=w[:, dc * 128:(dc + 1) * 128], idn=c.ident[:]:
                                 e.transpose(o, i, idn),
                                 reads=[w[:], c.ident[:]], writes=[pb])
                        dst = xnq2[:].rearrange("p (dc t) -> p dc t", t=512)[
                            :, half * 4:half * 4 + 4, j * 128:(j + 1) * 128]
                        src = pb[:].rearrange("p (dc t) -> p dc t", t=128)
                        _vop(p, "tensor_copy", dst, [src])
                for fc in range(16):
                    pg = pG[fc % 2][:, 0:512]
                    pu = pU[fc % 2][:, 0:512]
                    terms = [(pg, wsg[:, ic * 2048 + fc * 128: ic * 2048 + fc * 128 + 128],
                              xnq2[:, ic * 512: ic * 512 + 512]) for ic in range(8)]
                    p.mmg(terms)
                    terms = [(pu, wsu[:, ic * 2048 + fc * 128: ic * 2048 + fc * 128 + 128],
                              xnq2[:, ic * 512: ic * 512 + 512]) for ic in range(8)]
                    p.mmg(terms)
                    _act(p, gtmp[:], pg, AF.Gelu)
                    _vop(p, "tensor_tensor", h_sh[:, fc * 512:(fc + 1) * 512],
                         [gtmp[:], pu], op=ALU.mult)
                for j in range(4):
                    tt = tch * 4 + j
                    for dh in range(2):
                        pb = pDN[(j * 2 + dh) % 2][:, 0:512]
                        terms = [(pb,
                                  h_sh[:, fc2 * 512 + j * 128: fc2 * 512 + j * 128 + 128],
                                  wsd[:, fc2 * 1024 + dh * 512: fc2 * 1024 + dh * 512 + 512])
                                 for fc2 in range(16)]
                        p.mmg(terms)
                        _vop(p, "tensor_copy",
                             acc[:, tt * 1024 + dh * 512: tt * 1024 + dh * 512 + 512],
                             [pb])
        if "experts" in _L1F_PARTS:
         with p.phase() as es3:
            wg_h = _sb(c, es3, _nm("wgh"), 128, 8 * 1024, BF16)
            wu_h = _sb(c, es3, _nm("wuh"), 128, 8 * 1024, BF16)
            wd_s = _sb(c, es3, _nm("wds"), 128, 16 * 1024, BF16)
            xtk = _sb(c, es3, _nm("xtk"), 128, 2 * 1024, F32)
            xtf = _sb(c, es3, _nm("xtf"), 128, 8 * 256, BF16)
            h_fm = _sb(c, es3, _nm("hfm"), 128, 16 * 256, BF16)
            gtmp = _sb(c, es3, _nm("gtmp2"), 128, 256, BF16)
            eo = [_sb(c, es3, _nm("eo"), 128, 2 * 1024, BF16) for _ in range(4)]
            Ptt = _sb(c, es3, _nm("Ptt"), 128, 8 * 128, BF16)
            iu32 = _sb(c, es3, _nm("iu32"), 128, 2, U32)
            pG = [_ps(c, es3, _nm("pG3"), 128, 512) for _ in range(2)]
            pU = [_ps(c, es3, _nm("pU3"), 128, 512) for _ in range(2)]
            pDN = [_ps(c, es3, _nm("pDN3"), 128, 512) for _ in range(2)]
            pSC = [_ps(c, es3, _nm("pSC"), 128, 512) for _ in range(2)]
            xnd_rows = c.xnd
            for rnd in range(4):
                for er in range(4):
                    e = rnd * 4 + er
                    for ct in range(2):
                        _vop(p, "tensor_scalar_add", iu32[:, ct:ct + 1],
                             [idxT[:, ct * 16 + e: ct * 16 + e + 1], float(b * S)])
                        p.dma_gather(xtk[:, ct * 1024:(ct + 1) * 1024], xnd_rows,
                                     iu32[:, ct:ct + 1])
                        for half in range(2):
                            pb = pG[half]
                            for q in range(4):
                                dc = half * 4 + q
                                p.ex("pe", lambda e2, o=pb[:, q * 128:(q + 1) * 128],
                                     i=xtk[:, ct * 1024 + dc * 128: ct * 1024 + dc * 128 + 128],
                                     idn=c.ident[:]: e2.transpose(o, i, idn),
                                     reads=[xtk[:], c.ident[:]], writes=[pb])
                            dst = xtf[:].rearrange("p (dc t) -> p dc t", t=256)[
                                :, half * 4:half * 4 + 4, ct * 128:(ct + 1) * 128]
                            src = pb[:].rearrange("p (dc t) -> p dc t", t=128)
                            _vop(p, "tensor_copy", dst, [src])
                    _wslab_load(c, wd_s, c.dram["l1f_wd"][e], 2048, 1024)
                    for fh in range(2):
                        _wslab_load(c, wg_h,
                                    c.dram["l1f_wg"][e][:, fh * 1024:(fh + 1) * 1024],
                                    1024, 1024)
                        _wslab_load(c, wu_h,
                                    c.dram["l1f_wu"][e][:, fh * 1024:(fh + 1) * 1024],
                                    1024, 1024)
                        for fcl in range(8):
                            fc = fh * 8 + fcl
                            pg = pG[fcl % 2][:, 0:256]
                            pu = pU[fcl % 2][:, 0:256]
                            terms = [(pg,
                                      wg_h[:, ic * 1024 + fcl * 128: ic * 1024 + fcl * 128 + 128],
                                      xtf[:, ic * 256: ic * 256 + 256]) for ic in range(8)]
                            p.mmg(terms)
                            terms = [(pu,
                                      wu_h[:, ic * 1024 + fcl * 128: ic * 1024 + fcl * 128 + 128],
                                      xtf[:, ic * 256: ic * 256 + 256]) for ic in range(8)]
                            p.mmg(terms)
                            _act(p, gtmp[:], pg, AF.Gelu)
                            _vop(p, "tensor_tensor", h_fm[:, fc * 256:(fc + 1) * 256],
                                 [gtmp[:], pu], op=ALU.mult)
                    for ct in range(2):
                        for dh in range(2):
                            pb = pDN[(ct * 2 + dh) % 2][:, 0:512]
                            terms = [(pb,
                                      h_fm[:, fc2 * 256 + ct * 128: fc2 * 256 + ct * 128 + 128],
                                      wd_s[:, fc2 * 1024 + dh * 512: fc2 * 1024 + dh * 512 + 512])
                                     for fc2 in range(16)]
                            p.mmg(terms)
                            _vop(p, "tensor_copy",
                                 eo[er][:, ct * 1024 + dh * 512: ct * 1024 + dh * 512 + 512],
                                 [pb])
                for tt in range(NT):
                    for k in range(8):
                        er, ct = k // 2, k % 2
                        e = rnd * 4 + er
                        _vop(p, "tensor_scalar", Ptt[:, k * 128:(k + 1) * 128],
                             [c.iota_t[:, tt * 128:(tt + 1) * 128],
                              idxT[:, ct * 16 + e: ct * 16 + e + 1],
                              valsT[:, ct * 16 + e: ct * 16 + e + 1]],
                             op0=ALU.is_equal, op1=ALU.mult)
                    for dh in range(2):
                        pb = pSC[dh][:, 0:512]
                        terms = [(pb, Ptt[:, k * 128:(k + 1) * 128],
                                  eo[k // 2][:, (k % 2) * 1024 + dh * 512:
                                             (k % 2) * 1024 + dh * 512 + 512])
                                 for k in range(8)]
                        p.mmg(terms)
                        dst = acc[:, tt * 1024 + dh * 512: tt * 1024 + dh * 512 + 512]
                        _vop(p, "tensor_add", dst, [dst, pb])
        with p.phase() as es4:
            xr = [_sb(c, es4, _nm("xr"), 128, 1024, F32) for _ in range(2)]
            for tt in range(NT):
                xt = xr[tt % 2]
                p.dma("sp", xt[:], xsrc_t[b * NT + tt])
                _vop(p, "tensor_add", xt[:], [xt[:], acc[:, tt * 1024:(tt + 1) * 1024]])
                p.dma("act", xdst_t[b * NT + tt], xt[:])



def _build(arrs):
    nc = bass.Bass("TRN2", target_bir_lowering=False)
    es = ExitStack()
    c = _B()
    c.nc = nc
    c.dram = {}
    for k, a in arrs.items():
        dt = F32
        c.dram[k] = nc.dram_tensor(k, list(a.shape), dt, kind="ExternalInput").ap()
    x0 = nc.dram_tensor("x0", [BPC, S, D], F32, kind="ExternalInput").ap()
    out = nc.dram_tensor("out", [BPC, S, D], F32, kind="ExternalOutput").ap()
    xs1 = nc.dram_tensor("xs1", [BPC, S, D], F32, kind="Internal").ap()
    xs2 = nc.dram_tensor("xs2", [BPC, S, D], F32, kind="Internal").ap()
    xnd = nc.dram_tensor("xnd", [BPC * S, D], F32, kind="Internal").ap()
    tile = lambda ap: ap.rearrange("b (tt p) d -> (b tt) p d", p=128)
    x0_t, out_t, xs1_t, xs2_t = tile(x0), tile(out), tile(xs1), tile(xs2)

    p = c.p = Prog(nc, es)
    c.ident = _sb(c, es, "ident", 128, 128, F32)
    p.dma("sp", c.ident[:], c.dram["c_ident"])
    c.ones2 = _sb(c, es, "ones2", 2, 128, BF16)
    p.dma("pool", c.ones2[:], c.dram["c_ones2"])
    c.e16 = _sb(c, es, "e16", 16, 16 * 128, BF16)
    p.dma("pool", c.e16[:], c.dram["c_e16"])
    c.konst = _sb(c, es, "konst", 128, 4, F32)
    c.iota_t = _sb(c, es, "iotat", 128, 2048, F32)
    p.ex("pool", lambda e: e.iota(c.iota_t[:], [[1, 2048]], base=0,
                                  channel_multiplier=0,
                                  allow_small_or_imprecise_dtypes=True),
         writes=[c.iota_t[:]])
    _vop(p, "memset", c.konst[:, 0:1], [EPS])
    _vop(p, "memset", c.konst[:, 1:2], [0.0])
    c.xnd_t = xnd.rearrange("(n p) d -> n p d", p=128)
    c.xnd = xnd

    stages = sorted(_STAGES)
    last = stages[-1]

    def src_dst(i):
        srcs = x0_t if i == stages[0] else (xs1_t if i % 2 == 1 else xs2_t)
        dsts = out_t if i == last else (xs1_t if i % 2 == 0 else xs2_t)
        return srcs, dsts

    for b in range(BPC):
        for i in stages:
            xsrc_t, xdst_t = src_dst(i)
            if i == 0:
                _stage_attn_std(c, b, "l0a", xsrc_t, xdst_t,
                                "l0a_wkT", "l0a_wvT", "l0a_wqT", "l0a_woT",
                                "l0a_bq", "l0a_bk", "l0a_bv2", "l0a_bo2")
            elif i == 2:
                _stage_attn_std(c, b, "l0b", xsrc_t, xdst_t,
                                "l0b_wkT", "l0b_wvT", "l0b_wqT", "l0b_woT",
                                "l0b_bq", "l0b_bk", "l0b_bv2", "l0b_bo2")
            elif i == 4:
                _stage_l1a(c, b, xsrc_t, xdst_t)
            elif i in (1, 3):
                _stage_l0f(c, b, xsrc_t, xdst_t)
            elif i == 6:
                _stage_l1b(c, b, xsrc_t, xdst_t)
            else:
                _stage_l1f(c, b, xsrc_t, xdst_t)
    p.replay(es)
    c._es = es
    return nc, p


def kernel(**inputs):
    consts = _prep_consts()
    w = _prep_weights({k: np.asarray(v, dtype=np.float32) for k, v in inputs.items()
                       if k != "L0_attn_a_x"})
    x = np.asarray(inputs["L0_attn_a_x"], dtype=np.float32)
    common = {}
    common.update(consts)
    common.update(w)
    nc, p = _build(common)
    in_maps = []
    for cid in range(NCORES):
        m = dict(common)
        m["x0"] = np.ascontiguousarray(x[cid * BPC:(cid + 1) * BPC])
        in_maps.append(m)
    res = run_bass_kernel_spmd(nc, in_maps, core_ids=list(range(NCORES)))
    out = np.concatenate([np.asarray(r["out"], dtype=np.float32)
                          for r in res.results], axis=0)
    return out


# version 04
# speedup vs baseline: 1.2195x; 1.0583x over previous
import numpy as np
import ml_dtypes
import concourse.bass as bass
import concourse.mybir as mybir
from concourse.bass_utils import run_bass_kernel_spmd
from concourse.bass_core_rust import IndirectOffsetOnAxis
from contextlib import ExitStack, contextmanager

F32 = mybir.dt.float32
BF16 = mybir.dt.bfloat16
U32 = mybir.dt.uint32
AF = mybir.ActivationFunctionType
ALU = mybir.AluOpType
AX = mybir.AxisListType
BF = ml_dtypes.bfloat16

NCORES = 8
BPC = 2
S = 2048
D = 1024
NT = S // 128
NEG = -1.0e37
EPS = 1e-5
_STAGES = list(range(8))
_L1F_PARTS = ("shared", "experts")

ENGS = ("pe", "dve", "act", "pool", "sp")
_ENG_ATTR = {"pe": "tensor", "dve": "vector", "act": "scalar", "pool": "gpsimd", "sp": "sync"}
_DMA_NSEM = {"sp": 10, "act": 6, "pool": 10}


class _Buf:
    __slots__ = ("w", "readers")
    def __init__(self):
        self.w = None
        self.readers = []


class Prog:

    def __init__(self, nc, es):
        self.nc = nc
        self.semh = {}
        self.cnt = {}
        for ek in ENGS:
            self.semh[ek] = es.enter_context(nc.semaphore(f"S{ek}"))
            self.cnt[ek] = 0
        self.dmasems = {}
        self.dmarr = {}
        for qk, n in _DMA_NSEM.items():
            ring = []
            for i in range(n):
                sk = f"D{qk}{i}"
                self.semh[sk] = es.enter_context(nc.semaphore(sk))
                self.cnt[sk] = 0
                ring.append(sk)
            self.dmasems[qk] = ring
            self.dmarr[qk] = 0
        self.queue = {ek: [] for ek in ENGS}
        self.hw = {ek: {} for ek in ENGS}
        self.bufs = {}
        self.nisn = 0

    def _key(self, x):
        if isinstance(x, str):
            return x
        t = getattr(x, "tensor", None)
        return t.name if t is not None else x.name

    def _buf(self, x):
        k = self._key(x)
        b = self.bufs.get(k)
        if b is None:
            b = self.bufs[k] = _Buf()
        return b

    def _deps_for(self, reads, writes):
        deps = []
        for x in reads:
            b = self._buf(x)
            if b.w:
                deps.append(b.w)
        for x in writes:
            b = self._buf(x)
            if b.w:
                deps.append(b.w)
            deps.extend(b.readers)
        return deps

    def _dedup(self, ek, deps):
        hw = self.hw[ek]
        out = []
        for (sk, v) in deps:
            if hw.get(sk, 0) < v:
                hw[sk] = v
                out.append((sk, v))
        return out

    def _register(self, dep, reads, writes):
        for x in reads:
            self._buf(x).readers.append(dep)
        for x in writes:
            b = self._buf(x)
            b.w = dep
            b.readers = []

    def ex(self, ek, build, reads=(), writes=()):
        wl = self._dedup(ek, self._deps_for(reads, writes))
        self.cnt[ek] += 1
        dep = (ek, self.cnt[ek])
        semh, myh = self.semh, self.semh[ek]

        def item(eng, build=build, wl=wl):
            for sk, v in wl:
                eng.wait_ge(semh[sk], v)
            build(eng).then_inc(myh, 1)
        self.queue[ek].append(item)
        self.nisn += 1 + len(wl)
        self._register(dep, reads, writes)
        return dep

    def mmg(self, terms, reads=(), writes=(), start=True, stop=True):
        psum_ap = terms[0][0]
        opkeys = set()
        for (pap, lap, rap) in terms:
            opkeys.add(self._key(lap))
            opkeys.add(self._key(rap))
        opkeys.update(self._key(x) for x in reads)
        wkeys = {self._key(psum_ap)}
        wkeys.update(self._key(x) for x in writes)
        wl = self._dedup("pe", self._deps_for(opkeys, wkeys))
        self.cnt["pe"] += 1
        dep = ("pe", self.cnt["pe"])
        semh, myh = self.semh, self.semh["pe"]
        n = len(terms)

        def item(eng, terms=terms, wl=wl, n=n, start=start, stop=stop):
            for sk, v in wl:
                eng.wait_ge(semh[sk], v)
            for i, (pap, lap, rap) in enumerate(terms):
                inst = eng.matmul(pap, lap, rap, start=(start and i == 0),
                                  stop=(stop and i == n - 1))
            inst.then_inc(myh, 1)
        self.queue["pe"].append(item)
        self.nisn += n + len(wl)
        self._register(dep, opkeys, wkeys)
        return dep

    def _dma_dep(self, qk):
        ring = self.dmasems[qk]
        sk = ring[self.dmarr[qk] % len(ring)]
        self.dmarr[qk] += 1
        self.cnt[sk] += 16
        return sk, self.cnt[sk]

    def dma(self, qk, out_ap, in_ap, reads=(), writes=(), transpose=False):
        rk = [self._key(in_ap)] + [self._key(x) for x in reads]
        wk = [self._key(out_ap)] + [self._key(x) for x in writes]
        deps = self._deps_for(rk, wk)
        sk, v = self._dma_dep(qk)
        if v > 16:
            deps = deps + [(sk, v - 16)]
        wl = self._dedup(qk, deps)
        dep = (sk, v)
        semh = self.semh

        def item(eng, out_ap=out_ap, in_ap=in_ap, wl=wl, sk=sk, transpose=transpose):
            for k2, v2 in wl:
                eng.wait_ge(semh[k2], v2)
            if transpose:
                inst = eng.dma_start_transpose(out_ap, in_ap)
            else:
                inst = eng.dma_start(out_ap, in_ap)
            inst.then_inc(semh[sk], 16)
        self.queue[qk].append(item)
        self.nisn += 1 + len(wl)
        self._register(dep, rk, wk)
        return dep

    def dma_gather(self, out_ap, in_dram_ap, idx_ap, reads=(), writes=()):
        rk = [self._key(in_dram_ap), self._key(idx_ap)] + [self._key(x) for x in reads]
        wk = [self._key(out_ap)] + [self._key(x) for x in writes]
        deps = self._deps_for(rk, wk)
        sk, v = self._dma_dep("pool")
        if v > 16:
            deps = deps + [(sk, v - 16)]
        wl = self._dedup("pool", deps)
        dep = (sk, v)
        semh = self.semh

        def item(eng, out_ap=out_ap, in_dram_ap=in_dram_ap, idx_ap=idx_ap, wl=wl, sk=sk):
            for k2, v2 in wl:
                eng.wait_ge(semh[k2], v2)
            eng.indirect_dma_start(
                out_ap, None, in_dram_ap, IndirectOffsetOnAxis(idx_ap, 0)
            ).then_inc(semh[sk], 16)
        self.queue["pool"].append(item)
        self.nisn += 1 + len(wl)
        self._register(dep, rk, wk)
        return dep

    def barrier(self):
        allsems = list(self.cnt.keys())
        for ek in ENGS:
            wl = []
            hw = self.hw[ek]
            for sk in allsems:
                cur = self.cnt[sk]
                if cur > hw.get(sk, 0):
                    hw[sk] = cur
                    wl.append((sk, cur))
            if not wl:
                continue
            semh = self.semh

            def item(eng, wl=wl):
                for sk, v in wl:
                    eng.wait_ge(semh[sk], v)
            self.queue[ek].append(item)
            self.nisn += len(wl)
        self.bufs = {}

    @contextmanager
    def phase(self, name=""):
        es = ExitStack()
        try:
            yield es
        finally:
            self.barrier()
            es.close()

    def replay(self, es):
        blk = es.enter_context(self.nc.Block())
        for ek in ENGS:
            q = self.queue[ek]
            getattr(blk, _ENG_ATTR[ek])(lambda eng, q=q: [it(eng) for it in q])



def _hilo_rows(b):
    hi = b.astype(BF).astype(np.float32)
    lo = (b - hi).astype(BF).astype(np.float32)
    return np.ascontiguousarray(np.stack([hi, lo]))


def _col8(b):
    c = b.size // 128
    return np.ascontiguousarray(b.reshape(c, 128).T)


def _prep_consts():
    d = {}
    d["c_ident"] = np.eye(128, dtype=np.float32)
    cc = np.arange(2560)[None, :]
    ii = np.arange(128)[:, None]
    dd = ii + 2047 - cc
    alibi = np.where((dd < 0) | (dd >= 1024), NEG, -dd.astype(np.float64)).astype(np.float32)
    causal = np.where(dd < 0, NEG, 0.0).astype(np.float32)
    d["c_strip_alibi"] = np.ascontiguousarray(alibi)
    d["c_strip_causal"] = np.ascontiguousarray(causal)
    d["c_ones2"] = np.ones((2, 128), dtype=np.float32)
    e16 = np.zeros((16, 16 * 128), dtype=np.float32)
    for e in range(16):
        e16[e, e * 128:(e + 1) * 128] = 1.0
    d["c_e16"] = e16
    hd = 64
    theta = 1.0 / 10000.0 ** (np.arange(0, hd, 2, dtype=np.float32) / hd)
    t = np.arange(S, dtype=np.float32)
    fr = np.outer(theta, t)
    fr2 = np.concatenate([fr, fr], 0)
    cos = np.cos(fr2)
    sin = np.sin(fr2)
    sin_signed = sin.copy()
    sin_signed[:32] = -sin[:32]
    d["c_cos"] = np.ascontiguousarray(np.concatenate([cos, cos], 0).astype(np.float32))
    d["c_sin"] = np.ascontiguousarray(np.concatenate([sin_signed, sin_signed], 0).astype(np.float32))
    return d


def _rope_perm(nout):
    j = np.arange(nout)
    h = j // 64
    dd = j % 64
    return h * 64 + (dd + 32) % 64


def _prep_weights(inp):
    d = {}
    T = lambda a: np.ascontiguousarray(a.T)
    d["l0a_wqT"] = T(inp["L0_attn_a_wq"]); d["l0a_wkT"] = T(inp["L0_attn_a_wk"])
    d["l0a_wvT"] = T(inp["L0_attn_a_wv"]); d["l0a_woT"] = T(inp["L0_attn_a_wo"])
    d["l0a_bq"] = _col8(inp["L0_attn_a_bq"]); d["l0a_bk"] = _col8(inp["L0_attn_a_bk"])
    d["l0a_bv2"] = _hilo_rows(inp["L0_attn_a_bv"])
    d["l0a_bo2"] = _hilo_rows(inp["L0_attn_a_bo"])
    d["l0b_wqT"] = T(inp["L0_attn_b_Wq"]); d["l0b_wkT"] = T(inp["L0_attn_b_Wk"])
    d["l0b_wvT"] = T(inp["L0_attn_b_Wv"]); d["l0b_woT"] = T(inp["L0_attn_b_Wo"])
    d["l0b_bq"] = _col8(inp["L0_attn_b_bq"]); d["l0b_bk"] = _col8(inp["L0_attn_b_bk"])
    d["l0b_bv2"] = _hilo_rows(inp["L0_attn_b_bv"])
    d["l0b_bo2"] = _hilo_rows(inp["L0_attn_b_bo"])
    d["l0f_rw"] = np.ascontiguousarray(inp["L0_ffn_router_w"])
    d["l0f_rb2"] = _hilo_rows(inp["L0_ffn_router_b"])
    d["l0f_w1"] = np.ascontiguousarray(inp["L0_ffn_w1"])
    d["l0f_w2"] = np.ascontiguousarray(inp["L0_ffn_w2"])
    d["l0f_b1"] = np.ascontiguousarray(
        np.stack([_col8(inp["L0_ffn_b1"][e]) for e in range(16)])
        .transpose(1, 0, 2).reshape(128, 256))
    d["l0f_b2"] = np.ascontiguousarray(inp["L0_ffn_b2"])
    for k in ("W_dq", "W_uq", "W_dkv", "W_uk", "W_uv", "W_o"):
        d["l1a_" + k] = np.ascontiguousarray(inp["L1_attn_a_" + k])
    wqT = T(inp["L1_attn_b_Wq"])
    wkT = T(inp["L1_attn_b_Wk"])
    d["l1b_wqT"] = wqT
    d["l1b_wqTr"] = np.ascontiguousarray(wqT[:, _rope_perm(1024)])
    def _dup(w):
        r = w.reshape(1024, 4, 1, 64)
        return np.ascontiguousarray(
            np.concatenate([r, r], axis=2).reshape(1024, 512))
    d["l1b_wkT"] = _dup(wkT)
    d["l1b_wkTr"] = _dup(np.ascontiguousarray(wkT[:, _rope_perm(256)]))
    d["l1b_wvT"] = T(inp["L1_attn_b_Wv"])
    d["l1b_woT"] = T(inp["L1_attn_b_Wo"])
    d["l1b_bo2"] = _hilo_rows(inp["L1_attn_b_bo"])
    wsT = T(inp["L1_attn_b_Ws"])
    d["l1b_wsT"] = np.ascontiguousarray(np.concatenate([wsT, wsT], 0))
    d["l1b_bs"] = np.ascontiguousarray(inp["L1_attn_b_bs"].reshape(16, 1))
    d["l1f_gwT"] = T(inp["L1_ffn_gate_w"])
    d["l1f_wg"] = np.ascontiguousarray(inp["L1_ffn_Wg"].transpose(0, 2, 1))
    d["l1f_wu"] = np.ascontiguousarray(inp["L1_ffn_Wu"].transpose(0, 2, 1))
    d["l1f_wd"] = np.ascontiguousarray(inp["L1_ffn_Wd"].transpose(0, 2, 1))
    d["l1f_wsg"] = T(inp["L1_ffn_Ws_g"])
    d["l1f_wsu"] = T(inp["L1_ffn_Ws_u"])
    d["l1f_wsd"] = T(inp["L1_ffn_Ws_d"])
    return d



def _k(x):
    return x if isinstance(x, str) else x.tensor.name


def _apify(items):
    out = []
    for x in items:
        if isinstance(x, str) or hasattr(x, "tensor"):
            out.append(x)
    return out


class _B:
    pass


def _sb(c, es, name, rows, cols, dtype):
    return es.enter_context(c.nc.sbuf_tensor(name, [rows, cols], dtype))


def _ps(c, es, name, rows, cols):
    return es.enter_context(c.nc.psum_tensor(name, [rows, cols], F32))


_UNIQ = [0]
def _nm(base):
    _UNIQ[0] += 1
    return f"{base}_{_UNIQ[0]}"


def _vop(p, meth, out_ap, ins, reads=(), writes=(), eng="dve", **kw):
    def build(e, meth=meth, out_ap=out_ap, ins=ins, kw=kw):
        return getattr(e, meth)(out_ap, *ins, **kw)
    return p.ex(eng, build,
                reads=_apify(list(ins)) + _apify(list(reads)) + _apify(list(kw.values())),
                writes=[out_ap] + _apify(list(writes)))


def _act(p, out_ap, in_ap, func, bias=0.0, scale=1.0, accum=None):
    def build(e, out_ap=out_ap, in_ap=in_ap, func=func, bias=bias, scale=scale, accum=accum):
        if accum is not None:
            return e.activation(out_ap, in_ap, func, bias=bias, scale=scale, accum_out=accum)
        return e.activation(out_ap, in_ap, func, bias=bias, scale=scale)
    writes = [out_ap] + ([accum] if accum is not None else [])
    return p.ex("act", build, reads=_apify([in_ap, bias, scale]), writes=writes)


def _sadd(p, out_ap, in_ap, addv):
    def build(e, out_ap=out_ap, in_ap=in_ap, addv=addv):
        return e.add(out_ap, in_ap, addv)
    return p.ex("act", build, reads=_apify([in_ap, addv]), writes=[out_ap])


def _wslab_load(c, slab, wT_dram, din, dout):
    src = wT_dram.rearrange("(ic p) o -> p ic o", p=128)
    dst = slab[:].rearrange("p (ic o) -> p ic o", o=dout)
    return c.p.dma("pool", dst, src)


def _ln_tiles(c, es, b, xsrc_t, xn_fm, stats, rstd, tiles, pT, xw, xnf, xn_dram_t=None,
              st12=None):
    p, nc = c.p, c.nc
    for j, tt in enumerate(tiles):
        w = xw[j % 2]
        p.dma("sp", w[:], xsrc_t[b * NT + tt])
        _vop(p, "bn_stats", st12[:, 0:6], [w[:, 0:512]])
        _vop(p, "bn_stats", st12[:, 6:12], [w[:, 512:1024]])
        _vop(p, "bn_aggr", stats[:, 2 * tt:2 * tt + 2], [st12[:]])
        _act(p, rstd[:, tt:tt + 1], stats[:, 2 * tt + 1:2 * tt + 2], AF.Ln,
             bias=c.konst[:, 0:1])
        _act(p, rstd[:, tt:tt + 1], rstd[:, tt:tt + 1], AF.Exp, scale=-0.5)
        xf = xnf[j % 2]
        _vop(p, "tensor_scalar", xf[:], [w[:], stats[:, 2 * tt:2 * tt + 1],
                                         rstd[:, tt:tt + 1]],
             op0=ALU.subtract, op1=ALU.mult)
        if xn_dram_t is not None:
            p.dma("pool", xn_dram_t[b * NT + tt], xf[:])
        for half in range(2):
            pb = pT[half]
            for q in range(4):
                dc = half * 4 + q
                p.ex("pe", lambda e, o=pb[:, q * 128:(q + 1) * 128],
                     i=xf[:, dc * 128:(dc + 1) * 128], idn=c.ident[:]:
                     e.transpose(o, i, idn),
                     reads=[xf[:], c.ident[:]], writes=[pb[:]])
            dst = xn_fm[:].rearrange("p (dc t) -> p dc t", t=S)[
                :, half * 4:half * 4 + 4, tt * 128:(tt + 1) * 128]
            src = pb[:].rearrange("p (dc t) -> p dc t", t=128)
            _vop(p, "tensor_copy", dst, [src])


def _proj_fm(c, out_fm, slab, din, dout, src_fm, src_w, bias_col, pbanks, tlo, thi,
             out_w=S):
    p = c.p
    nic, noc = din // 128, dout // 128
    nb = len(pbanks)
    i = 0
    for oc in range(noc):
        for t0 in range(tlo, thi, 512):
            pb = pbanks[i % nb]; i += 1
            terms = [(pb[:, 0:512],
                      slab[:, ic * dout + oc * 128: ic * dout + oc * 128 + 128],
                      src_fm[:, ic * src_w + t0: ic * src_w + t0 + 512])
                     for ic in range(nic)]
            p.mmg(terms)
            dst = out_fm[:, oc * out_w + (t0 - tlo): oc * out_w + (t0 - tlo) + 512]
            bias = bias_col[:, oc:oc + 1] if bias_col is not None else 0.0
            _sadd(p, dst, pb[:, 0:512], bias)


def _proj_tm(c, out_tm, slab, din, dout, src_fm, src_w, bias2, pbanks, tiles):
    p = c.p
    nic = din // 128
    nb = len(pbanks)
    i = 0
    for tt in tiles:
        for o0 in range(0, dout, 512):
            pb = pbanks[i % nb]; i += 1
            terms = []
            if bias2 is not None:
                terms.append((pb[:, 0:512], c.ones2[:, 0:128], bias2[:, o0:o0 + 512]))
            terms += [(pb[:, 0:512],
                       src_fm[:, ic * src_w + tt * 128: ic * src_w + tt * 128 + 128],
                       slab[:, ic * dout + o0: ic * dout + o0 + 512])
                      for ic in range(nic)]
            p.mmg(terms)
            _vop(p, "tensor_copy", out_tm[:, tt * dout + o0: tt * dout + o0 + 512],
                 [pb[:, 0:512]])


def _attn_out_proj(c, es, b, qch, attn_fm, wo_slab, bo2_sb, xdst_t, pbanks, xr,
                   resid_src_t):
    p = c.p
    for j in range(4):
        tt = qch * 4 + j
        xt = xr[j % 2]
        p.dma("sp", xt[:], resid_src_t[b * NT + tt])
        for oh in range(2):
            pb = pbanks[(j * 2 + oh) % len(pbanks)]
            terms = [(pb[:, 0:512], c.ones2[:, 0:128], bo2_sb[:, oh * 512:oh * 512 + 512])]
            terms += [(pb[:, 0:512],
                       attn_fm[:, dc * 512 + j * 128: dc * 512 + j * 128 + 128],
                       wo_slab[:, dc * 1024 + oh * 512: dc * 1024 + oh * 512 + 512])
                      for dc in range(8)]
            p.mmg(terms)
            _vop(p, "tensor_add", xt[:, oh * 512:oh * 512 + 512],
                 [xt[:, oh * 512:oh * 512 + 512], pb[:, 0:512]])
        p.dma("act", xdst_t[b * NT + tt], xt[:])


def _load_bias_rows2(c, es, name, dram, n):
    t = _sb(c, es, _nm(name), 2, n, BF16)
    c.p.dma("pool", t[:], dram)
    return t


def _load_bias_col(c, es, name, dram, ncol):
    t = _sb(c, es, _nm(name), 128, ncol, F32)
    c.p.dma("sp", t[:], dram)
    return t


def _attn_core(c, es, b, kind, K_fm, V_tm, wq_slab, wo_slab, bq_col, bo2_sb,
               resid_src_t, xdst_t, xn_fm, cq_fm=None):
    p, nc = c.p, c.nc
    scale = 0.125 if kind in ("l0a", "l1a") else 1.0 / 32.0
    KW = 1536 if kind == "l0a" else 2048
    R = 3
    Qf = _sb(c, es, _nm("Qf"), 128, 8 * 512, BF16)
    probs = [_sb(c, es, _nm("probs"), 128, KW, BF16) for _ in range(R)]
    probsT = [_sb(c, es, _nm("probsT"), 128, KW, BF16) for _ in range(R)]
    if kind == "l0a":
        asbc = [_sb(c, es, _nm("asbc"), 128, 512, F32) for _ in range(3)]
    pS = [_ps(c, es, _nm("pS"), 128, 512) for _ in range(6)]
    pAV = [_ps(c, es, _nm("pAV"), 128, 512) for _ in range(2)]
    pb_t = [pAV[0][:, 0:512], pAV[1][:, 0:512]]
    afm = _sb(c, es, _nm("afm"), 128, 8 * 512, BF16)
    xr = [_sb(c, es, _nm("xr"), 128, 1024, F32) for _ in range(2)]
    nmc = _sb(c, es, _nm("nmc"), 128, R * 4, F32)
    nmsc = _sb(c, es, _nm("nmsc"), 128, R * 4, F32)
    fvec = _sb(c, es, _nm("fvec"), 128, R * 4, F32)
    zc = _sb(c, es, _nm("zc"), 128, R * 4, F32)
    nm1 = _sb(c, es, _nm("nm"), 128, R, F32)
    zz = _sb(c, es, _nm("zz"), 128, R, F32)
    scnt = [0]

    def slot_geom(i):
        h, j = divmod(i, 4)
        return h, j

    for qch in range(4):
        if kind == "l1a":
            nic, src, srcw = 2, cq_fm, S
        else:
            nic, src, srcw = 8, xn_fm, S
        off = qch * 512
        for oc in range(8):
            pb = pb_t[oc % 2]
            terms = [(pb, wq_slab[:, ic * 1024 + oc * 128: ic * 1024 + oc * 128 + 128],
                      src[:, ic * srcw + off: ic * srcw + off + 512])
                     for ic in range(nic)]
            p.mmg(terms)
            bias = bq_col[:, oc:oc + 1] if bq_col is not None else 0.0
            _sadd(p, Qf[:, oc * 512:(oc + 1) * 512], pb, bias)

        def geom(i):
            h, j = slot_geom(i)
            qt = qch * 4 + j
            qb = qt * 128
            if kind == "l0a":
                kc_lo = max(0, qb - 1023) // 512
                kc_hi = (qb + 127) // 512
            elif kind == "l1a":
                kc_lo, kc_hi = 0, (qb + 127) // 512
            else:
                kc_lo, kc_hi = 0, 3
            return h, j, qb, kc_lo, kc_hi

        def S1(i):
            h, j, qb, kc_lo, kc_hi = geom(i)
            r = i % R
            lq = Qf[(h % 2) * 64:(h % 2) * 64 + 64,
                    (h // 2) * 512 + j * 128: (h // 2) * 512 + j * 128 + 128]
            for i2, kc in enumerate(range(kc_lo, kc_hi + 1)):
                pb = pS[scnt[0] % 6][:]
                scnt[0] += 1
                rk = K_fm[(h % 2) * 64:(h % 2) * 64 + 64,
                          (h // 2) * 2048 + kc * 512: (h // 2) * 2048 + kc * 512 + 512]
                p.mmg([(pb, lq, rk)])
                esrc = pb
                if kind == "l0a":
                    co = 2047 - (qb - kc * 512)
                    slope = (2.0 ** (-2.0 ** -1.0)) ** (h + 1)
                    ac = asbc[scnt[0] % 3][:]
                    _vop(p, "scalar_tensor_tensor", ac,
                         [c.strip[:, co:co + 512], 8.0 * slope, pb],
                         op0=ALU.mult, op1=ALU.add)
                    esrc = ac
                elif kind == "l1a" and kc == kc_hi:
                    co = 2047 - (qb - kc * 512)
                    _vop(p, "tensor_add", pb, [pb, c.strip[:, co:co + 512]])
                col = nmc[:, r * 4 + i2: r * 4 + i2 + 1]
                cols = nmsc[:, r * 4 + i2: r * 4 + i2 + 1]
                _vop(p, "reduce_max", col, [esrc], axis=AX.X, negate=True)
                _vop(p, "tensor_scalar_mul", cols, [col, scale])
                _act(p, probs[r][:, i2 * 512:(i2 + 1) * 512], esrc, AF.Exp,
                     bias=cols, scale=scale,
                     accum=zc[:, r * 4 + i2: r * 4 + i2 + 1])

        def S2(i):
            h, j, qb, kc_lo, kc_hi = geom(i)
            r = i % R
            nkc = kc_hi - kc_lo + 1
            nmv = nm1[:, r:r + 1]
            zzv = zz[:, r:r + 1]
            fv = fvec[:, r * 4: r * 4 + nkc]
            _vop(p, "tensor_reduce", nmv, [nmc[:, r * 4: r * 4 + nkc]],
                 axis=AX.X, op=ALU.min)
            _vop(p, "tensor_scalar", fv, [nmc[:, r * 4: r * 4 + nkc], nmv, None],
                 op0=ALU.subtract)
            _act(p, fv, fv, AF.Exp, scale=-scale)
            _vop(p, "tensor_mul", zc[:, r * 4: r * 4 + nkc],
                 [zc[:, r * 4: r * 4 + nkc], fv])
            _vop(p, "reduce_sum", zzv, [zc[:, r * 4: r * 4 + nkc]], axis=AX.X)
            _vop(p, "reciprocal", zzv, [zzv])
            _vop(p, "tensor_scalar_mul", fv, [fv, zzv])
            for i2 in range(nkc):
                sl = probs[r][:, i2 * 512:(i2 + 1) * 512]
                _vop(p, "tensor_scalar_mul", sl, [sl, fv[:, i2:i2 + 1]])
                p.dma("sp", probsT[r][:, i2 * 512:(i2 + 1) * 512].rearrange(
                    "p (kt q) -> p kt q", q=128), sl, transpose=True)

        def AV(i):
            h, j, qb, kc_lo, kc_hi = geom(i)
            r = i % R
            nkc = kc_hi - kc_lo + 1
            pav = pAV[i % 2][0:64, 0:128]
            terms = []
            for kt in range(nkc * 4):
                ktg = kc_lo * 4 + kt
                terms.append((pav,
                              V_tm[:, ktg * 1024 + h * 64: ktg * 1024 + h * 64 + 64],
                              probsT[r][:, kt * 128: kt * 128 + 128]))
            p.mmg(terms)
            dst = afm[(h % 2) * 64:(h % 2) * 64 + 64,
                      (h // 2) * 512 + j * 128: (h // 2) * 512 + j * 128 + 128]
            _vop(p, "tensor_copy", dst, [pav])

        n = 64
        for i in range(n + 2):
            if i < n:
                S1(i)
            if 1 <= i <= n:
                S2(i - 1)
            if i >= 2:
                AV(i - 2)
        _attn_out_proj(c, es, b, qch, afm, wo_slab, bo2_sb, xdst_t,
                       pb_t, xr, resid_src_t)


def _stage_attn_std(c, b, kind, xsrc_t, xdst_t, wk, wv, wq, wo, bq, bk, bv2, bo2):
    p = c.p
    with p.phase() as eso:
        if kind == "l0a":
            c.strip = _sb(c, eso, _nm("strip"), 128, 2560, F32)
            p.dma("sp", c.strip[:], c.dram["c_strip_alibi"])
        slab_a = _sb(c, eso, _nm("slabA"), 128, 8 * 1024, BF16)
        slab_b = _sb(c, eso, _nm("slabB"), 128, 8 * 1024, BF16)
        K_fm = _sb(c, eso, _nm("Kfm"), 128, 8 * 2048, BF16)
        V_tm = _sb(c, eso, _nm("Vtm"), 128, 16 * 1024, BF16)
        stats = _sb(c, eso, _nm("stats"), 128, 2 * NT, F32)
        rstd = _sb(c, eso, _nm("rstd"), 128, NT, F32)
        bq_col = _load_bias_col(c, eso, "bq", c.dram[bq], 8)
        bk_col = _load_bias_col(c, eso, "bk", c.dram[bk], 8)
        bv2_sb = _load_bias_rows2(c, eso, "bv2", c.dram[bv2], 1024)
        bo2_sb = _load_bias_rows2(c, eso, "bo2", c.dram[bo2], 1024)
        xn_fm = _sb(c, eso, _nm("xnfm"), 128, 8 * 2048, BF16)
        _wslab_load(c, slab_a, c.dram[wk], 1024, 1024)
        _wslab_load(c, slab_b, c.dram[wv], 1024, 1024)
        with p.phase() as es1:
            st12 = _sb(c, es1, _nm("st12"), 128, 12, F32)
            xw = [_sb(c, es1, _nm("xw"), 128, 1024, F32) for _ in range(2)]
            xnf = [_sb(c, es1, _nm("xnf"), 128, 1024, F32) for _ in range(2)]
            pT = [_ps(c, es1, _nm("pT"), 128, 512) for _ in range(2)]
            pK = [_ps(c, es1, _nm("pK"), 128, 512) for _ in range(2)]
            pV = [_ps(c, es1, _nm("pV"), 128, 512) for _ in range(2)]
            _ln_tiles(c, es1, b, xsrc_t, xn_fm, stats, rstd, range(NT), pT,
                      xw, xnf, st12=st12)
            _proj_fm(c, K_fm, slab_a, 1024, 1024, xn_fm, S, bk_col, pK, 0, S)
            _proj_tm(c, V_tm, slab_b, 1024, 1024, xn_fm, S, bv2_sb, pV, range(NT))
        _wslab_load(c, slab_a, c.dram[wq], 1024, 1024)
        _wslab_load(c, slab_b, c.dram[wo], 1024, 1024)
        with p.phase() as es2:
            _attn_core(c, es2, b, kind, K_fm, V_tm, slab_a, slab_b, bq_col,
                       bo2_sb, xsrc_t, xdst_t, xn_fm)


def _stage_l1a(c, b, xsrc_t, xdst_t):
    p = c.p
    with p.phase() as eso:
        c.strip = _sb(c, eso, _nm("strip"), 128, 2560, F32)
        p.dma("sp", c.strip[:], c.dram["c_strip_causal"])
        s_dq = _sb(c, eso, _nm("sdq"), 128, 8 * 256, BF16)
        s_dkv = _sb(c, eso, _nm("sdkv"), 128, 8 * 256, BF16)
        s_uk = _sb(c, eso, _nm("suk"), 128, 2 * 1024, BF16)
        s_uv = _sb(c, eso, _nm("suv"), 128, 2 * 1024, BF16)
        K_fm = _sb(c, eso, _nm("Kfm"), 128, 8 * 2048, BF16)
        V_tm = _sb(c, eso, _nm("Vtm"), 128, 16 * 1024, BF16)
        cq_fm = _sb(c, eso, _nm("cqfm"), 128, 2 * 2048, BF16)
        ckv_fm = _sb(c, eso, _nm("ckvfm"), 128, 2 * 2048, BF16)
        stats = _sb(c, eso, _nm("stats"), 128, 2 * NT, F32)
        rstd = _sb(c, eso, _nm("rstd"), 128, NT, F32)
        st12 = _sb(c, eso, _nm("st12"), 128, 12, F32)
        xw = [_sb(c, eso, _nm("xw"), 128, 1024, F32) for _ in range(2)]
        xnf = [_sb(c, eso, _nm("xnf"), 128, 1024, F32) for _ in range(2)]
        zero2 = _sb(c, eso, _nm("zero2"), 2, 1024, BF16)
        _vop(p, "memset", zero2[:], [0.0], eng="dve")
        _wslab_load(c, s_dq, c.dram["l1a_W_dq"], 1024, 256)
        _wslab_load(c, s_dkv, c.dram["l1a_W_dkv"], 1024, 256)
        _wslab_load(c, s_uk, c.dram["l1a_W_uk"], 256, 1024)
        _wslab_load(c, s_uv, c.dram["l1a_W_uv"], 256, 1024)
        with p.phase() as es1:
            xn_fm = _sb(c, es1, _nm("xnfm"), 128, 8 * 2048, BF16)
            pT = [_ps(c, es1, _nm("pT"), 128, 512) for _ in range(2)]
            pK = [_ps(c, es1, _nm("pK"), 128, 512) for _ in range(2)]
            pV = [_ps(c, es1, _nm("pV"), 128, 512) for _ in range(2)]
            _ln_tiles(c, es1, b, xsrc_t, xn_fm, stats, rstd, range(NT), pT,
                      xw, xnf, st12=st12)
            _proj_fm(c, cq_fm, s_dq, 1024, 256, xn_fm, S, None, pK, 0, S)
            _proj_fm(c, ckv_fm, s_dkv, 1024, 256, xn_fm, S, None, pV, 0, S)
            _proj_fm(c, K_fm, s_uk, 256, 1024, ckv_fm, S, None, pK, 0, S)
            _proj_tm(c, V_tm, s_uv, 256, 1024, ckv_fm, S, None, pV, range(NT))
        s_uq = _sb(c, eso, _nm("suq"), 128, 2 * 1024, BF16)
        s_o = _sb(c, eso, _nm("so"), 128, 8 * 1024, BF16)
        _wslab_load(c, s_uq, c.dram["l1a_W_uq"], 256, 1024)
        _wslab_load(c, s_o, c.dram["l1a_W_o"], 1024, 1024)
        with p.phase() as es2:
            _attn_core(c, es2, b, "l1a", K_fm, V_tm, s_uq, s_o, None,
                       zero2, xsrc_t, xdst_t, None, cq_fm=cq_fm)


def _stage_l0f(c, b, xsrc_t, xdst_t):
    p = c.p
    with p.phase() as eso:
        xn_fm = _sb(c, eso, _nm("xnfm"), 128, 8 * 2048, BF16)
        acc = _sb(c, eso, _nm("acc"), 128, 16 * 1024, F32)
        C_sb = _sb(c, eso, _nm("Csb"), 16, 2048, BF16)
        stats = _sb(c, eso, _nm("stats"), 128, 2 * NT, F32)
        rstd = _sb(c, eso, _nm("rstd"), 128, NT, F32)
        with p.phase() as es1:
            st12 = _sb(c, es1, _nm("st12"), 128, 12, F32)
            xw = [_sb(c, es1, _nm("xw"), 128, 1024, F32) for _ in range(2)]
            xnf = [_sb(c, es1, _nm("xnf"), 128, 1024, F32) for _ in range(2)]
            rw_slab = _sb(c, es1, _nm("rws"), 128, 8 * 16, BF16)
            rb2_sb = _load_bias_rows2(c, es1, "rb2", c.dram["l0f_rb2"], 16)
            _wslab_load(c, rw_slab, c.dram["l0f_rw"], 1024, 16)
            pT = [_ps(c, es1, _nm("pT"), 128, 512) for _ in range(2)]
            pR = [_ps(c, es1, _nm("pR"), 128, 512) for _ in range(2)]
            _ln_tiles(c, es1, b, xsrc_t, xn_fm, stats, rstd, range(NT), pT,
                      xw, xnf, st12=st12)
            prb = _sb(c, es1, _nm("prb"), 128, 16, F32)
            m8 = _sb(c, es1, _nm("m8"), 128, 8, F32)
            msk = _sb(c, es1, _nm("msk"), 128, 16, F32)
            ctm = _sb(c, es1, _nm("ctm"), 128, 16, F32)
            nmz = _sb(c, es1, _nm("nmz"), 128, 2, F32)
            for tt in range(NT):
                pb = pR[tt % 2]
                terms = [(pb[:, 0:16], c.ones2[:, 0:128], rb2_sb[:, 0:16])]
                terms += [(pb[:, 0:16],
                           xn_fm[:, ic * 2048 + tt * 128: ic * 2048 + tt * 128 + 128],
                           rw_slab[:, ic * 16: ic * 16 + 16]) for ic in range(8)]
                p.mmg(terms)
                _vop(p, "reduce_max", nmz[:, 0:1], [pb[:, 0:16]], axis=AX.X, negate=True)
                _act(p, prb[:], pb[:, 0:16], AF.Exp, bias=nmz[:, 0:1])
                _vop(p, "max", m8[:], [prb[:]])
                _vop(p, "tensor_scalar", msk[:], [prb[:], m8[:, 3:4], None],
                     op0=ALU.is_ge)
                _vop(p, "tensor_mul", msk[:], [msk[:], prb[:]])
                _vop(p, "reduce_sum", nmz[:, 1:2], [msk[:]], axis=AX.X)
                _vop(p, "reciprocal", nmz[:, 1:2], [nmz[:, 1:2]])
                _vop(p, "tensor_scalar_mul", ctm[:], [msk[:], nmz[:, 1:2]])
                p.ex("pe", lambda e, o=pb[0:16, 16:144], i=ctm[:], idn=c.ident[:]:
                     e.transpose(o, i, idn), reads=[ctm[:], c.ident[:]],
                     writes=[pb])
                _vop(p, "tensor_copy", C_sb[0:16, tt * 128:(tt + 1) * 128],
                     [pb[0:16, 16:144]])
        with p.phase() as es2:
            w1s = [_sb(c, es2, _nm("w1s"), 128, 8 * 2048, BF16)]
            w2s = [_sb(c, es2, _nm("w2s"), 128, 16 * 1024, BF16)]
            hbuf = _sb(c, es2, _nm("hbuf"), 128, 16 * 512, BF16)
            b1_all = _sb(c, es2, _nm("b1a"), 128, 16 * 16, F32)
            p.dma("sp", b1_all[:], c.dram["l0f_b1"])
            b2_sb = _sb(c, es2, _nm("b2s"), 16, 1024, BF16)
            p.dma("pool", b2_sb[:], c.dram["l0f_b2"])
            _wslab_load(c, w1s[0], c.dram["l0f_w1"][0], 1024, 2048)
            _wslab_load(c, w2s[0], c.dram["l0f_w2"][0], 2048, 1024)
            pUP = [_ps(c, es2, _nm("pUP"), 128, 512) for _ in range(2)]
            pDN = [_ps(c, es2, _nm("pDN"), 128, 512) for _ in range(2)]
            pCE = _ps(c, es2, _nm("pCE"), 128, 512)
            ceb = _sb(c, es2, _nm("ceb"), 128, 512, BF16)
            for tt in range(NT):
                for dh in range(2):
                    pb = pDN[(tt * 2 + dh) % 2]
                    p.mmg([(pb[:, 0:512], C_sb[0:16, tt * 128:(tt + 1) * 128],
                            b2_sb[0:16, dh * 512:(dh + 1) * 512])])
                    _vop(p, "tensor_copy",
                         acc[:, tt * 1024 + dh * 512: tt * 1024 + dh * 512 + 512],
                         [pb[:, 0:512]])
            for e in range(16):
                for tch in range(4):
                    p.mmg([(pCE[:, 0:512], c.e16[:, e * 128:(e + 1) * 128],
                            C_sb[0:16, tch * 512:(tch + 1) * 512])])
                    _vop(p, "tensor_copy", ceb[:], [pCE[:, 0:512]])
                    for fc in range(16):
                        pb = pUP[fc % 2]
                        terms = [(pb[:, 0:512],
                                  w1s[0][:, ic * 2048 + fc * 128: ic * 2048 + fc * 128 + 128],
                                  xn_fm[:, ic * 2048 + tch * 512: ic * 2048 + tch * 512 + 512])
                                 for ic in range(8)]
                        p.mmg(terms)
                        _act(p, hbuf[:, fc * 512:(fc + 1) * 512], pb[:, 0:512],
                             AF.Gelu, bias=b1_all[:, e * 16 + fc: e * 16 + fc + 1])
                    _vop(p, "tensor_tensor", hbuf[:].rearrange("p (fc t) -> p fc t", t=512),
                         [hbuf[:].rearrange("p (fc t) -> p fc t", t=512),
                          ceb[:].rearrange("p (o t) -> p o t", o=1).to_broadcast([128, 16, 512])],
                         op=ALU.mult)
                    for j in range(4):
                        tt = tch * 4 + j
                        for dh in range(2):
                            pb = pDN[(j * 2 + dh) % 2]
                            terms = [(pb[:, 0:512],
                                      hbuf[:, fc * 512 + j * 128: fc * 512 + j * 128 + 128],
                                      w2s[0][:, fc * 1024 + dh * 512: fc * 1024 + dh * 512 + 512])
                                     for fc in range(16)]
                            p.mmg(terms)
                            dst = acc[:, tt * 1024 + dh * 512: tt * 1024 + dh * 512 + 512]
                            _vop(p, "tensor_add", dst, [dst, pb[:, 0:512]])
                if e + 1 < 16:
                    _wslab_load(c, w1s[0], c.dram["l0f_w1"][e + 1], 1024, 2048)
                    _wslab_load(c, w2s[0], c.dram["l0f_w2"][e + 1], 2048, 1024)
        with p.phase() as es3:
            xr = [_sb(c, es3, _nm("xr"), 128, 1024, F32) for _ in range(2)]
            for tt in range(NT):
                xt = xr[tt % 2]
                p.dma("sp", xt[:], xsrc_t[b * NT + tt])
                _vop(p, "tensor_add", xt[:], [xt[:], acc[:, tt * 1024:(tt + 1) * 1024]])
                p.dma("act", xdst_t[b * NT + tt], xt[:])


def _rope_combine(c, pA, pB, cosv, sinv, dst):
    p = c.p
    t1 = c.rs1
    t2 = c.rs2
    _vop(p, "tensor_mul", t1[:], [pB, sinv])
    _vop(p, "tensor_mul", t2[:], [pA, cosv])
    _vop(p, "tensor_add", dst, [t1[:], t2[:]])


def _stage_l1b(c, b, xsrc_t, xdst_t):
    p = c.p
    with p.phase() as eso:
        c.strip = _sb(c, eso, _nm("strip"), 128, 2560, F32)
        p.dma("sp", c.strip[:], c.dram["c_strip_causal"])
        K_ro = _sb(c, eso, _nm("Kro"), 128, 4 * 2048, BF16)
        V_tm = _sb(c, eso, _nm("Vtm"), 128, 16 * 256, BF16)
        k_sp = _sb(c, eso, _nm("ksp"), 16, 4 * 2048, BF16)
        stats = _sb(c, eso, _nm("stats"), 128, 2 * NT, F32)
        rstd = _sb(c, eso, _nm("rstd"), 128, NT, F32)
        cosb = _sb(c, eso, _nm("cosb"), 128, 2048, F32)
        sinb = _sb(c, eso, _nm("sinb"), 128, 2048, F32)
        wsT = _sb(c, eso, _nm("wsT"), 128, 16, BF16)
        bs_col = _sb(c, eso, _nm("bs"), 16, 1, F32)
        bo2_sb = _load_bias_rows2(c, eso, "bo2", c.dram["l1b_bo2"], 1024)
        c.rs1 = _sb(c, eso, _nm("rs1"), 128, 512, F32)
        c.rs2 = _sb(c, eso, _nm("rs2"), 128, 512, F32)
        p.dma("sp", cosb[:], c.dram["c_cos"])
        p.dma("sp", sinb[:], c.dram["c_sin"])
        p.dma("pool", wsT[:], c.dram["l1b_wsT"])
        p.dma("sp", bs_col[:], c.dram["l1b_bs"])
        with p.phase() as es1:
            xn_fm = _sb(c, es1, _nm("xnfm"), 128, 8 * 2048, BF16)
            st12 = _sb(c, es1, _nm("st12"), 128, 12, F32)
            xw = [_sb(c, es1, _nm("xw"), 128, 1024, F32) for _ in range(2)]
            xnf = [_sb(c, es1, _nm("xnf"), 128, 1024, F32) for _ in range(2)]
            s_k = _sb(c, es1, _nm("sk"), 128, 8 * 512, BF16)
            s_kr = _sb(c, es1, _nm("skr"), 128, 8 * 512, BF16)
            s_v = _sb(c, es1, _nm("sv"), 128, 8 * 256, BF16)
            _wslab_load(c, s_k, c.dram["l1b_wkT"], 1024, 512)
            _wslab_load(c, s_kr, c.dram["l1b_wkTr"], 1024, 512)
            _wslab_load(c, s_v, c.dram["l1b_wvT"], 1024, 256)
            pT = [_ps(c, es1, _nm("pT"), 128, 512) for _ in range(2)]
            pK = [_ps(c, es1, _nm("pK"), 128, 512) for _ in range(2)]
            pV = [_ps(c, es1, _nm("pV"), 128, 512) for _ in range(2)]
            _ln_tiles(c, es1, b, xsrc_t, xn_fm, stats, rstd, range(NT), pT,
                      xw, xnf, st12=st12)
            for oc in range(4):
                for tch in range(4):
                    pa, pb = pK[0][:, 0:512], pK[1][:, 0:512]
                    t0 = tch * 512
                    terms = [(pa, s_k[:, ic * 512 + oc * 128: ic * 512 + oc * 128 + 128],
                              xn_fm[:, ic * 2048 + t0: ic * 2048 + t0 + 512])
                             for ic in range(8)]
                    p.mmg(terms)
                    terms = [(pb, s_kr[:, ic * 512 + oc * 128: ic * 512 + oc * 128 + 128],
                              xn_fm[:, ic * 2048 + t0: ic * 2048 + t0 + 512])
                             for ic in range(8)]
                    p.mmg(terms)
                    _rope_combine(c, pa, pb, cosb[:, t0:t0 + 512], sinb[:, t0:t0 + 512],
                                  K_ro[:, oc * 2048 + t0: oc * 2048 + t0 + 512])
            for tt in range(NT):
                pbv = pV[tt % 2][:, 0:256]
                terms = [(pbv,
                          xn_fm[:, ic * 2048 + tt * 128: ic * 2048 + tt * 128 + 128],
                          s_v[:, ic * 256: ic * 256 + 256]) for ic in range(8)]
                p.mmg(terms)
                _vop(p, "tensor_copy", V_tm[:, tt * 256:(tt + 1) * 256], [pbv])
            for kvh in range(4):
                for tch in range(4):
                    pbs = pT[tch % 2][0:16, 0:512]
                    rk = K_ro[0:64, kvh * 2048 + tch * 512: kvh * 2048 + tch * 512 + 512]
                    p.mmg([(pbs, wsT[0:64, 0:16], rk)])
                    _sadd(p, k_sp[0:16, kvh * 2048 + tch * 512: kvh * 2048 + tch * 512 + 512],
                          pbs, bs_col[0:16, 0:1])
        s_q = _sb(c, eso, _nm("sq"), 128, 8 * 1024, BF16)
        s_qr = _sb(c, eso, _nm("sqr"), 128, 8 * 1024, BF16)
        s_o = _sb(c, eso, _nm("so"), 128, 8 * 1024, BF16)
        _wslab_load(c, s_q, c.dram["l1b_wqT"], 1024, 1024)
        _wslab_load(c, s_qr, c.dram["l1b_wqTr"], 1024, 1024)
        _wslab_load(c, s_o, c.dram["l1b_woT"], 1024, 1024)
        with p.phase() as es2:
            xnf = [_sb(c, es2, _nm("xnf"), 128, 1024, F32) for _ in range(2)]
            xnq = _sb(c, es2, _nm("xnq"), 128, 8 * 512, BF16)
            Qro = _sb(c, es2, _nm("Qro"), 128, 8 * 512, BF16)
            q_sp = _sb(c, es2, _nm("qsp"), 16, 512, BF16)
            tnh = [c.rs1, c.rs2]
            probs = [_sb(c, es2, _nm("probs"), 128, 2048, BF16) for _ in range(3)]
            probsT = [_sb(c, es2, _nm("probsT"), 128, 2048, BF16) for _ in range(3)]
            asbc = [_sb(c, es2, _nm("asbc"), 128, 512, F32) for _ in range(3)]
            afm = _sb(c, es2, _nm("afm"), 128, 8 * 512, BF16)
            xr = [_sb(c, es2, _nm("xr"), 128, 1024, F32) for _ in range(2)]
            nmc = _sb(c, es2, _nm("nmc"), 128, 12, F32)
            nmsc = _sb(c, es2, _nm("nmsc"), 128, 12, F32)
            fvec = _sb(c, es2, _nm("fvec"), 128, 12, F32)
            zc = _sb(c, es2, _nm("zc"), 128, 12, F32)
            nm1 = _sb(c, es2, _nm("nm"), 128, 3, F32)
            zz = _sb(c, es2, _nm("zz"), 128, 3, F32)
            pQK = [_ps(c, es2, _nm("pQK"), 128, 512) for _ in range(3)]
            pSP = [_ps(c, es2, _nm("pSP"), 128, 512) for _ in range(3)]
            pAV = [_ps(c, es2, _nm("pAV"), 128, 512) for _ in range(2)]
            pb_t = [pAV[0][:, 0:512], pAV[1][:, 0:512]]
            for qch in range(4):
                for j in range(4):
                    tt = qch * 4 + j
                    w = xr[j % 2]
                    p.dma("sp", w[:], xsrc_t[b * NT + tt])
                    xf = xnf[j % 2]
                    _vop(p, "tensor_scalar", xf[:], [w[:], stats[:, 2 * tt:2 * tt + 1],
                                                     rstd[:, tt:tt + 1]],
                         op0=ALU.subtract, op1=ALU.mult)
                    for half in range(2):
                        pbh = pb_t[half]
                        for q in range(4):
                            dc = half * 4 + q
                            p.ex("pe", lambda e, o=pbh[:, q * 128:(q + 1) * 128],
                                 i=xf[:, dc * 128:(dc + 1) * 128], idn=c.ident[:]:
                                 e.transpose(o, i, idn),
                                 reads=[xf[:], c.ident[:]], writes=[pbh])
                        dst = xnq[:].rearrange("p (dc t) -> p dc t", t=512)[
                            :, half * 4:half * 4 + 4, j * 128:(j + 1) * 128]
                        src = pbh.rearrange("p (dc t) -> p dc t", t=128)
                        _vop(p, "tensor_copy", dst, [src])
                t0g = qch * 512
                for oc in range(8):
                    pa, pb2 = pQK[0][:, 0:512], pQK[1][:, 0:512]
                    terms = [(pa, s_q[:, ic * 1024 + oc * 128: ic * 1024 + oc * 128 + 128],
                              xnq[:, ic * 512: ic * 512 + 512]) for ic in range(8)]
                    p.mmg(terms)
                    terms = [(pb2, s_qr[:, ic * 1024 + oc * 128: ic * 1024 + oc * 128 + 128],
                              xnq[:, ic * 512: ic * 512 + 512]) for ic in range(8)]
                    p.mmg(terms)
                    _rope_combine(c, pa, pb2, cosb[:, t0g:t0g + 512],
                                  sinb[:, t0g:t0g + 512], Qro[:, oc * 512:(oc + 1) * 512])
                def geom(i):
                    h, j = divmod(i, 4)
                    qt = qch * 4 + j
                    qb = qt * 128
                    return h, j, qb, (qb + 127) // 512

                def S1(i):
                    h, j, qb, kc_hi = geom(i)
                    r = i % 3
                    kvh = h // 4
                    base = (h % 2) * 64
                    if j == 0:
                        pbs = pSP[0][0:16, 0:512]
                        p.mmg([(pbs, wsT[base:base + 64, 0:16],
                                Qro[base:base + 64, (h // 2) * 512:(h // 2) * 512 + 512])])
                        _sadd(p, q_sp[0:16, 0:512], pbs, bs_col[0:16, 0:1])
                    lq = Qro[base:base + 64, (h // 2) * 512 + j * 128:
                             (h // 2) * 512 + j * 128 + 128]
                    for kc in range(kc_hi + 1):
                        pqk = pQK[kc % 3][:, 0:512]
                        psp = pSP[1 + kc % 2][:, 0:512]
                        rk = K_ro[base:base + 64,
                                  kvh * 2048 + kc * 512: kvh * 2048 + kc * 512 + 512]
                        p.mmg([(pqk, lq, rk)])
                        p.mmg([(psp, q_sp[0:16, j * 128:j * 128 + 128],
                                k_sp[0:16, kvh * 2048 + kc * 512: kvh * 2048 + kc * 512 + 512])])
                        tb = tnh[kc % 2]
                        _act(p, tb[:], psp, AF.Tanh, scale=0.125)
                        ac = asbc[kc % 3][:]
                        _vop(p, "scalar_tensor_tensor", ac, [tb[:], 1.0, pqk],
                             op0=ALU.add, op1=ALU.mult)
                        if kc == kc_hi:
                            co = 2047 - (qb - kc * 512)
                            _vop(p, "tensor_add", ac, [ac, c.strip[:, co:co + 512]])
                        col = nmc[:, r * 4 + kc: r * 4 + kc + 1]
                        cols = nmsc[:, r * 4 + kc: r * 4 + kc + 1]
                        _vop(p, "reduce_max", col, [ac], axis=AX.X, negate=True)
                        _vop(p, "tensor_scalar_mul", cols, [col, 1.0 / 16.0])
                        _act(p, probs[r][:, kc * 512:(kc + 1) * 512], ac, AF.Exp,
                             bias=cols, scale=1.0 / 16.0,
                             accum=zc[:, r * 4 + kc: r * 4 + kc + 1])

                def S2(i):
                    h, j, qb, kc_hi = geom(i)
                    r = i % 3
                    nkc = kc_hi + 1
                    nmv = nm1[:, r:r + 1]
                    zzv = zz[:, r:r + 1]
                    fv = fvec[:, r * 4: r * 4 + nkc]
                    _vop(p, "tensor_reduce", nmv, [nmc[:, r * 4: r * 4 + nkc]],
                         axis=AX.X, op=ALU.min)
                    _vop(p, "tensor_scalar", fv, [nmc[:, r * 4: r * 4 + nkc], nmv, None],
                         op0=ALU.subtract)
                    _act(p, fv, fv, AF.Exp, scale=-1.0 / 16.0)
                    _vop(p, "tensor_mul", zc[:, r * 4: r * 4 + nkc],
                         [zc[:, r * 4: r * 4 + nkc], fv])
                    _vop(p, "reduce_sum", zzv, [zc[:, r * 4: r * 4 + nkc]], axis=AX.X)
                    _vop(p, "reciprocal", zzv, [zzv])
                    _vop(p, "tensor_scalar_mul", fv, [fv, zzv])
                    for kc in range(nkc):
                        sl = probs[r][:, kc * 512:(kc + 1) * 512]
                        _vop(p, "tensor_scalar_mul", sl, [sl, fv[:, kc:kc + 1]])
                        p.dma("sp", probsT[r][:, kc * 512:(kc + 1) * 512].rearrange(
                            "p (kt q) -> p kt q", q=128), sl, transpose=True)

                def AV(i):
                    h, j, qb, kc_hi = geom(i)
                    r = i % 3
                    kvh = h // 4
                    base = (h % 2) * 64
                    pav = pAV[i % 2][0:64, 0:128]
                    terms = []
                    for kt in range((kc_hi + 1) * 4):
                        terms.append((pav,
                                      V_tm[:, kt * 256 + kvh * 64: kt * 256 + kvh * 64 + 64],
                                      probsT[r][:, kt * 128: kt * 128 + 128]))
                    p.mmg(terms)
                    dst = afm[base:base + 64,
                              (h // 2) * 512 + j * 128:(h // 2) * 512 + j * 128 + 128]
                    _vop(p, "tensor_copy", dst, [pav])

                for i in range(66):
                    if i < 64:
                        S1(i)
                    if 1 <= i <= 64:
                        S2(i - 1)
                    if i >= 2:
                        AV(i - 2)
                _attn_out_proj(c, es2, b, qch, afm, s_o, bo2_sb, xdst_t,
                               pb_t, xr, xsrc_t)


def _stage_l1f(c, b, xsrc_t, xdst_t):
    p = c.p
    E, CAP = 16, 256
    with p.phase() as eso:
        acc = _sb(c, eso, _nm("acc"), 128, 16 * 1024, F32)
        stats = _sb(c, eso, _nm("stats"), 128, 2 * NT, F32)
        rstd = _sb(c, eso, _nm("rstd"), 128, NT, F32)
        valsT = _sb(c, eso, _nm("valsT"), 128, 2 * 16, F32)
        idxT = _sb(c, eso, _nm("idxT"), 128, 2 * 16, F32)
        with p.phase() as es1:
            xn_fm = _sb(c, es1, _nm("xnfm"), 128, 8 * 2048, BF16)
            st12 = _sb(c, es1, _nm("st12"), 128, 12, F32)
            xw = [_sb(c, es1, _nm("xw"), 128, 1024, F32) for _ in range(2)]
            xnf = [_sb(c, es1, _nm("xnf"), 128, 1024, F32) for _ in range(2)]
            gw_slab = _sb(c, es1, _nm("gws"), 128, 8 * 16, BF16)
            _wslab_load(c, gw_slab, c.dram["l1f_gwT"], 1024, 16)
            S_eb = _sb(c, es1, _nm("Seb"), 16, 2048, F32)
            prb = _sb(c, es1, _nm("prb"), 128, 16, F32)
            sctm = _sb(c, es1, _nm("sctm"), 128, 16, F32)
            nmz = _sb(c, es1, _nm("nmz"), 128, 2, F32)
            w0 = _sb(c, es1, _nm("w0"), 16, 2048, F32)
            w1 = _sb(c, es1, _nm("w1"), 16, 2048, F32)
            vals = _sb(c, es1, _nm("vals"), 16, 256, F32)
            idxu = _sb(c, es1, _nm("idxu"), 16, 256, U32)
            idxf = _sb(c, es1, _nm("idxf"), 16, 256, F32)
            pT = [_ps(c, es1, _nm("pT"), 128, 512) for _ in range(2)]
            pR = [_ps(c, es1, _nm("pR"), 128, 512) for _ in range(2)]
            _ln_tiles(c, es1, b, xsrc_t, xn_fm, stats, rstd, range(NT), pT,
                      xw, xnf, xn_dram_t=c.xnd_t, st12=st12)
            for tt in range(NT):
                pb = pR[tt % 2]
                terms = [(pb[:, 0:16],
                          xn_fm[:, ic * 2048 + tt * 128: ic * 2048 + tt * 128 + 128],
                          gw_slab[:, ic * 16: ic * 16 + 16]) for ic in range(8)]
                p.mmg(terms)
                _vop(p, "reduce_max", nmz[:, 0:1], [pb[:, 0:16]], axis=AX.X, negate=True)
                _act(p, prb[:], pb[:, 0:16], AF.Exp, bias=nmz[:, 0:1],
                     accum=nmz[:, 1:2])
                _vop(p, "reciprocal", nmz[:, 1:2], [nmz[:, 1:2]])
                _vop(p, "tensor_scalar_mul", sctm[:], [prb[:], nmz[:, 1:2]])
                p.ex("pe", lambda e, o=pb[0:16, 16:144], i=sctm[:], idn=c.ident[:]:
                     e.transpose(o, i, idn), reads=[sctm[:], c.ident[:]],
                     writes=[pb])
                _vop(p, "tensor_copy", S_eb[0:16, tt * 128:(tt + 1) * 128],
                     [pb[0:16, 16:144]])
            _vop(p, "tensor_copy", w0[:], [S_eb[:]])
            for r in range(32):
                src = w0 if r % 2 == 0 else w1
                dst = w1 if r % 2 == 0 else w0
                vs = vals[0:16, r * 8:(r + 1) * 8]
                _vop(p, "max", vs, [src[:]])
                _vop(p, "max_index", idxu[0:16, r * 8:(r + 1) * 8], [vs, src[:]])
                _vop(p, "match_replace", dst[:], [vs, src[:]], imm_value=-1.0)
            _vop(p, "tensor_copy", idxf[:], [idxu[:]])
            for ct in range(2):
                p.ex("pe", lambda e, o=pR[0][:, 0:16],
                     i=vals[0:16, ct * 128:(ct + 1) * 128], idn=c.ident[0:16, 0:16]:
                     e.transpose(o, i, idn), reads=[vals[:], c.ident[:]],
                     writes=[pR[0]])
                _vop(p, "tensor_copy", valsT[:, ct * 16:(ct + 1) * 16],
                     [pR[0][:, 0:16]])
                p.ex("pe", lambda e, o=pR[1][:, 0:16],
                     i=idxf[0:16, ct * 128:(ct + 1) * 128], idn=c.ident[0:16, 0:16]:
                     e.transpose(o, i, idn), reads=[idxf[:], c.ident[:]],
                     writes=[pR[1]])
                _vop(p, "tensor_copy", idxT[:, ct * 16:(ct + 1) * 16],
                     [pR[1][:, 0:16]])
        if "shared" not in _L1F_PARTS:
            with p.phase() as esz:
                zcl = _sb(c, esz, _nm("zcl"), 128, 1024, F32)
                _vop(p, "memset", zcl[:], [0.0])
                for tt in range(NT):
                    _vop(p, "tensor_copy", acc[:, tt * 1024:(tt + 1) * 1024], [zcl[:]])
        if "shared" in _L1F_PARTS:
         with p.phase() as es2:
            xnq2 = _sb(c, es2, _nm("xnq2"), 128, 8 * 512, BF16)
            wsd = _sb(c, es2, _nm("wsd"), 128, 16 * 1024, BF16)
            wsg = _sb(c, es2, _nm("wsg"), 128, 8 * 2048, BF16)
            wsu = _sb(c, es2, _nm("wsu"), 128, 8 * 2048, BF16)
            h_sh = _sb(c, es2, _nm("hsh"), 128, 16 * 512, BF16)
            gtmp = _sb(c, es2, _nm("gtmp"), 128, 512, BF16)
            xw2 = [_sb(c, es2, _nm("xw2"), 128, 1024, F32)]
            _wslab_load(c, wsd, c.dram["l1f_wsd"], 2048, 1024)
            _wslab_load(c, wsg, c.dram["l1f_wsg"], 1024, 2048)
            _wslab_load(c, wsu, c.dram["l1f_wsu"], 1024, 2048)
            pG = [_ps(c, es2, _nm("pG"), 128, 512) for _ in range(2)]
            pU = [_ps(c, es2, _nm("pU"), 128, 512) for _ in range(2)]
            pDN = [_ps(c, es2, _nm("pDN"), 128, 512) for _ in range(2)]
            for tch in range(4):
                for j in range(4):
                    tt = tch * 4 + j
                    w = xw2[0]
                    p.dma("sp", w[:], c.xnd_t[b * NT + tt])
                    for half in range(2):
                        pb = pG[half]
                        for q in range(4):
                            dc = half * 4 + q
                            p.ex("pe", lambda e, o=pb[:, q * 128:(q + 1) * 128],
                                 i=w[:, dc * 128:(dc + 1) * 128], idn=c.ident[:]:
                                 e.transpose(o, i, idn),
                                 reads=[w[:], c.ident[:]], writes=[pb])
                        dst = xnq2[:].rearrange("p (dc t) -> p dc t", t=512)[
                            :, half * 4:half * 4 + 4, j * 128:(j + 1) * 128]
                        src = pb[:].rearrange("p (dc t) -> p dc t", t=128)
                        _vop(p, "tensor_copy", dst, [src])
                for fc in range(16):
                    pg = pG[fc % 2][:, 0:512]
                    pu = pU[fc % 2][:, 0:512]
                    terms = [(pg, wsg[:, ic * 2048 + fc * 128: ic * 2048 + fc * 128 + 128],
                              xnq2[:, ic * 512: ic * 512 + 512]) for ic in range(8)]
                    p.mmg(terms)
                    terms = [(pu, wsu[:, ic * 2048 + fc * 128: ic * 2048 + fc * 128 + 128],
                              xnq2[:, ic * 512: ic * 512 + 512]) for ic in range(8)]
                    p.mmg(terms)
                    _act(p, gtmp[:], pg, AF.Gelu)
                    _vop(p, "tensor_tensor", h_sh[:, fc * 512:(fc + 1) * 512],
                         [gtmp[:], pu], op=ALU.mult)
                for j in range(4):
                    tt = tch * 4 + j
                    for dh in range(2):
                        pb = pDN[(j * 2 + dh) % 2][:, 0:512]
                        terms = [(pb,
                                  h_sh[:, fc2 * 512 + j * 128: fc2 * 512 + j * 128 + 128],
                                  wsd[:, fc2 * 1024 + dh * 512: fc2 * 1024 + dh * 512 + 512])
                                 for fc2 in range(16)]
                        p.mmg(terms)
                        _vop(p, "tensor_copy",
                             acc[:, tt * 1024 + dh * 512: tt * 1024 + dh * 512 + 512],
                             [pb])
        if "experts" in _L1F_PARTS:
         with p.phase() as es3:
            wg_h = [_sb(c, es3, _nm("wgh"), 128, 8 * 1024, BF16) for _ in range(2)]
            wu_h = [_sb(c, es3, _nm("wuh"), 128, 8 * 1024, BF16) for _ in range(2)]
            wd_h = [_sb(c, es3, _nm("wdh"), 128, 8 * 1024, BF16) for _ in range(2)]
            xtk = [_sb(c, es3, _nm("xtk"), 128, 2 * 1024, F32)]
            xtf = _sb(c, es3, _nm("xtf"), 128, 8 * 256, BF16)
            h_fm = _sb(c, es3, _nm("hfm"), 128, 16 * 256, BF16)
            gtmp = _sb(c, es3, _nm("gtmp2"), 128, 256, BF16)
            eo = [_sb(c, es3, _nm("eo"), 128, 2 * 1024, BF16) for _ in range(2)]
            Ptt = _sb(c, es3, _nm("Ptt"), 128, 4 * 128, BF16)
            iu32 = [_sb(c, es3, _nm("iu32"), 128, 2, U32) for _ in range(2)]
            pG = [_ps(c, es3, _nm("pG3"), 128, 512) for _ in range(2)]
            pU = [_ps(c, es3, _nm("pU3"), 128, 512) for _ in range(2)]
            pDN = [_ps(c, es3, _nm("pDN3"), 128, 512) for _ in range(2)]
            pSC = [_ps(c, es3, _nm("pSC"), 128, 512) for _ in range(2)]
            xnd_rows = c.xnd

            def gather(e):
                for ct in range(2):
                    _vop(p, "tensor_scalar_add", iu32[e % 2][:, ct:ct + 1],
                         [idxT[:, ct * 16 + e: ct * 16 + e + 1], float(b * S)])
                    p.dma_gather(xtk[0][:, ct * 1024:(ct + 1) * 1024], xnd_rows,
                                 iu32[e % 2][:, ct:ct + 1])

            def wload(e, fh):
                _wslab_load(c, wg_h[fh], c.dram["l1f_wg"][e][:, fh * 1024:(fh + 1) * 1024],
                            1024, 1024)
                _wslab_load(c, wu_h[fh], c.dram["l1f_wu"][e][:, fh * 1024:(fh + 1) * 1024],
                            1024, 1024)
                _wslab_load(c, wd_h[fh],
                            c.dram["l1f_wd"][e].rearrange("(fh x) d -> fh x d", fh=2)[fh],
                            1024, 1024)

            gather(0)
            wload(0, 0)
            for rnd in range(8):
                for er in range(2):
                    e = rnd * 2 + er
                    for ct in range(2):
                        for half in range(2):
                            pb = pG[half]
                            for q in range(4):
                                dc = half * 4 + q
                                p.ex("pe", lambda e2, o=pb[:, q * 128:(q + 1) * 128],
                                     i=xtk[0][:, ct * 1024 + dc * 128: ct * 1024 + dc * 128 + 128],
                                     idn=c.ident[:]: e2.transpose(o, i, idn),
                                     reads=[xtk[0][:], c.ident[:]], writes=[pb])
                            dst = xtf[:].rearrange("p (dc t) -> p dc t", t=256)[
                                :, half * 4:half * 4 + 4, ct * 128:(ct + 1) * 128]
                            src = pb[:].rearrange("p (dc t) -> p dc t", t=128)
                            _vop(p, "tensor_copy", dst, [src])
                    if e + 1 < 16:
                        gather(e + 1)
                    for fh in range(2):
                        if fh == 1:
                            wload(e, 1)
                        for fcl in range(8):
                            fc = fh * 8 + fcl
                            pg = pG[fcl % 2][:, 0:256]
                            pu = pU[fcl % 2][:, 0:256]
                            terms = [(pg,
                                      wg_h[fh][:, ic * 1024 + fcl * 128: ic * 1024 + fcl * 128 + 128],
                                      xtf[:, ic * 256: ic * 256 + 256]) for ic in range(8)]
                            p.mmg(terms)
                            terms = [(pu,
                                      wu_h[fh][:, ic * 1024 + fcl * 128: ic * 1024 + fcl * 128 + 128],
                                      xtf[:, ic * 256: ic * 256 + 256]) for ic in range(8)]
                            p.mmg(terms)
                            _act(p, gtmp[:], pg, AF.Gelu)
                            _vop(p, "tensor_tensor", h_fm[:, fc * 256:(fc + 1) * 256],
                                 [gtmp[:], pu], op=ALU.mult)
                    for ct in range(2):
                        for dh in range(2):
                            pb = pDN[(ct * 2 + dh) % 2][:, 0:512]
                            terms = [(pb,
                                      h_fm[:, fc2 * 256 + ct * 128: fc2 * 256 + ct * 128 + 128],
                                      wd_h[fc2 // 8][:, (fc2 % 8) * 1024 + dh * 512:
                                                     (fc2 % 8) * 1024 + dh * 512 + 512])
                                     for fc2 in range(16)]
                            p.mmg(terms)
                            _vop(p, "tensor_copy",
                                 eo[er][:, ct * 1024 + dh * 512: ct * 1024 + dh * 512 + 512],
                                 [pb])
                    if e + 1 < 16:
                        wload(e + 1, 0)
                for tt in range(NT):
                    for k in range(4):
                        er2, ct = k // 2, k % 2
                        e2x = rnd * 2 + er2
                        _vop(p, "tensor_scalar", Ptt[:, k * 128:(k + 1) * 128],
                             [c.iota_t[:, tt * 128:(tt + 1) * 128],
                              idxT[:, ct * 16 + e2x: ct * 16 + e2x + 1],
                              valsT[:, ct * 16 + e2x: ct * 16 + e2x + 1]],
                             op0=ALU.is_equal, op1=ALU.mult)
                    for dh in range(2):
                        pb = pSC[dh][:, 0:512]
                        terms = [(pb, Ptt[:, k * 128:(k + 1) * 128],
                                  eo[k // 2][:, (k % 2) * 1024 + dh * 512:
                                             (k % 2) * 1024 + dh * 512 + 512])
                                 for k in range(4)]
                        p.mmg(terms)
                        dst = acc[:, tt * 1024 + dh * 512: tt * 1024 + dh * 512 + 512]
                        _vop(p, "tensor_add", dst, [dst, pb])
        with p.phase() as es4:
            xr = [_sb(c, es4, _nm("xr"), 128, 1024, F32) for _ in range(2)]
            for tt in range(NT):
                xt = xr[tt % 2]
                p.dma("sp", xt[:], xsrc_t[b * NT + tt])
                _vop(p, "tensor_add", xt[:], [xt[:], acc[:, tt * 1024:(tt + 1) * 1024]])
                p.dma("act", xdst_t[b * NT + tt], xt[:])



def _build(arrs):
    nc = bass.Bass("TRN2", target_bir_lowering=False)
    es = ExitStack()
    c = _B()
    c.nc = nc
    c.dram = {}
    for k, a in arrs.items():
        dt = F32
        c.dram[k] = nc.dram_tensor(k, list(a.shape), dt, kind="ExternalInput").ap()
    x0 = nc.dram_tensor("x0", [BPC, S, D], F32, kind="ExternalInput").ap()
    out = nc.dram_tensor("out", [BPC, S, D], F32, kind="ExternalOutput").ap()
    xs1 = nc.dram_tensor("xs1", [BPC, S, D], F32, kind="Internal").ap()
    xs2 = nc.dram_tensor("xs2", [BPC, S, D], F32, kind="Internal").ap()
    xnd = nc.dram_tensor("xnd", [BPC * S, D], F32, kind="Internal").ap()
    tile = lambda ap: ap.rearrange("b (tt p) d -> (b tt) p d", p=128)
    x0_t, out_t, xs1_t, xs2_t = tile(x0), tile(out), tile(xs1), tile(xs2)

    p = c.p = Prog(nc, es)
    c.ident = _sb(c, es, "ident", 128, 128, F32)
    p.dma("sp", c.ident[:], c.dram["c_ident"])
    c.ones2 = _sb(c, es, "ones2", 2, 128, BF16)
    p.dma("pool", c.ones2[:], c.dram["c_ones2"])
    c.e16 = _sb(c, es, "e16", 16, 16 * 128, BF16)
    p.dma("pool", c.e16[:], c.dram["c_e16"])
    c.konst = _sb(c, es, "konst", 128, 4, F32)
    c.iota_t = _sb(c, es, "iotat", 128, 2048, F32)
    p.ex("pool", lambda e: e.iota(c.iota_t[:], [[1, 2048]], base=0,
                                  channel_multiplier=0,
                                  allow_small_or_imprecise_dtypes=True),
         writes=[c.iota_t[:]])
    _vop(p, "memset", c.konst[:, 0:1], [EPS])
    _vop(p, "memset", c.konst[:, 1:2], [0.0])
    c.xnd_t = xnd.rearrange("(n p) d -> n p d", p=128)
    c.xnd = xnd

    stages = sorted(_STAGES)
    last = stages[-1]

    def src_dst(i):
        srcs = x0_t if i == stages[0] else (xs1_t if i % 2 == 1 else xs2_t)
        dsts = out_t if i == last else (xs1_t if i % 2 == 0 else xs2_t)
        return srcs, dsts

    for b in range(BPC):
        for i in stages:
            xsrc_t, xdst_t = src_dst(i)
            if i == 0:
                _stage_attn_std(c, b, "l0a", xsrc_t, xdst_t,
                                "l0a_wkT", "l0a_wvT", "l0a_wqT", "l0a_woT",
                                "l0a_bq", "l0a_bk", "l0a_bv2", "l0a_bo2")
            elif i == 2:
                _stage_attn_std(c, b, "l0b", xsrc_t, xdst_t,
                                "l0b_wkT", "l0b_wvT", "l0b_wqT", "l0b_woT",
                                "l0b_bq", "l0b_bk", "l0b_bv2", "l0b_bo2")
            elif i == 4:
                _stage_l1a(c, b, xsrc_t, xdst_t)
            elif i in (1, 3):
                _stage_l0f(c, b, xsrc_t, xdst_t)
            elif i == 6:
                _stage_l1b(c, b, xsrc_t, xdst_t)
            else:
                _stage_l1f(c, b, xsrc_t, xdst_t)
    p.replay(es)
    c._es = es
    return nc, p


def kernel(**inputs):
    consts = _prep_consts()
    w = _prep_weights({k: np.asarray(v, dtype=np.float32) for k, v in inputs.items()
                       if k != "L0_attn_a_x"})
    x = np.asarray(inputs["L0_attn_a_x"], dtype=np.float32)
    common = {}
    common.update(consts)
    common.update(w)
    nc, p = _build(common)
    in_maps = []
    for cid in range(NCORES):
        m = dict(common)
        m["x0"] = np.ascontiguousarray(x[cid * BPC:(cid + 1) * BPC])
        in_maps.append(m)
    res = run_bass_kernel_spmd(nc, in_maps, core_ids=list(range(NCORES)))
    out = np.concatenate([np.asarray(r["out"], dtype=np.float32)
                          for r in res.results], axis=0)
    return out


# version 05 (final)
# speedup vs baseline: 1.4260x; 1.1693x over previous
import numpy as np
import ml_dtypes
import concourse.bass as bass
import concourse.mybir as mybir
from concourse.bass_utils import run_bass_kernel_spmd
from concourse.bass_core_rust import IndirectOffsetOnAxis
from contextlib import ExitStack, contextmanager

F32 = mybir.dt.float32
BF16 = mybir.dt.bfloat16
U32 = mybir.dt.uint32
AF = mybir.ActivationFunctionType
ALU = mybir.AluOpType
AX = mybir.AxisListType
BF = ml_dtypes.bfloat16

NCORES = 8
BPC = 2
S = 2048
D = 1024
NT = S // 128
NEG = -1.0e37
EPS = 1e-5
_STAGES = list(range(8))
_L1F_PARTS = ("shared", "experts")

ENGS = ("pe", "dve", "act", "pool", "sp")
_ENG_ATTR = {"pe": "tensor", "dve": "vector", "act": "scalar", "pool": "gpsimd", "sp": "sync"}
_DMA_NSEM = {"sp": 10, "act": 6, "pool": 10}


class _Buf:
    __slots__ = ("w", "readers")
    def __init__(self):
        self.w = None
        self.readers = []


class Prog:

    def __init__(self, nc, es):
        self.nc = nc
        self.semh = {}
        self.cnt = {}
        for ek in ENGS:
            self.semh[ek] = es.enter_context(nc.semaphore(f"S{ek}"))
            self.cnt[ek] = 0
        self.dmasems = {}
        self.dmarr = {}
        for qk, n in _DMA_NSEM.items():
            ring = []
            for i in range(n):
                sk = f"D{qk}{i}"
                self.semh[sk] = es.enter_context(nc.semaphore(sk))
                self.cnt[sk] = 0
                ring.append(sk)
            self.dmasems[qk] = ring
            self.dmarr[qk] = 0
        self.queue = {ek: [] for ek in ENGS}
        self.hw = {ek: {} for ek in ENGS}
        self.bufs = {}
        self.nisn = 0

    def _key(self, x):
        if isinstance(x, str):
            return x
        t = getattr(x, "tensor", None)
        return t.name if t is not None else x.name

    def _buf(self, x):
        k = self._key(x)
        b = self.bufs.get(k)
        if b is None:
            b = self.bufs[k] = _Buf()
        return b

    def _deps_for(self, reads, writes):
        deps = []
        for x in reads:
            b = self._buf(x)
            if b.w:
                deps.append(b.w)
        for x in writes:
            b = self._buf(x)
            if b.w:
                deps.append(b.w)
            deps.extend(b.readers)
        return deps

    def _dedup(self, ek, deps):
        hw = self.hw[ek]
        out = []
        for (sk, v) in deps:
            if hw.get(sk, 0) < v:
                hw[sk] = v
                out.append((sk, v))
        return out

    def _register(self, dep, reads, writes):
        for x in reads:
            self._buf(x).readers.append(dep)
        for x in writes:
            b = self._buf(x)
            b.w = dep
            b.readers = []

    def ex(self, ek, build, reads=(), writes=()):
        wl = self._dedup(ek, self._deps_for(reads, writes))
        self.cnt[ek] += 1
        dep = (ek, self.cnt[ek])
        semh, myh = self.semh, self.semh[ek]

        def item(eng, build=build, wl=wl):
            for sk, v in wl:
                eng.wait_ge(semh[sk], v)
            build(eng).then_inc(myh, 1)
        self.queue[ek].append(item)
        self.nisn += 1 + len(wl)
        self._register(dep, reads, writes)
        return dep

    def mmg(self, terms, reads=(), writes=(), start=True, stop=True):
        psum_ap = terms[0][0]
        opkeys = set()
        for (pap, lap, rap) in terms:
            opkeys.add(self._key(lap))
            opkeys.add(self._key(rap))
        opkeys.update(self._key(x) for x in reads)
        wkeys = {self._key(psum_ap)}
        wkeys.update(self._key(x) for x in writes)
        wl = self._dedup("pe", self._deps_for(opkeys, wkeys))
        self.cnt["pe"] += 1
        dep = ("pe", self.cnt["pe"])
        semh, myh = self.semh, self.semh["pe"]
        n = len(terms)

        def item(eng, terms=terms, wl=wl, n=n, start=start, stop=stop):
            for sk, v in wl:
                eng.wait_ge(semh[sk], v)
            for i, (pap, lap, rap) in enumerate(terms):
                inst = eng.matmul(pap, lap, rap, start=(start and i == 0),
                                  stop=(stop and i == n - 1))
            inst.then_inc(myh, 1)
        self.queue["pe"].append(item)
        self.nisn += n + len(wl)
        self._register(dep, opkeys, wkeys)
        return dep

    def _dma_dep(self, qk):
        ring = self.dmasems[qk]
        sk = ring[self.dmarr[qk] % len(ring)]
        self.dmarr[qk] += 1
        self.cnt[sk] += 16
        return sk, self.cnt[sk]

    def dma(self, qk, out_ap, in_ap, reads=(), writes=(), transpose=False):
        rk = [self._key(in_ap)] + [self._key(x) for x in reads]
        wk = [self._key(out_ap)] + [self._key(x) for x in writes]
        deps = self._deps_for(rk, wk)
        sk, v = self._dma_dep(qk)
        if v > 16:
            deps = deps + [(sk, v - 16)]
        wl = self._dedup(qk, deps)
        dep = (sk, v)
        semh = self.semh

        def item(eng, out_ap=out_ap, in_ap=in_ap, wl=wl, sk=sk, transpose=transpose):
            for k2, v2 in wl:
                eng.wait_ge(semh[k2], v2)
            if transpose:
                inst = eng.dma_start_transpose(out_ap, in_ap)
            else:
                inst = eng.dma_start(out_ap, in_ap)
            inst.then_inc(semh[sk], 16)
        self.queue[qk].append(item)
        self.nisn += 1 + len(wl)
        self._register(dep, rk, wk)
        return dep

    def dma_gather(self, out_ap, in_dram_ap, idx_ap, reads=(), writes=()):
        rk = [self._key(in_dram_ap), self._key(idx_ap)] + [self._key(x) for x in reads]
        wk = [self._key(out_ap)] + [self._key(x) for x in writes]
        deps = self._deps_for(rk, wk)
        sk, v = self._dma_dep("pool")
        if v > 16:
            deps = deps + [(sk, v - 16)]
        wl = self._dedup("pool", deps)
        dep = (sk, v)
        semh = self.semh

        def item(eng, out_ap=out_ap, in_dram_ap=in_dram_ap, idx_ap=idx_ap, wl=wl, sk=sk):
            for k2, v2 in wl:
                eng.wait_ge(semh[k2], v2)
            eng.indirect_dma_start(
                out_ap, None, in_dram_ap, IndirectOffsetOnAxis(idx_ap, 0)
            ).then_inc(semh[sk], 16)
        self.queue["pool"].append(item)
        self.nisn += 1 + len(wl)
        self._register(dep, rk, wk)
        return dep

    def barrier(self):
        allsems = list(self.cnt.keys())
        for ek in ENGS:
            wl = []
            hw = self.hw[ek]
            for sk in allsems:
                cur = self.cnt[sk]
                if cur > hw.get(sk, 0):
                    hw[sk] = cur
                    wl.append((sk, cur))
            if not wl:
                continue
            semh = self.semh

            def item(eng, wl=wl):
                for sk, v in wl:
                    eng.wait_ge(semh[sk], v)
            self.queue[ek].append(item)
            self.nisn += len(wl)
        self.bufs = {}

    @contextmanager
    def phase(self, name=""):
        es = ExitStack()
        try:
            yield es
        finally:
            self.barrier()
            es.close()

    def replay(self, es):
        blk = es.enter_context(self.nc.Block())
        for ek in ENGS:
            q = self.queue[ek]
            getattr(blk, _ENG_ATTR[ek])(lambda eng, q=q: [it(eng) for it in q])



def _hilo_rows(b):
    hi = b.astype(BF).astype(np.float32)
    lo = (b - hi).astype(BF).astype(np.float32)
    return np.ascontiguousarray(np.stack([hi, lo]))


def _col8(b):
    c = b.size // 128
    return np.ascontiguousarray(b.reshape(c, 128).T)


def _prep_consts():
    d = {}
    d["c_ident"] = np.eye(128, dtype=np.float32)
    cc = np.arange(2560)[None, :]
    ii = np.arange(128)[:, None]
    dd = ii + 2047 - cc
    alibi = np.where((dd < 0) | (dd >= 1024), NEG, -dd.astype(np.float64)).astype(np.float32)
    causal = np.where(dd < 0, NEG, 0.0).astype(np.float32)
    d["c_strip_alibi"] = np.ascontiguousarray(alibi)
    d["c_strip_causal"] = np.ascontiguousarray(causal)
    d["c_ones2"] = np.ones((2, 128), dtype=np.float32)
    e16 = np.zeros((16, 16 * 128), dtype=np.float32)
    for e in range(16):
        e16[e, e * 128:(e + 1) * 128] = 1.0
    d["c_e16"] = e16
    hd = 64
    theta = 1.0 / 10000.0 ** (np.arange(0, hd, 2, dtype=np.float32) / hd)
    t = np.arange(S, dtype=np.float32)
    fr = np.outer(theta, t)
    fr2 = np.concatenate([fr, fr], 0)
    cos = np.cos(fr2)
    sin = np.sin(fr2)
    sin_signed = sin.copy()
    sin_signed[:32] = -sin[:32]
    d["c_cos"] = np.ascontiguousarray(np.concatenate([cos, cos], 0).astype(np.float32))
    d["c_sin"] = np.ascontiguousarray(np.concatenate([sin_signed, sin_signed], 0).astype(np.float32))
    return d


def _rope_perm(nout):
    j = np.arange(nout)
    h = j // 64
    dd = j % 64
    return h * 64 + (dd + 32) % 64


def _prep_weights(inp):
    d = {}
    T = lambda a: np.ascontiguousarray(a.T)
    d["l0a_wqT"] = T(inp["L0_attn_a_wq"]); d["l0a_wkT"] = T(inp["L0_attn_a_wk"])
    d["l0a_wvT"] = T(inp["L0_attn_a_wv"]); d["l0a_woT"] = T(inp["L0_attn_a_wo"])
    d["l0a_bq"] = _col8(inp["L0_attn_a_bq"]); d["l0a_bk"] = _col8(inp["L0_attn_a_bk"])
    d["l0a_bv2"] = _hilo_rows(inp["L0_attn_a_bv"])
    d["l0a_bo2"] = _hilo_rows(inp["L0_attn_a_bo"])
    d["l0b_wqT"] = T(inp["L0_attn_b_Wq"]); d["l0b_wkT"] = T(inp["L0_attn_b_Wk"])
    d["l0b_wvT"] = T(inp["L0_attn_b_Wv"]); d["l0b_woT"] = T(inp["L0_attn_b_Wo"])
    d["l0b_bq"] = _col8(inp["L0_attn_b_bq"]); d["l0b_bk"] = _col8(inp["L0_attn_b_bk"])
    d["l0b_bv2"] = _hilo_rows(inp["L0_attn_b_bv"])
    d["l0b_bo2"] = _hilo_rows(inp["L0_attn_b_bo"])
    d["l0f_rw"] = np.ascontiguousarray(inp["L0_ffn_router_w"])
    d["l0f_rb2"] = _hilo_rows(inp["L0_ffn_router_b"])
    d["l0f_w1"] = np.ascontiguousarray(inp["L0_ffn_w1"])
    d["l0f_w2"] = np.ascontiguousarray(inp["L0_ffn_w2"])
    d["l0f_b1"] = np.ascontiguousarray(
        np.stack([_col8(inp["L0_ffn_b1"][e]) for e in range(16)])
        .transpose(1, 0, 2).reshape(128, 256))
    d["l0f_b2"] = np.ascontiguousarray(inp["L0_ffn_b2"])
    for k in ("W_dq", "W_uq", "W_dkv", "W_uk", "W_uv", "W_o"):
        d["l1a_" + k] = np.ascontiguousarray(inp["L1_attn_a_" + k])
    wqT = T(inp["L1_attn_b_Wq"])
    wkT = T(inp["L1_attn_b_Wk"])
    d["l1b_wqT"] = wqT
    d["l1b_wqTr"] = np.ascontiguousarray(wqT[:, _rope_perm(1024)])
    def _dup(w):
        r = w.reshape(1024, 4, 1, 64)
        return np.ascontiguousarray(
            np.concatenate([r, r], axis=2).reshape(1024, 512))
    d["l1b_wkT"] = _dup(wkT)
    d["l1b_wkTr"] = _dup(np.ascontiguousarray(wkT[:, _rope_perm(256)]))
    d["l1b_wvT"] = T(inp["L1_attn_b_Wv"])
    d["l1b_woT"] = T(inp["L1_attn_b_Wo"])
    d["l1b_bo2"] = _hilo_rows(inp["L1_attn_b_bo"])
    wsT = T(inp["L1_attn_b_Ws"])
    d["l1b_wsT"] = np.ascontiguousarray(np.concatenate([wsT, wsT], 0))
    d["l1b_bs"] = np.ascontiguousarray(inp["L1_attn_b_bs"].reshape(16, 1))
    d["l1f_gwT"] = T(inp["L1_ffn_gate_w"])
    d["l1f_wg"] = np.ascontiguousarray(inp["L1_ffn_Wg"].transpose(0, 2, 1))
    d["l1f_wu"] = np.ascontiguousarray(inp["L1_ffn_Wu"].transpose(0, 2, 1))
    d["l1f_wd"] = np.ascontiguousarray(inp["L1_ffn_Wd"].transpose(0, 2, 1))
    d["l1f_wsg"] = T(inp["L1_ffn_Ws_g"])
    d["l1f_wsu"] = T(inp["L1_ffn_Ws_u"])
    d["l1f_wsd"] = T(inp["L1_ffn_Ws_d"])
    return d



def _k(x):
    return x if isinstance(x, str) else x.tensor.name


def _apify(items):
    out = []
    for x in items:
        if isinstance(x, str) or hasattr(x, "tensor"):
            out.append(x)
    return out


class _B:
    pass


def _sb(c, es, name, rows, cols, dtype):
    return es.enter_context(c.nc.sbuf_tensor(name, [rows, cols], dtype))


def _ps(c, es, name, rows, cols):
    return es.enter_context(c.nc.psum_tensor(name, [rows, cols], F32))


_UNIQ = [0]
def _nm(base):
    _UNIQ[0] += 1
    return f"{base}_{_UNIQ[0]}"


def _vop(p, meth, out_ap, ins, reads=(), writes=(), eng="dve", **kw):
    def build(e, meth=meth, out_ap=out_ap, ins=ins, kw=kw):
        return getattr(e, meth)(out_ap, *ins, **kw)
    return p.ex(eng, build,
                reads=_apify(list(ins)) + _apify(list(reads)) + _apify(list(kw.values())),
                writes=[out_ap] + _apify(list(writes)))


def _act(p, out_ap, in_ap, func, bias=0.0, scale=1.0, accum=None):
    def build(e, out_ap=out_ap, in_ap=in_ap, func=func, bias=bias, scale=scale, accum=accum):
        if accum is not None:
            return e.activation(out_ap, in_ap, func, bias=bias, scale=scale, accum_out=accum)
        return e.activation(out_ap, in_ap, func, bias=bias, scale=scale)
    writes = [out_ap] + ([accum] if accum is not None else [])
    return p.ex("act", build, reads=_apify([in_ap, bias, scale]), writes=writes)


def _sadd(p, out_ap, in_ap, addv):
    def build(e, out_ap=out_ap, in_ap=in_ap, addv=addv):
        return e.add(out_ap, in_ap, addv)
    return p.ex("act", build, reads=_apify([in_ap, addv]), writes=[out_ap])


def _wslab_load(c, slab, wT_dram, din, dout):
    src = wT_dram.rearrange("(ic p) o -> p ic o", p=128)
    dst = slab[:].rearrange("p (ic o) -> p ic o", o=dout)
    return c.p.dma("pool", dst, src)


def _ln_tiles(c, es, b, xsrc_t, xn_fm, stats, rstd, tiles, pT, xw, xnf, xn_dram_t=None,
              st12=None):
    p, nc = c.p, c.nc
    for j, tt in enumerate(tiles):
        w = xw[j % 2]
        p.dma("sp", w[:], xsrc_t[b * NT + tt])
        _vop(p, "bn_stats", st12[:, 0:6], [w[:, 0:512]])
        _vop(p, "bn_stats", st12[:, 6:12], [w[:, 512:1024]])
        _vop(p, "bn_aggr", stats[:, 2 * tt:2 * tt + 2], [st12[:]])
        _act(p, rstd[:, tt:tt + 1], stats[:, 2 * tt + 1:2 * tt + 2], AF.Ln,
             bias=c.konst[:, 0:1])
        _act(p, rstd[:, tt:tt + 1], rstd[:, tt:tt + 1], AF.Exp, scale=-0.5)
        xf = xnf[j % 2]
        _vop(p, "tensor_scalar", xf[:], [w[:], stats[:, 2 * tt:2 * tt + 1],
                                         rstd[:, tt:tt + 1]],
             op0=ALU.subtract, op1=ALU.mult)
        if xn_dram_t is not None:
            p.dma("pool", xn_dram_t[b * NT + tt], xf[:])
        for half in range(2):
            pb = pT[half]
            for q in range(4):
                dc = half * 4 + q
                p.ex("pe", lambda e, o=pb[:, q * 128:(q + 1) * 128],
                     i=xf[:, dc * 128:(dc + 1) * 128], idn=c.ident[:]:
                     e.transpose(o, i, idn),
                     reads=[xf[:], c.ident[:]], writes=[pb[:]])
            dst = xn_fm[:].rearrange("p (dc t) -> p dc t", t=S)[
                :, half * 4:half * 4 + 4, tt * 128:(tt + 1) * 128]
            src = pb[:].rearrange("p (dc t) -> p dc t", t=128)
            _vop(p, "tensor_copy", dst, [src])


def _proj_fm(c, out_fm, slab, din, dout, src_fm, src_w, bias_col, pbanks, tlo, thi,
             out_w=S):
    p = c.p
    nic, noc = din // 128, dout // 128
    nb = len(pbanks)
    i = 0
    for oc in range(noc):
        for t0 in range(tlo, thi, 512):
            pb = pbanks[i % nb]; i += 1
            terms = [(pb[:, 0:512],
                      slab[:, ic * dout + oc * 128: ic * dout + oc * 128 + 128],
                      src_fm[:, ic * src_w + t0: ic * src_w + t0 + 512])
                     for ic in range(nic)]
            p.mmg(terms)
            dst = out_fm[:, oc * out_w + (t0 - tlo): oc * out_w + (t0 - tlo) + 512]
            bias = bias_col[:, oc:oc + 1] if bias_col is not None else 0.0
            _sadd(p, dst, pb[:, 0:512], bias)


def _proj_tm(c, out_tm, slab, din, dout, src_fm, src_w, bias2, pbanks, tiles):
    p = c.p
    nic = din // 128
    nb = len(pbanks)
    i = 0
    for tt in tiles:
        for o0 in range(0, dout, 512):
            pb = pbanks[i % nb]; i += 1
            terms = []
            if bias2 is not None:
                terms.append((pb[:, 0:512], c.ones2[:, 0:128], bias2[:, o0:o0 + 512]))
            terms += [(pb[:, 0:512],
                       src_fm[:, ic * src_w + tt * 128: ic * src_w + tt * 128 + 128],
                       slab[:, ic * dout + o0: ic * dout + o0 + 512])
                      for ic in range(nic)]
            p.mmg(terms)
            _vop(p, "tensor_copy", out_tm[:, tt * dout + o0: tt * dout + o0 + 512],
                 [pb[:, 0:512]])


def _attn_out_proj(c, es, b, qch, attn_fm, wo_slab, bo2_sb, xdst_t, pbanks, xr,
                   resid_src_t):
    p = c.p
    for j in range(4):
        tt = qch * 4 + j
        xt = xr[j % 2]
        p.dma("sp", xt[:], resid_src_t[b * NT + tt])
        for oh in range(2):
            pb = pbanks[(j * 2 + oh) % len(pbanks)]
            terms = [(pb[:, 0:512], c.ones2[:, 0:128], bo2_sb[:, oh * 512:oh * 512 + 512])]
            terms += [(pb[:, 0:512],
                       attn_fm[:, dc * 512 + j * 128: dc * 512 + j * 128 + 128],
                       wo_slab[:, dc * 1024 + oh * 512: dc * 1024 + oh * 512 + 512])
                      for dc in range(8)]
            p.mmg(terms)
            _vop(p, "tensor_add", xt[:, oh * 512:oh * 512 + 512],
                 [xt[:, oh * 512:oh * 512 + 512], pb[:, 0:512]])
        p.dma("act", xdst_t[b * NT + tt], xt[:])


def _load_bias_rows2(c, es, name, dram, n):
    t = _sb(c, es, _nm(name), 2, n, BF16)
    c.p.dma("pool", t[:], dram)
    return t


def _load_bias_col(c, es, name, dram, ncol):
    t = _sb(c, es, _nm(name), 128, ncol, F32)
    c.p.dma("sp", t[:], dram)
    return t


def _attn_core(c, es, b, kind, K_fm, V_tm, wq_slab, wo_slab, bq_col, bo2_sb,
               resid_src_t, xdst_t, xn_fm, cq_fm=None):
    p, nc = c.p, c.nc
    scale = 0.125 if kind in ("l0a", "l1a") else 1.0 / 32.0
    KW = 1536 if kind == "l0a" else 2048
    R = 3
    NC_ = KW // 512
    Qf = _sb(c, es, _nm("Qf"), 128, 8 * 512, BF16)
    probs = [[_sb(c, es, _nm("probs"), 128, 512, BF16) for _ in range(NC_)]
             for _ in range(R)]
    probsT = [[_sb(c, es, _nm("probsT"), 128, 512, BF16) for _ in range(NC_)]
              for _ in range(R)]
    if kind == "l0a":
        asbc = [_sb(c, es, _nm("asbc"), 128, 512, F32) for _ in range(3)]
    pS = [_ps(c, es, _nm("pS"), 128, 512) for _ in range(6)]
    pAV = [_ps(c, es, _nm("pAV"), 128, 512) for _ in range(2)]
    pb_t = [pAV[0][:, 0:512], pAV[1][:, 0:512]]
    afm = _sb(c, es, _nm("afm"), 128, 8 * 512, BF16)
    xr = [_sb(c, es, _nm("xr"), 128, 1024, F32) for _ in range(2)]
    nmc = [_sb(c, es, _nm("nmc"), 128, 4, F32) for _ in range(R)]
    nmsc = [_sb(c, es, _nm("nmsc"), 128, 4, F32) for _ in range(R)]
    fvec = [_sb(c, es, _nm("fvec"), 128, 4, F32) for _ in range(R)]
    zc = [_sb(c, es, _nm("zc"), 128, 4, F32) for _ in range(R)]
    nm1 = [_sb(c, es, _nm("nm"), 128, 1, F32) for _ in range(R)]
    zz = [_sb(c, es, _nm("zz"), 128, 1, F32) for _ in range(R)]
    scnt = [0]

    def slot_geom(i):
        h, j = divmod(i, 4)
        return h, j

    for qch in range(4):
        if kind == "l1a":
            nic, src, srcw = 2, cq_fm, S
        else:
            nic, src, srcw = 8, xn_fm, S
        off = qch * 512
        for oc in range(8):
            pb = pb_t[oc % 2]
            terms = [(pb, wq_slab[:, ic * 1024 + oc * 128: ic * 1024 + oc * 128 + 128],
                      src[:, ic * srcw + off: ic * srcw + off + 512])
                     for ic in range(nic)]
            p.mmg(terms)
            bias = bq_col[:, oc:oc + 1] if bq_col is not None else 0.0
            _sadd(p, Qf[:, oc * 512:(oc + 1) * 512], pb, bias)

        def geom(i):
            h, j = slot_geom(i)
            qt = qch * 4 + j
            qb = qt * 128
            if kind == "l0a":
                kc_lo = max(0, qb - 1023) // 512
                kc_hi = (qb + 127) // 512
            elif kind == "l1a":
                kc_lo, kc_hi = 0, (qb + 127) // 512
            else:
                kc_lo, kc_hi = 0, 3
            return h, j, qb, kc_lo, kc_hi

        def S1(i):
            h, j, qb, kc_lo, kc_hi = geom(i)
            r = i % R
            lq = Qf[(h % 2) * 64:(h % 2) * 64 + 64,
                    (h // 2) * 512 + j * 128: (h // 2) * 512 + j * 128 + 128]
            for i2, kc in enumerate(range(kc_lo, kc_hi + 1)):
                pb = pS[scnt[0] % 6][:]
                scnt[0] += 1
                rk = K_fm[(h % 2) * 64:(h % 2) * 64 + 64,
                          (h // 2) * 2048 + kc * 512: (h // 2) * 2048 + kc * 512 + 512]
                p.mmg([(pb, lq, rk)])
                esrc = pb
                if kind == "l0a":
                    co = 2047 - (qb - kc * 512)
                    slope = (2.0 ** (-2.0 ** -1.0)) ** (h + 1)
                    ac = asbc[scnt[0] % 3][:]
                    _vop(p, "scalar_tensor_tensor", ac,
                         [c.strip[:, co:co + 512], 8.0 * slope, pb],
                         op0=ALU.mult, op1=ALU.add)
                    esrc = ac
                elif kind == "l1a" and kc == kc_hi:
                    co = 2047 - (qb - kc * 512)
                    _vop(p, "tensor_add", pb, [pb, c.strip[:, co:co + 512]])
                col = nmc[r][:, i2:i2 + 1]
                cols = nmsc[r][:, i2:i2 + 1]
                _vop(p, "reduce_max", col, [esrc], axis=AX.X, negate=True)
                _vop(p, "tensor_scalar_mul", cols, [col, scale])
                _act(p, probs[r][i2][:], esrc, AF.Exp,
                     bias=cols, scale=scale, accum=zc[r][:, i2:i2 + 1])

        def S2(i):
            h, j, qb, kc_lo, kc_hi = geom(i)
            r = i % R
            nkc = kc_hi - kc_lo + 1
            nmv = nm1[r][:]
            zzv = zz[r][:]
            fv = fvec[r][:, 0:nkc]
            _vop(p, "tensor_reduce", nmv, [nmc[r][:, 0:nkc]], axis=AX.X, op=ALU.min)
            _vop(p, "tensor_scalar", fv, [nmc[r][:, 0:nkc], nmv, None],
                 op0=ALU.subtract)
            _act(p, fv, fv, AF.Exp, scale=-scale)
            _vop(p, "tensor_mul", zc[r][:, 0:nkc], [zc[r][:, 0:nkc], fv])
            _vop(p, "reduce_sum", zzv, [zc[r][:, 0:nkc]], axis=AX.X)
            _vop(p, "reciprocal", zzv, [zzv])
            _vop(p, "tensor_scalar_mul", fv, [fv, zzv])
            for i2 in range(nkc):
                sl = probs[r][i2][:]
                _vop(p, "tensor_scalar_mul", sl, [sl, fv[:, i2:i2 + 1]])
                p.dma("sp", probsT[r][i2][:].rearrange("p (kt q) -> p kt q", q=128),
                      sl, transpose=True)

        def AV(i):
            h, j, qb, kc_lo, kc_hi = geom(i)
            r = i % R
            nkc = kc_hi - kc_lo + 1
            pav = pAV[i % 2][0:64, 0:128]
            terms = []
            for kt in range(nkc * 4):
                ktg = kc_lo * 4 + kt
                terms.append((pav,
                              V_tm[:, ktg * 1024 + h * 64: ktg * 1024 + h * 64 + 64],
                              probsT[r][kt // 4][:, (kt % 4) * 128: (kt % 4) * 128 + 128]))
            p.mmg(terms)
            dst = afm[(h % 2) * 64:(h % 2) * 64 + 64,
                      (h // 2) * 512 + j * 128: (h // 2) * 512 + j * 128 + 128]
            _vop(p, "tensor_copy", dst, [pav])

        n = 64
        for i in range(n + 2):
            if i < n:
                S1(i)
            if 1 <= i <= n:
                S2(i - 1)
            if i >= 2:
                AV(i - 2)
        _attn_out_proj(c, es, b, qch, afm, wo_slab, bo2_sb, xdst_t,
                       pb_t, xr, resid_src_t)


def _stage_attn_std(c, b, kind, xsrc_t, xdst_t, wk, wv, wq, wo, bq, bk, bv2, bo2):
    p = c.p
    with p.phase() as eso:
        if kind == "l0a":
            c.strip = _sb(c, eso, _nm("strip"), 128, 2560, F32)
            p.dma("sp", c.strip[:], c.dram["c_strip_alibi"])
        slab_a = _sb(c, eso, _nm("slabA"), 128, 8 * 1024, BF16)
        slab_b = _sb(c, eso, _nm("slabB"), 128, 8 * 1024, BF16)
        K_fm = _sb(c, eso, _nm("Kfm"), 128, 8 * 2048, BF16)
        V_tm = _sb(c, eso, _nm("Vtm"), 128, 16 * 1024, BF16)
        stats = _sb(c, eso, _nm("stats"), 128, 2 * NT, F32)
        rstd = _sb(c, eso, _nm("rstd"), 128, NT, F32)
        bq_col = _load_bias_col(c, eso, "bq", c.dram[bq], 8)
        bk_col = _load_bias_col(c, eso, "bk", c.dram[bk], 8)
        bv2_sb = _load_bias_rows2(c, eso, "bv2", c.dram[bv2], 1024)
        bo2_sb = _load_bias_rows2(c, eso, "bo2", c.dram[bo2], 1024)
        xn_fm = _sb(c, eso, _nm("xnfm"), 128, 8 * 2048, BF16)
        _wslab_load(c, slab_a, c.dram[wk], 1024, 1024)
        _wslab_load(c, slab_b, c.dram[wv], 1024, 1024)
        with p.phase() as es1:
            st12 = _sb(c, es1, _nm("st12"), 128, 12, F32)
            xw = [_sb(c, es1, _nm("xw"), 128, 1024, F32) for _ in range(2)]
            xnf = [_sb(c, es1, _nm("xnf"), 128, 1024, F32) for _ in range(2)]
            pT = [_ps(c, es1, _nm("pT"), 128, 512) for _ in range(2)]
            pK = [_ps(c, es1, _nm("pK"), 128, 512) for _ in range(2)]
            pV = [_ps(c, es1, _nm("pV"), 128, 512) for _ in range(2)]
            _ln_tiles(c, es1, b, xsrc_t, xn_fm, stats, rstd, range(NT), pT,
                      xw, xnf, st12=st12)
            _proj_fm(c, K_fm, slab_a, 1024, 1024, xn_fm, S, bk_col, pK, 0, S)
            _proj_tm(c, V_tm, slab_b, 1024, 1024, xn_fm, S, bv2_sb, pV, range(NT))
        _wslab_load(c, slab_a, c.dram[wq], 1024, 1024)
        _wslab_load(c, slab_b, c.dram[wo], 1024, 1024)
        with p.phase() as es2:
            _attn_core(c, es2, b, kind, K_fm, V_tm, slab_a, slab_b, bq_col,
                       bo2_sb, xsrc_t, xdst_t, xn_fm)


def _stage_l1a(c, b, xsrc_t, xdst_t):
    p = c.p
    with p.phase() as eso:
        c.strip = _sb(c, eso, _nm("strip"), 128, 2560, F32)
        p.dma("sp", c.strip[:], c.dram["c_strip_causal"])
        s_dq = _sb(c, eso, _nm("sdq"), 128, 8 * 256, BF16)
        s_dkv = _sb(c, eso, _nm("sdkv"), 128, 8 * 256, BF16)
        s_uk = _sb(c, eso, _nm("suk"), 128, 2 * 1024, BF16)
        s_uv = _sb(c, eso, _nm("suv"), 128, 2 * 1024, BF16)
        K_fm = _sb(c, eso, _nm("Kfm"), 128, 8 * 2048, BF16)
        V_tm = _sb(c, eso, _nm("Vtm"), 128, 16 * 1024, BF16)
        cq_fm = _sb(c, eso, _nm("cqfm"), 128, 2 * 2048, BF16)
        ckv_fm = _sb(c, eso, _nm("ckvfm"), 128, 2 * 2048, BF16)
        stats = _sb(c, eso, _nm("stats"), 128, 2 * NT, F32)
        rstd = _sb(c, eso, _nm("rstd"), 128, NT, F32)
        st12 = _sb(c, eso, _nm("st12"), 128, 12, F32)
        xw = [_sb(c, eso, _nm("xw"), 128, 1024, F32) for _ in range(2)]
        xnf = [_sb(c, eso, _nm("xnf"), 128, 1024, F32) for _ in range(2)]
        zero2 = _sb(c, eso, _nm("zero2"), 2, 1024, BF16)
        _vop(p, "memset", zero2[:], [0.0], eng="dve")
        _wslab_load(c, s_dq, c.dram["l1a_W_dq"], 1024, 256)
        _wslab_load(c, s_dkv, c.dram["l1a_W_dkv"], 1024, 256)
        _wslab_load(c, s_uk, c.dram["l1a_W_uk"], 256, 1024)
        _wslab_load(c, s_uv, c.dram["l1a_W_uv"], 256, 1024)
        with p.phase() as es1:
            xn_fm = _sb(c, es1, _nm("xnfm"), 128, 8 * 2048, BF16)
            pT = [_ps(c, es1, _nm("pT"), 128, 512) for _ in range(2)]
            pK = [_ps(c, es1, _nm("pK"), 128, 512) for _ in range(2)]
            pV = [_ps(c, es1, _nm("pV"), 128, 512) for _ in range(2)]
            _ln_tiles(c, es1, b, xsrc_t, xn_fm, stats, rstd, range(NT), pT,
                      xw, xnf, st12=st12)
            _proj_fm(c, cq_fm, s_dq, 1024, 256, xn_fm, S, None, pK, 0, S)
            _proj_fm(c, ckv_fm, s_dkv, 1024, 256, xn_fm, S, None, pV, 0, S)
            _proj_fm(c, K_fm, s_uk, 256, 1024, ckv_fm, S, None, pK, 0, S)
            _proj_tm(c, V_tm, s_uv, 256, 1024, ckv_fm, S, None, pV, range(NT))
        s_uq = _sb(c, eso, _nm("suq"), 128, 2 * 1024, BF16)
        s_o = _sb(c, eso, _nm("so"), 128, 8 * 1024, BF16)
        _wslab_load(c, s_uq, c.dram["l1a_W_uq"], 256, 1024)
        _wslab_load(c, s_o, c.dram["l1a_W_o"], 1024, 1024)
        with p.phase() as es2:
            _attn_core(c, es2, b, "l1a", K_fm, V_tm, s_uq, s_o, None,
                       zero2, xsrc_t, xdst_t, None, cq_fm=cq_fm)


def _stage_l0f(c, b, xsrc_t, xdst_t):
    p = c.p
    with p.phase() as eso:
        xn_fm = _sb(c, eso, _nm("xnfm"), 128, 8 * 2048, BF16)
        acc = _sb(c, eso, _nm("acc"), 128, 16 * 1024, F32)
        C_sb = _sb(c, eso, _nm("Csb"), 16, 2048, BF16)
        stats = _sb(c, eso, _nm("stats"), 128, 2 * NT, F32)
        rstd = _sb(c, eso, _nm("rstd"), 128, NT, F32)
        with p.phase() as es1:
            st12 = _sb(c, es1, _nm("st12"), 128, 12, F32)
            xw = [_sb(c, es1, _nm("xw"), 128, 1024, F32) for _ in range(2)]
            xnf = [_sb(c, es1, _nm("xnf"), 128, 1024, F32) for _ in range(2)]
            rw_slab = _sb(c, es1, _nm("rws"), 128, 8 * 16, BF16)
            rb2_sb = _load_bias_rows2(c, es1, "rb2", c.dram["l0f_rb2"], 16)
            _wslab_load(c, rw_slab, c.dram["l0f_rw"], 1024, 16)
            pT = [_ps(c, es1, _nm("pT"), 128, 512) for _ in range(2)]
            pR = [_ps(c, es1, _nm("pR"), 128, 512) for _ in range(2)]
            _ln_tiles(c, es1, b, xsrc_t, xn_fm, stats, rstd, range(NT), pT,
                      xw, xnf, st12=st12)
            prb = _sb(c, es1, _nm("prb"), 128, 16, F32)
            m8 = _sb(c, es1, _nm("m8"), 128, 8, F32)
            msk = _sb(c, es1, _nm("msk"), 128, 16, F32)
            ctm = _sb(c, es1, _nm("ctm"), 128, 16, F32)
            nmz = _sb(c, es1, _nm("nmz"), 128, 2, F32)
            for tt in range(NT):
                pb = pR[tt % 2]
                terms = [(pb[:, 0:16], c.ones2[:, 0:128], rb2_sb[:, 0:16])]
                terms += [(pb[:, 0:16],
                           xn_fm[:, ic * 2048 + tt * 128: ic * 2048 + tt * 128 + 128],
                           rw_slab[:, ic * 16: ic * 16 + 16]) for ic in range(8)]
                p.mmg(terms)
                _vop(p, "reduce_max", nmz[:, 0:1], [pb[:, 0:16]], axis=AX.X, negate=True)
                _act(p, prb[:], pb[:, 0:16], AF.Exp, bias=nmz[:, 0:1])
                _vop(p, "max", m8[:], [prb[:]])
                _vop(p, "tensor_scalar", msk[:], [prb[:], m8[:, 3:4], None],
                     op0=ALU.is_ge)
                _vop(p, "tensor_mul", msk[:], [msk[:], prb[:]])
                _vop(p, "reduce_sum", nmz[:, 1:2], [msk[:]], axis=AX.X)
                _vop(p, "reciprocal", nmz[:, 1:2], [nmz[:, 1:2]])
                _vop(p, "tensor_scalar_mul", ctm[:], [msk[:], nmz[:, 1:2]])
                p.ex("pe", lambda e, o=pb[0:16, 16:144], i=ctm[:], idn=c.ident[:]:
                     e.transpose(o, i, idn), reads=[ctm[:], c.ident[:]],
                     writes=[pb])
                _vop(p, "tensor_copy", C_sb[0:16, tt * 128:(tt + 1) * 128],
                     [pb[0:16, 16:144]])
        with p.phase() as es2:
            w1s = [_sb(c, es2, _nm("w1s"), 128, 8 * 2048, BF16)]
            w2s = [_sb(c, es2, _nm("w2s"), 128, 16 * 1024, BF16)]
            hbuf = _sb(c, es2, _nm("hbuf"), 128, 16 * 512, BF16)
            b1_all = _sb(c, es2, _nm("b1a"), 128, 16 * 16, F32)
            p.dma("sp", b1_all[:], c.dram["l0f_b1"])
            b2_sb = _sb(c, es2, _nm("b2s"), 16, 1024, BF16)
            p.dma("pool", b2_sb[:], c.dram["l0f_b2"])
            _wslab_load(c, w1s[0], c.dram["l0f_w1"][0], 1024, 2048)
            _wslab_load(c, w2s[0], c.dram["l0f_w2"][0], 2048, 1024)
            pUP = [_ps(c, es2, _nm("pUP"), 128, 512) for _ in range(2)]
            pDN = [_ps(c, es2, _nm("pDN"), 128, 512) for _ in range(2)]
            pCE = _ps(c, es2, _nm("pCE"), 128, 512)
            ceb = _sb(c, es2, _nm("ceb"), 128, 512, BF16)
            for tt in range(NT):
                for dh in range(2):
                    pb = pDN[(tt * 2 + dh) % 2]
                    p.mmg([(pb[:, 0:512], C_sb[0:16, tt * 128:(tt + 1) * 128],
                            b2_sb[0:16, dh * 512:(dh + 1) * 512])])
                    _vop(p, "tensor_copy",
                         acc[:, tt * 1024 + dh * 512: tt * 1024 + dh * 512 + 512],
                         [pb[:, 0:512]])
            for e in range(16):
                for tch in range(4):
                    p.mmg([(pCE[:, 0:512], c.e16[:, e * 128:(e + 1) * 128],
                            C_sb[0:16, tch * 512:(tch + 1) * 512])])
                    _vop(p, "tensor_copy", ceb[:], [pCE[:, 0:512]])
                    for fc in range(16):
                        pb = pUP[fc % 2]
                        terms = [(pb[:, 0:512],
                                  w1s[0][:, ic * 2048 + fc * 128: ic * 2048 + fc * 128 + 128],
                                  xn_fm[:, ic * 2048 + tch * 512: ic * 2048 + tch * 512 + 512])
                                 for ic in range(8)]
                        p.mmg(terms)
                        _act(p, hbuf[:, fc * 512:(fc + 1) * 512], pb[:, 0:512],
                             AF.Gelu, bias=b1_all[:, e * 16 + fc: e * 16 + fc + 1])
                    _vop(p, "tensor_tensor", hbuf[:].rearrange("p (fc t) -> p fc t", t=512),
                         [hbuf[:].rearrange("p (fc t) -> p fc t", t=512),
                          ceb[:].rearrange("p (o t) -> p o t", o=1).to_broadcast([128, 16, 512])],
                         op=ALU.mult)
                    for j in range(4):
                        tt = tch * 4 + j
                        for dh in range(2):
                            pb = pDN[(j * 2 + dh) % 2]
                            terms = [(pb[:, 0:512],
                                      hbuf[:, fc * 512 + j * 128: fc * 512 + j * 128 + 128],
                                      w2s[0][:, fc * 1024 + dh * 512: fc * 1024 + dh * 512 + 512])
                                     for fc in range(16)]
                            p.mmg(terms)
                            dst = acc[:, tt * 1024 + dh * 512: tt * 1024 + dh * 512 + 512]
                            _vop(p, "tensor_add", dst, [dst, pb[:, 0:512]])
                if e + 1 < 16:
                    _wslab_load(c, w1s[0], c.dram["l0f_w1"][e + 1], 1024, 2048)
                    _wslab_load(c, w2s[0], c.dram["l0f_w2"][e + 1], 2048, 1024)
        with p.phase() as es3:
            xr = [_sb(c, es3, _nm("xr"), 128, 1024, F32) for _ in range(2)]
            for tt in range(NT):
                xt = xr[tt % 2]
                p.dma("sp", xt[:], xsrc_t[b * NT + tt])
                _vop(p, "tensor_add", xt[:], [xt[:], acc[:, tt * 1024:(tt + 1) * 1024]])
                p.dma("act", xdst_t[b * NT + tt], xt[:])


def _rope_combine(c, pA, pB, cosv, sinv, dst):
    p = c.p
    t1 = c.rs1
    t2 = c.rs2
    _vop(p, "tensor_mul", t1[:], [pB, sinv])
    _vop(p, "tensor_mul", t2[:], [pA, cosv])
    _vop(p, "tensor_add", dst, [t1[:], t2[:]])


def _stage_l1b(c, b, xsrc_t, xdst_t):
    p = c.p
    with p.phase() as eso:
        c.strip = _sb(c, eso, _nm("strip"), 128, 2560, F32)
        p.dma("sp", c.strip[:], c.dram["c_strip_causal"])
        K_ro = _sb(c, eso, _nm("Kro"), 128, 4 * 2048, BF16)
        V_tm = _sb(c, eso, _nm("Vtm"), 128, 16 * 256, BF16)
        k_sp = _sb(c, eso, _nm("ksp"), 16, 4 * 2048, BF16)
        stats = _sb(c, eso, _nm("stats"), 128, 2 * NT, F32)
        rstd = _sb(c, eso, _nm("rstd"), 128, NT, F32)
        cosb = _sb(c, eso, _nm("cosb"), 128, 2048, F32)
        sinb = _sb(c, eso, _nm("sinb"), 128, 2048, F32)
        wsT = _sb(c, eso, _nm("wsT"), 128, 16, BF16)
        bs_col = _sb(c, eso, _nm("bs"), 16, 1, F32)
        bo2_sb = _load_bias_rows2(c, eso, "bo2", c.dram["l1b_bo2"], 1024)
        c.rs1 = _sb(c, eso, _nm("rs1"), 128, 512, F32)
        c.rs2 = _sb(c, eso, _nm("rs2"), 128, 512, F32)
        p.dma("sp", cosb[:], c.dram["c_cos"])
        p.dma("sp", sinb[:], c.dram["c_sin"])
        p.dma("pool", wsT[:], c.dram["l1b_wsT"])
        p.dma("sp", bs_col[:], c.dram["l1b_bs"])
        with p.phase() as es1:
            xn_fm = _sb(c, es1, _nm("xnfm"), 128, 8 * 2048, BF16)
            st12 = _sb(c, es1, _nm("st12"), 128, 12, F32)
            xw = [_sb(c, es1, _nm("xw"), 128, 1024, F32) for _ in range(2)]
            xnf = [_sb(c, es1, _nm("xnf"), 128, 1024, F32) for _ in range(2)]
            s_k = _sb(c, es1, _nm("sk"), 128, 8 * 512, BF16)
            s_kr = _sb(c, es1, _nm("skr"), 128, 8 * 512, BF16)
            s_v = _sb(c, es1, _nm("sv"), 128, 8 * 256, BF16)
            _wslab_load(c, s_k, c.dram["l1b_wkT"], 1024, 512)
            _wslab_load(c, s_kr, c.dram["l1b_wkTr"], 1024, 512)
            _wslab_load(c, s_v, c.dram["l1b_wvT"], 1024, 256)
            pT = [_ps(c, es1, _nm("pT"), 128, 512) for _ in range(2)]
            pK = [_ps(c, es1, _nm("pK"), 128, 512) for _ in range(2)]
            pV = [_ps(c, es1, _nm("pV"), 128, 512) for _ in range(2)]
            _ln_tiles(c, es1, b, xsrc_t, xn_fm, stats, rstd, range(NT), pT,
                      xw, xnf, st12=st12)
            for oc in range(4):
                for tch in range(4):
                    pa, pb = pK[0][:, 0:512], pK[1][:, 0:512]
                    t0 = tch * 512
                    terms = [(pa, s_k[:, ic * 512 + oc * 128: ic * 512 + oc * 128 + 128],
                              xn_fm[:, ic * 2048 + t0: ic * 2048 + t0 + 512])
                             for ic in range(8)]
                    p.mmg(terms)
                    terms = [(pb, s_kr[:, ic * 512 + oc * 128: ic * 512 + oc * 128 + 128],
                              xn_fm[:, ic * 2048 + t0: ic * 2048 + t0 + 512])
                             for ic in range(8)]
                    p.mmg(terms)
                    _rope_combine(c, pa, pb, cosb[:, t0:t0 + 512], sinb[:, t0:t0 + 512],
                                  K_ro[:, oc * 2048 + t0: oc * 2048 + t0 + 512])
            for tt in range(NT):
                pbv = pV[tt % 2][:, 0:256]
                terms = [(pbv,
                          xn_fm[:, ic * 2048 + tt * 128: ic * 2048 + tt * 128 + 128],
                          s_v[:, ic * 256: ic * 256 + 256]) for ic in range(8)]
                p.mmg(terms)
                _vop(p, "tensor_copy", V_tm[:, tt * 256:(tt + 1) * 256], [pbv])
            for kvh in range(4):
                for tch in range(4):
                    pbs = pT[tch % 2][0:16, 0:512]
                    rk = K_ro[0:64, kvh * 2048 + tch * 512: kvh * 2048 + tch * 512 + 512]
                    p.mmg([(pbs, wsT[0:64, 0:16], rk)])
                    _sadd(p, k_sp[0:16, kvh * 2048 + tch * 512: kvh * 2048 + tch * 512 + 512],
                          pbs, bs_col[0:16, 0:1])
        s_q = _sb(c, eso, _nm("sq"), 128, 8 * 1024, BF16)
        s_qr = _sb(c, eso, _nm("sqr"), 128, 8 * 1024, BF16)
        s_o = _sb(c, eso, _nm("so"), 128, 8 * 1024, BF16)
        _wslab_load(c, s_q, c.dram["l1b_wqT"], 1024, 1024)
        _wslab_load(c, s_qr, c.dram["l1b_wqTr"], 1024, 1024)
        _wslab_load(c, s_o, c.dram["l1b_woT"], 1024, 1024)
        with p.phase() as es2:
            xnf = [_sb(c, es2, _nm("xnf"), 128, 1024, F32) for _ in range(2)]
            xnq = _sb(c, es2, _nm("xnq"), 128, 8 * 512, BF16)
            Qro = _sb(c, es2, _nm("Qro"), 128, 8 * 512, BF16)
            q_sp = _sb(c, es2, _nm("qsp"), 16, 512, BF16)
            tnh = [c.rs1, c.rs2]
            probs = [[_sb(c, es2, _nm("probs"), 128, 512, BF16) for _ in range(4)]
                     for _ in range(3)]
            probsT = [[_sb(c, es2, _nm("probsT"), 128, 512, BF16) for _ in range(4)]
                      for _ in range(3)]
            asbc = [_sb(c, es2, _nm("asbc"), 128, 512, F32) for _ in range(3)]
            afm = _sb(c, es2, _nm("afm"), 128, 8 * 512, BF16)
            xr = [_sb(c, es2, _nm("xr"), 128, 1024, F32) for _ in range(2)]
            nmc = [_sb(c, es2, _nm("nmc"), 128, 4, F32) for _ in range(3)]
            nmsc = [_sb(c, es2, _nm("nmsc"), 128, 4, F32) for _ in range(3)]
            fvec = [_sb(c, es2, _nm("fvec"), 128, 4, F32) for _ in range(3)]
            zc = [_sb(c, es2, _nm("zc"), 128, 4, F32) for _ in range(3)]
            nm1 = [_sb(c, es2, _nm("nm"), 128, 1, F32) for _ in range(3)]
            zz = [_sb(c, es2, _nm("zz"), 128, 1, F32) for _ in range(3)]
            pQK = [_ps(c, es2, _nm("pQK"), 128, 512) for _ in range(3)]
            pSP = [_ps(c, es2, _nm("pSP"), 128, 512) for _ in range(3)]
            pAV = [_ps(c, es2, _nm("pAV"), 128, 512) for _ in range(2)]
            pb_t = [pAV[0][:, 0:512], pAV[1][:, 0:512]]
            for qch in range(4):
                for j in range(4):
                    tt = qch * 4 + j
                    w = xr[j % 2]
                    p.dma("sp", w[:], xsrc_t[b * NT + tt])
                    xf = xnf[j % 2]
                    _vop(p, "tensor_scalar", xf[:], [w[:], stats[:, 2 * tt:2 * tt + 1],
                                                     rstd[:, tt:tt + 1]],
                         op0=ALU.subtract, op1=ALU.mult)
                    for half in range(2):
                        pbh = pb_t[half]
                        for q in range(4):
                            dc = half * 4 + q
                            p.ex("pe", lambda e, o=pbh[:, q * 128:(q + 1) * 128],
                                 i=xf[:, dc * 128:(dc + 1) * 128], idn=c.ident[:]:
                                 e.transpose(o, i, idn),
                                 reads=[xf[:], c.ident[:]], writes=[pbh])
                        dst = xnq[:].rearrange("p (dc t) -> p dc t", t=512)[
                            :, half * 4:half * 4 + 4, j * 128:(j + 1) * 128]
                        src = pbh.rearrange("p (dc t) -> p dc t", t=128)
                        _vop(p, "tensor_copy", dst, [src])
                t0g = qch * 512
                for oc in range(8):
                    pa, pb2 = pQK[0][:, 0:512], pQK[1][:, 0:512]
                    terms = [(pa, s_q[:, ic * 1024 + oc * 128: ic * 1024 + oc * 128 + 128],
                              xnq[:, ic * 512: ic * 512 + 512]) for ic in range(8)]
                    p.mmg(terms)
                    terms = [(pb2, s_qr[:, ic * 1024 + oc * 128: ic * 1024 + oc * 128 + 128],
                              xnq[:, ic * 512: ic * 512 + 512]) for ic in range(8)]
                    p.mmg(terms)
                    _rope_combine(c, pa, pb2, cosb[:, t0g:t0g + 512],
                                  sinb[:, t0g:t0g + 512], Qro[:, oc * 512:(oc + 1) * 512])
                def geom(i):
                    h, j = divmod(i, 4)
                    qt = qch * 4 + j
                    qb = qt * 128
                    return h, j, qb, (qb + 127) // 512

                def S1(i):
                    h, j, qb, kc_hi = geom(i)
                    r = i % 3
                    kvh = h // 4
                    base = (h % 2) * 64
                    if j == 0:
                        pbs = pSP[0][0:16, 0:512]
                        p.mmg([(pbs, wsT[base:base + 64, 0:16],
                                Qro[base:base + 64, (h // 2) * 512:(h // 2) * 512 + 512])])
                        _sadd(p, q_sp[0:16, 0:512], pbs, bs_col[0:16, 0:1])
                    lq = Qro[base:base + 64, (h // 2) * 512 + j * 128:
                             (h // 2) * 512 + j * 128 + 128]
                    for kc in range(kc_hi + 1):
                        pqk = pQK[kc % 3][:, 0:512]
                        psp = pSP[1 + kc % 2][:, 0:512]
                        rk = K_ro[base:base + 64,
                                  kvh * 2048 + kc * 512: kvh * 2048 + kc * 512 + 512]
                        p.mmg([(pqk, lq, rk)])
                        p.mmg([(psp, q_sp[0:16, j * 128:j * 128 + 128],
                                k_sp[0:16, kvh * 2048 + kc * 512: kvh * 2048 + kc * 512 + 512])])
                        tb = tnh[kc % 2]
                        _act(p, tb[:], psp, AF.Tanh, scale=0.125)
                        ac = asbc[kc % 3][:]
                        _vop(p, "scalar_tensor_tensor", ac, [tb[:], 1.0, pqk],
                             op0=ALU.add, op1=ALU.mult)
                        if kc == kc_hi:
                            co = 2047 - (qb - kc * 512)
                            _vop(p, "tensor_add", ac, [ac, c.strip[:, co:co + 512]])
                        col = nmc[r][:, kc:kc + 1]
                        cols = nmsc[r][:, kc:kc + 1]
                        _vop(p, "reduce_max", col, [ac], axis=AX.X, negate=True)
                        _vop(p, "tensor_scalar_mul", cols, [col, 1.0 / 16.0])
                        _act(p, probs[r][kc][:], ac, AF.Exp,
                             bias=cols, scale=1.0 / 16.0, accum=zc[r][:, kc:kc + 1])

                def S2(i):
                    h, j, qb, kc_hi = geom(i)
                    r = i % 3
                    nkc = kc_hi + 1
                    nmv = nm1[r][:]
                    zzv = zz[r][:]
                    fv = fvec[r][:, 0:nkc]
                    _vop(p, "tensor_reduce", nmv, [nmc[r][:, 0:nkc]], axis=AX.X,
                         op=ALU.min)
                    _vop(p, "tensor_scalar", fv, [nmc[r][:, 0:nkc], nmv, None],
                         op0=ALU.subtract)
                    _act(p, fv, fv, AF.Exp, scale=-1.0 / 16.0)
                    _vop(p, "tensor_mul", zc[r][:, 0:nkc], [zc[r][:, 0:nkc], fv])
                    _vop(p, "reduce_sum", zzv, [zc[r][:, 0:nkc]], axis=AX.X)
                    _vop(p, "reciprocal", zzv, [zzv])
                    _vop(p, "tensor_scalar_mul", fv, [fv, zzv])
                    for kc in range(nkc):
                        sl = probs[r][kc][:]
                        _vop(p, "tensor_scalar_mul", sl, [sl, fv[:, kc:kc + 1]])
                        p.dma("sp", probsT[r][kc][:].rearrange(
                            "p (kt q) -> p kt q", q=128), sl, transpose=True)

                def AV(i):
                    h, j, qb, kc_hi = geom(i)
                    r = i % 3
                    kvh = h // 4
                    base = (h % 2) * 64
                    pav = pAV[i % 2][0:64, 0:128]
                    terms = []
                    for kt in range((kc_hi + 1) * 4):
                        terms.append((pav,
                                      V_tm[:, kt * 256 + kvh * 64: kt * 256 + kvh * 64 + 64],
                                      probsT[r][kt // 4][:, (kt % 4) * 128: (kt % 4) * 128 + 128]))
                    p.mmg(terms)
                    dst = afm[base:base + 64,
                              (h // 2) * 512 + j * 128:(h // 2) * 512 + j * 128 + 128]
                    _vop(p, "tensor_copy", dst, [pav])

                for i in range(66):
                    if i < 64:
                        S1(i)
                    if 1 <= i <= 64:
                        S2(i - 1)
                    if i >= 2:
                        AV(i - 2)
                _attn_out_proj(c, es2, b, qch, afm, s_o, bo2_sb, xdst_t,
                               pb_t, xr, xsrc_t)


def _stage_l1f(c, b, xsrc_t, xdst_t):
    p = c.p
    E, CAP = 16, 256
    with p.phase() as eso:
        acc = _sb(c, eso, _nm("acc"), 128, 16 * 1024, F32)
        stats = _sb(c, eso, _nm("stats"), 128, 2 * NT, F32)
        rstd = _sb(c, eso, _nm("rstd"), 128, NT, F32)
        valsT = _sb(c, eso, _nm("valsT"), 128, 2 * 16, F32)
        idxT = _sb(c, eso, _nm("idxT"), 128, 2 * 16, F32)
        with p.phase() as es1:
            xn_fm = _sb(c, es1, _nm("xnfm"), 128, 8 * 2048, BF16)
            st12 = _sb(c, es1, _nm("st12"), 128, 12, F32)
            xw = [_sb(c, es1, _nm("xw"), 128, 1024, F32) for _ in range(2)]
            xnf = [_sb(c, es1, _nm("xnf"), 128, 1024, F32) for _ in range(2)]
            gw_slab = _sb(c, es1, _nm("gws"), 128, 8 * 16, BF16)
            _wslab_load(c, gw_slab, c.dram["l1f_gwT"], 1024, 16)
            S_eb = _sb(c, es1, _nm("Seb"), 16, 2048, F32)
            prb = _sb(c, es1, _nm("prb"), 128, 16, F32)
            sctm = _sb(c, es1, _nm("sctm"), 128, 16, F32)
            nmz = _sb(c, es1, _nm("nmz"), 128, 2, F32)
            w0 = _sb(c, es1, _nm("w0"), 16, 2048, F32)
            w1 = _sb(c, es1, _nm("w1"), 16, 2048, F32)
            vals = _sb(c, es1, _nm("vals"), 16, 256, F32)
            idxu = _sb(c, es1, _nm("idxu"), 16, 256, U32)
            idxf = _sb(c, es1, _nm("idxf"), 16, 256, F32)
            pT = [_ps(c, es1, _nm("pT"), 128, 512) for _ in range(2)]
            pR = [_ps(c, es1, _nm("pR"), 128, 512) for _ in range(2)]
            _ln_tiles(c, es1, b, xsrc_t, xn_fm, stats, rstd, range(NT), pT,
                      xw, xnf, xn_dram_t=c.xnd_t, st12=st12)
            for tt in range(NT):
                pb = pR[tt % 2]
                terms = [(pb[:, 0:16],
                          xn_fm[:, ic * 2048 + tt * 128: ic * 2048 + tt * 128 + 128],
                          gw_slab[:, ic * 16: ic * 16 + 16]) for ic in range(8)]
                p.mmg(terms)
                _vop(p, "reduce_max", nmz[:, 0:1], [pb[:, 0:16]], axis=AX.X, negate=True)
                _act(p, prb[:], pb[:, 0:16], AF.Exp, bias=nmz[:, 0:1],
                     accum=nmz[:, 1:2])
                _vop(p, "reciprocal", nmz[:, 1:2], [nmz[:, 1:2]])
                _vop(p, "tensor_scalar_mul", sctm[:], [prb[:], nmz[:, 1:2]])
                p.ex("pe", lambda e, o=pb[0:16, 16:144], i=sctm[:], idn=c.ident[:]:
                     e.transpose(o, i, idn), reads=[sctm[:], c.ident[:]],
                     writes=[pb])
                _vop(p, "tensor_copy", S_eb[0:16, tt * 128:(tt + 1) * 128],
                     [pb[0:16, 16:144]])
            _vop(p, "tensor_copy", w0[:], [S_eb[:]])
            for r in range(32):
                src = w0 if r % 2 == 0 else w1
                dst = w1 if r % 2 == 0 else w0
                vs = vals[0:16, r * 8:(r + 1) * 8]
                _vop(p, "max", vs, [src[:]])
                _vop(p, "max_index", idxu[0:16, r * 8:(r + 1) * 8], [vs, src[:]])
                _vop(p, "match_replace", dst[:], [vs, src[:]], imm_value=-1.0)
            _vop(p, "tensor_copy", idxf[:], [idxu[:]])
            for ct in range(2):
                p.ex("pe", lambda e, o=pR[0][:, 0:16],
                     i=vals[0:16, ct * 128:(ct + 1) * 128], idn=c.ident[0:16, 0:16]:
                     e.transpose(o, i, idn), reads=[vals[:], c.ident[:]],
                     writes=[pR[0]])
                _vop(p, "tensor_copy", valsT[:, ct * 16:(ct + 1) * 16],
                     [pR[0][:, 0:16]])
                p.ex("pe", lambda e, o=pR[1][:, 0:16],
                     i=idxf[0:16, ct * 128:(ct + 1) * 128], idn=c.ident[0:16, 0:16]:
                     e.transpose(o, i, idn), reads=[idxf[:], c.ident[:]],
                     writes=[pR[1]])
                _vop(p, "tensor_copy", idxT[:, ct * 16:(ct + 1) * 16],
                     [pR[1][:, 0:16]])
        if "shared" not in _L1F_PARTS:
            with p.phase() as esz:
                zcl = _sb(c, esz, _nm("zcl"), 128, 1024, F32)
                _vop(p, "memset", zcl[:], [0.0])
                for tt in range(NT):
                    _vop(p, "tensor_copy", acc[:, tt * 1024:(tt + 1) * 1024], [zcl[:]])
        if "shared" in _L1F_PARTS:
         with p.phase() as es2:
            xnq2 = _sb(c, es2, _nm("xnq2"), 128, 8 * 512, BF16)
            wsd = _sb(c, es2, _nm("wsd"), 128, 16 * 1024, BF16)
            wsg = _sb(c, es2, _nm("wsg"), 128, 8 * 2048, BF16)
            wsu = _sb(c, es2, _nm("wsu"), 128, 8 * 2048, BF16)
            h_sh = _sb(c, es2, _nm("hsh"), 128, 16 * 512, BF16)
            gtmp = _sb(c, es2, _nm("gtmp"), 128, 512, BF16)
            xw2 = [_sb(c, es2, _nm("xw2"), 128, 1024, F32)]
            _wslab_load(c, wsd, c.dram["l1f_wsd"], 2048, 1024)
            _wslab_load(c, wsg, c.dram["l1f_wsg"], 1024, 2048)
            _wslab_load(c, wsu, c.dram["l1f_wsu"], 1024, 2048)
            pG = [_ps(c, es2, _nm("pG"), 128, 512) for _ in range(2)]
            pU = [_ps(c, es2, _nm("pU"), 128, 512) for _ in range(2)]
            pDN = [_ps(c, es2, _nm("pDN"), 128, 512) for _ in range(2)]
            for tch in range(4):
                for j in range(4):
                    tt = tch * 4 + j
                    w = xw2[0]
                    p.dma("sp", w[:], c.xnd_t[b * NT + tt])
                    for half in range(2):
                        pb = pG[half]
                        for q in range(4):
                            dc = half * 4 + q
                            p.ex("pe", lambda e, o=pb[:, q * 128:(q + 1) * 128],
                                 i=w[:, dc * 128:(dc + 1) * 128], idn=c.ident[:]:
                                 e.transpose(o, i, idn),
                                 reads=[w[:], c.ident[:]], writes=[pb])
                        dst = xnq2[:].rearrange("p (dc t) -> p dc t", t=512)[
                            :, half * 4:half * 4 + 4, j * 128:(j + 1) * 128]
                        src = pb[:].rearrange("p (dc t) -> p dc t", t=128)
                        _vop(p, "tensor_copy", dst, [src])
                for fc in range(16):
                    pg = pG[fc % 2][:, 0:512]
                    pu = pU[fc % 2][:, 0:512]
                    terms = [(pg, wsg[:, ic * 2048 + fc * 128: ic * 2048 + fc * 128 + 128],
                              xnq2[:, ic * 512: ic * 512 + 512]) for ic in range(8)]
                    p.mmg(terms)
                    terms = [(pu, wsu[:, ic * 2048 + fc * 128: ic * 2048 + fc * 128 + 128],
                              xnq2[:, ic * 512: ic * 512 + 512]) for ic in range(8)]
                    p.mmg(terms)
                    _act(p, gtmp[:], pg, AF.Gelu)
                    _vop(p, "tensor_tensor", h_sh[:, fc * 512:(fc + 1) * 512],
                         [gtmp[:], pu], op=ALU.mult)
                for j in range(4):
                    tt = tch * 4 + j
                    for dh in range(2):
                        pb = pDN[(j * 2 + dh) % 2][:, 0:512]
                        terms = [(pb,
                                  h_sh[:, fc2 * 512 + j * 128: fc2 * 512 + j * 128 + 128],
                                  wsd[:, fc2 * 1024 + dh * 512: fc2 * 1024 + dh * 512 + 512])
                                 for fc2 in range(16)]
                        p.mmg(terms)
                        _vop(p, "tensor_copy",
                             acc[:, tt * 1024 + dh * 512: tt * 1024 + dh * 512 + 512],
                             [pb])
        if "experts" in _L1F_PARTS:
         with p.phase() as es3:
            wg_h = [_sb(c, es3, _nm("wgh"), 128, 8 * 1024, BF16) for _ in range(2)]
            wu_h = [_sb(c, es3, _nm("wuh"), 128, 8 * 1024, BF16) for _ in range(2)]
            wd_h = [_sb(c, es3, _nm("wdh"), 128, 8 * 1024, BF16) for _ in range(2)]
            xtk = [_sb(c, es3, _nm("xtk"), 128, 2 * 1024, F32)]
            xtf = _sb(c, es3, _nm("xtf"), 128, 8 * 256, BF16)
            h_fm = _sb(c, es3, _nm("hfm"), 128, 16 * 256, BF16)
            gtmp = _sb(c, es3, _nm("gtmp2"), 128, 256, BF16)
            eo = [_sb(c, es3, _nm("eo"), 128, 2 * 1024, BF16) for _ in range(2)]
            Ptt = _sb(c, es3, _nm("Ptt"), 128, 4 * 128, BF16)
            iu32 = [_sb(c, es3, _nm("iu32"), 128, 2, U32) for _ in range(2)]
            pG = [_ps(c, es3, _nm("pG3"), 128, 512) for _ in range(2)]
            pU = [_ps(c, es3, _nm("pU3"), 128, 512) for _ in range(2)]
            pDN = [_ps(c, es3, _nm("pDN3"), 128, 512) for _ in range(2)]
            pSC = [_ps(c, es3, _nm("pSC"), 128, 512) for _ in range(2)]
            xnd_rows = c.xnd

            def gather(e):
                for ct in range(2):
                    _vop(p, "tensor_scalar_add", iu32[e % 2][:, ct:ct + 1],
                         [idxT[:, ct * 16 + e: ct * 16 + e + 1], float(b * S)])
                    p.dma_gather(xtk[0][:, ct * 1024:(ct + 1) * 1024], xnd_rows,
                                 iu32[e % 2][:, ct:ct + 1])

            def wload(e, fh):
                _wslab_load(c, wg_h[fh], c.dram["l1f_wg"][e][:, fh * 1024:(fh + 1) * 1024],
                            1024, 1024)
                _wslab_load(c, wu_h[fh], c.dram["l1f_wu"][e][:, fh * 1024:(fh + 1) * 1024],
                            1024, 1024)
                _wslab_load(c, wd_h[fh],
                            c.dram["l1f_wd"][e].rearrange("(fh x) d -> fh x d", fh=2)[fh],
                            1024, 1024)

            gather(0)
            wload(0, 0)
            for rnd in range(8):
                for er in range(2):
                    e = rnd * 2 + er
                    for ct in range(2):
                        for half in range(2):
                            pb = pG[half]
                            for q in range(4):
                                dc = half * 4 + q
                                p.ex("pe", lambda e2, o=pb[:, q * 128:(q + 1) * 128],
                                     i=xtk[0][:, ct * 1024 + dc * 128: ct * 1024 + dc * 128 + 128],
                                     idn=c.ident[:]: e2.transpose(o, i, idn),
                                     reads=[xtk[0][:], c.ident[:]], writes=[pb])
                            dst = xtf[:].rearrange("p (dc t) -> p dc t", t=256)[
                                :, half * 4:half * 4 + 4, ct * 128:(ct + 1) * 128]
                            src = pb[:].rearrange("p (dc t) -> p dc t", t=128)
                            _vop(p, "tensor_copy", dst, [src])
                    if e + 1 < 16:
                        gather(e + 1)
                    for fh in range(2):
                        if fh == 1:
                            wload(e, 1)
                        for fcl in range(8):
                            fc = fh * 8 + fcl
                            pg = pG[fcl % 2][:, 0:256]
                            pu = pU[fcl % 2][:, 0:256]
                            terms = [(pg,
                                      wg_h[fh][:, ic * 1024 + fcl * 128: ic * 1024 + fcl * 128 + 128],
                                      xtf[:, ic * 256: ic * 256 + 256]) for ic in range(8)]
                            p.mmg(terms)
                            terms = [(pu,
                                      wu_h[fh][:, ic * 1024 + fcl * 128: ic * 1024 + fcl * 128 + 128],
                                      xtf[:, ic * 256: ic * 256 + 256]) for ic in range(8)]
                            p.mmg(terms)
                            _act(p, gtmp[:], pg, AF.Gelu)
                            _vop(p, "tensor_tensor", h_fm[:, fc * 256:(fc + 1) * 256],
                                 [gtmp[:], pu], op=ALU.mult)
                    for ct in range(2):
                        for dh in range(2):
                            pb = pDN[(ct * 2 + dh) % 2][:, 0:512]
                            terms = [(pb,
                                      h_fm[:, fc2 * 256 + ct * 128: fc2 * 256 + ct * 128 + 128],
                                      wd_h[fc2 // 8][:, (fc2 % 8) * 1024 + dh * 512:
                                                     (fc2 % 8) * 1024 + dh * 512 + 512])
                                     for fc2 in range(16)]
                            p.mmg(terms)
                            _vop(p, "tensor_copy",
                                 eo[er][:, ct * 1024 + dh * 512: ct * 1024 + dh * 512 + 512],
                                 [pb])
                    if e + 1 < 16:
                        wload(e + 1, 0)
                for tt in range(NT):
                    for k in range(4):
                        er2, ct = k // 2, k % 2
                        e2x = rnd * 2 + er2
                        _vop(p, "tensor_scalar", Ptt[:, k * 128:(k + 1) * 128],
                             [c.iota_t[:, tt * 128:(tt + 1) * 128],
                              idxT[:, ct * 16 + e2x: ct * 16 + e2x + 1],
                              valsT[:, ct * 16 + e2x: ct * 16 + e2x + 1]],
                             op0=ALU.is_equal, op1=ALU.mult)
                    for dh in range(2):
                        pb = pSC[dh][:, 0:512]
                        terms = [(pb, Ptt[:, k * 128:(k + 1) * 128],
                                  eo[k // 2][:, (k % 2) * 1024 + dh * 512:
                                             (k % 2) * 1024 + dh * 512 + 512])
                                 for k in range(4)]
                        p.mmg(terms)
                        dst = acc[:, tt * 1024 + dh * 512: tt * 1024 + dh * 512 + 512]
                        _vop(p, "tensor_add", dst, [dst, pb])
        with p.phase() as es4:
            xr = [_sb(c, es4, _nm("xr"), 128, 1024, F32) for _ in range(2)]
            for tt in range(NT):
                xt = xr[tt % 2]
                p.dma("sp", xt[:], xsrc_t[b * NT + tt])
                _vop(p, "tensor_add", xt[:], [xt[:], acc[:, tt * 1024:(tt + 1) * 1024]])
                p.dma("act", xdst_t[b * NT + tt], xt[:])



def _build(arrs):
    nc = bass.Bass("TRN2", target_bir_lowering=False)
    es = ExitStack()
    c = _B()
    c.nc = nc
    c.dram = {}
    for k, a in arrs.items():
        dt = F32
        c.dram[k] = nc.dram_tensor(k, list(a.shape), dt, kind="ExternalInput").ap()
    x0 = nc.dram_tensor("x0", [BPC, S, D], F32, kind="ExternalInput").ap()
    out = nc.dram_tensor("out", [BPC, S, D], F32, kind="ExternalOutput").ap()
    xs1 = nc.dram_tensor("xs1", [BPC, S, D], F32, kind="Internal").ap()
    xs2 = nc.dram_tensor("xs2", [BPC, S, D], F32, kind="Internal").ap()
    xnd = nc.dram_tensor("xnd", [BPC * S, D], F32, kind="Internal").ap()
    tile = lambda ap: ap.rearrange("b (tt p) d -> (b tt) p d", p=128)
    x0_t, out_t, xs1_t, xs2_t = tile(x0), tile(out), tile(xs1), tile(xs2)

    p = c.p = Prog(nc, es)
    c.ident = _sb(c, es, "ident", 128, 128, F32)
    p.dma("sp", c.ident[:], c.dram["c_ident"])
    c.ones2 = _sb(c, es, "ones2", 2, 128, BF16)
    p.dma("pool", c.ones2[:], c.dram["c_ones2"])
    c.e16 = _sb(c, es, "e16", 16, 16 * 128, BF16)
    p.dma("pool", c.e16[:], c.dram["c_e16"])
    c.konst = _sb(c, es, "konst", 128, 4, F32)
    c.iota_t = _sb(c, es, "iotat", 128, 2048, F32)
    p.ex("pool", lambda e: e.iota(c.iota_t[:], [[1, 2048]], base=0,
                                  channel_multiplier=0,
                                  allow_small_or_imprecise_dtypes=True),
         writes=[c.iota_t[:]])
    _vop(p, "memset", c.konst[:, 0:1], [EPS])
    _vop(p, "memset", c.konst[:, 1:2], [0.0])
    c.xnd_t = xnd.rearrange("(n p) d -> n p d", p=128)
    c.xnd = xnd

    stages = sorted(_STAGES)
    last = stages[-1]

    def src_dst(i):
        srcs = x0_t if i == stages[0] else (xs1_t if i % 2 == 1 else xs2_t)
        dsts = out_t if i == last else (xs1_t if i % 2 == 0 else xs2_t)
        return srcs, dsts

    for b in range(BPC):
        for i in stages:
            xsrc_t, xdst_t = src_dst(i)
            if i == 0:
                _stage_attn_std(c, b, "l0a", xsrc_t, xdst_t,
                                "l0a_wkT", "l0a_wvT", "l0a_wqT", "l0a_woT",
                                "l0a_bq", "l0a_bk", "l0a_bv2", "l0a_bo2")
            elif i == 2:
                _stage_attn_std(c, b, "l0b", xsrc_t, xdst_t,
                                "l0b_wkT", "l0b_wvT", "l0b_wqT", "l0b_woT",
                                "l0b_bq", "l0b_bk", "l0b_bv2", "l0b_bo2")
            elif i == 4:
                _stage_l1a(c, b, xsrc_t, xdst_t)
            elif i in (1, 3):
                _stage_l0f(c, b, xsrc_t, xdst_t)
            elif i == 6:
                _stage_l1b(c, b, xsrc_t, xdst_t)
            else:
                _stage_l1f(c, b, xsrc_t, xdst_t)
    p.replay(es)
    c._es = es
    return nc, p


def kernel(**inputs):
    consts = _prep_consts()
    w = _prep_weights({k: np.asarray(v, dtype=np.float32) for k, v in inputs.items()
                       if k != "L0_attn_a_x"})
    x = np.asarray(inputs["L0_attn_a_x"], dtype=np.float32)
    common = {}
    common.update(consts)
    common.update(w)
    nc, p = _build(common)
    in_maps = []
    for cid in range(NCORES):
        m = dict(common)
        m["x0"] = np.ascontiguousarray(x[cid * BPC:(cid + 1) * BPC])
        in_maps.append(m)
    res = run_bass_kernel_spmd(nc, in_maps, core_ids=list(range(NCORES)))
    out = np.concatenate([np.asarray(r["out"], dtype=np.float32)
                          for r in res.results], axis=0)
    return out
